# version 01
# baseline (speedup 1.0000x reference)
import numpy as np
import ml_dtypes

import concourse.bass as bass
import concourse.mybir as mybir
from concourse.bass import IndirectOffsetOnAxis
from concourse.bass_utils import run_bass_kernel_spmd

F32 = mybir.dt.float32
BF16 = mybir.dt.bfloat16
U32 = mybir.dt.uint32
AF = mybir.ActivationFunctionType
ALU = mybir.AluOpType
f32 = np.float32

T, C, NKV, DH, DKV, DF = 8192, 2048, 4, 128, 512, 8192
NC, NS = 8, 8
LT = NS * 128
CAP = 640
EPS = 1e-5
SHIFT = 8.0
INVSQD = float(1.0 / np.sqrt(128.0))
BISECT = 48
RG = [list(range(NC))]

_TAPS = ()


def q_tile(c, s):
    return 8 * s + ((s + c) % NC)


def kchunk_owner(j):
    s = j // 8
    return (j % 8 - s) % NC, s


def build_program(taps=()):
    nc = bass.Bass("TRN2", target_bir_lowering=False)

    dI = lambda n, s, d=F32: nc.dram_tensor(n, s, d, kind="ExternalInput").ap()
    x_loc = dI("x_loc", [LT, C])
    Wq = dI("Wq", [C, C]); bq = dI("bq", [1, C])
    Wk = dI("Wk", [C, DKV]); bk = dI("bk", [1, DKV])
    Wv = dI("Wv", [C, DKV]); bv = dI("bv", [1, DKV])
    Wo = dI("Wo", [DKV, C]); bo = dI("bo", [1, C])
    w1 = dI("w1", [C, DF]); b1 = dI("b1", [1, DF])
    w2 = dI("w2", [DF, C]); b2 = dI("b2", [1, C])
    wr = dI("wr", [1, C]); br = dI("br", [1, 1])
    cosT = dI("cosT", [LT, 256])
    sinT = dI("sinT", [LT, 256])
    masksI = dI("masks", [128, NS * 1024], BF16)
    identI = dI("ident", [128, 128], BF16)

    out = nc.dram_tensor("out", [LT, C], F32, kind="ExternalOutput").ap()
    tap_shapes = {"x1": [LT, C], "thr": [1, 1], "mask": [128, NS],
                  "scores": [64, 128], "gidx": [128, 5], "ynT": [128, NKV * 1024],
                  "pdbg": [128, 1024]}
    tap_out = {t: nc.dram_tensor("tap_" + t, tap_shapes[t], F32, kind="ExternalOutput").ap()
               for t in taps}

    k_in = nc.dram_tensor("k_in", [DKV, LT], BF16).ap()
    k_out = nc.dram_tensor("k_out", [NC * DKV, LT], BF16, addr_space="Shared").ap()
    v_in = nc.dram_tensor("v_in", [LT, DKV], BF16).ap()
    v_out = nc.dram_tensor("v_out", [NC * LT, DKV], BF16, addr_space="Shared").ap()
    sc_in = nc.dram_tensor("sc_in", [NS, 128], F32).ap()
    sc_out = nc.dram_tensor("sc_out", [NC * NS, 128], F32, addr_space="Shared").ap()
    x1_dram = nc.dram_tensor("x1_dram", [LT, C], F32).ap()
    gbuf = nc.dram_tensor("gbuf", [LT, 128], F32).ap()

    sems = {}
    cnt = {}
    opens = {}

    def semalloc(k):
        cm = nc.semaphore("sem_" + k)
        opens["sem_" + k] = cm
        sems[k] = cm.__enter__()
        cnt[k] = 0

    for k in ("PE", "VEC", "ACT", "POOL", "CC"):
        semalloc(k)

    def inc(instr, k, n=1):
        instr.then_inc(sems[k], n)
        cnt[k] += n
        return cnt[k]

    dqs = {}
    for qn, dep in (("SYNQ", 4), ("POOLQ", 4), ("SCLQ", 4)):
        slots = []
        for i in range(dep):
            cm = nc.semaphore(f"q_{qn}{i}")
            opens[f"q_{qn}{i}"] = cm
            slots.append(cm.__enter__())
        dqs[qn] = {"sems": slots, "val": [0] * dep, "n": 0, "d": dep}

    def dq_pre(eng, qn):
        q = dqs[qn]
        i = q["n"] % q["d"]
        if q["val"][i] > 0:
            eng.wait_ge(q["sems"][i], q["val"][i])

    def dq_post(instr, qn):
        q = dqs[qn]
        i = q["n"] % q["d"]
        q["val"][i] += 16
        instr.then_inc(q["sems"][i], 16)
        q["n"] += 1
        return (qn, i, q["val"][i])

    def snapq(qn):
        q = dqs[qn]
        return [(qn, i, q["val"][i]) for i in range(q["d"]) if q["val"][i] > 0]

    def salloc(n, s, d):
        cm = nc.sbuf_tensor(n, s, d)
        opens[n] = cm
        return cm.__enter__()

    def palloc(n, s, d=F32):
        cm = nc.psum_tensor(n, s, d)
        opens[n] = cm
        return cm.__enter__()

    def free(*names):
        for n in names:
            opens.pop(n).__exit__(None, None, None)

    identb = salloc("identb", [128, 128], BF16)
    ident32 = salloc("ident32", [128, 128], F32)
    ones_bf = salloc("ones_bf", [1, 512], BF16)
    ones_f = salloc("ones_f", [1, 128], F32)
    ones_c = salloc("ones_c", [128, 1], BF16)
    ones64 = salloc("ones64", [64, 1], F32)
    zerop = salloc("zerop", [128, 1], F32)
    bq_eff = salloc("bq_eff", [1, DKV], BF16)
    bk_b = salloc("bk_b", [1, DKV], BF16)
    bv_b = salloc("bv_b", [1, DKV], BF16)
    s_loc = salloc("s_loc", [128, NS], F32)
    m_loc = salloc("m_loc", [128, NS], F32)
    bnst = salloc("bnst", [128, 24], F32)
    mv = salloc("mv", [128, 2], F32)
    lnc1 = salloc("lnc1", [128, 1], F32)
    lnc2 = salloc("lnc2", [128, 1], F32)
    gidx = salloc("gidx", [128, 5], U32)
    gf32 = salloc("gf32", [128, 5], F32)
    x1t = salloc("x1t", [128, C], F32)
    v1t = salloc("v1t", [128, C], F32)

    qTa = salloc("qTa", [128, NKV * 1024], BF16)
    ynT = salloc("ynT", [128, NKV * 1024], BF16)
    Wo_sb = salloc("Wo_sb", [128, NKV * C], BF16)
    bo_b = salloc("bo_b", [1, C], BF16)

    xfm_all = salloc("xfm_all", [128, 16 * 1024], BF16)
    cos4 = salloc("cos4", [128, NS * 256], F32)
    sin4 = salloc("sin4", [128, NS * 256], F32)
    Wk_sb = salloc("Wk_sb", [128, 16 * 512], BF16)
    Wv_sb = salloc("Wv_sb", [128, 16 * 512], BF16)
    Wq_eff = salloc("Wq_eff", [128, 16 * 512], BF16)
    Wq_tmp = [salloc(f"Wq_tmp{i}", [128, C], BF16) for i in range(2)]
    xbuf = [salloc(f"xbuf{i}", [128, C], F32) for i in range(2)]
    xb16 = [salloc(f"xb16{i}", [128, C], BF16) for i in range(2)]
    qr = salloc("qr", [128, DKV], BF16)
    kr = salloc("kr", [128, DKV], BF16)
    vr = salloc("vr", [128, DKV], BF16)
    ktT = salloc("ktT", [128, NKV * 128], BF16)
    rt1 = salloc("rt1", [128, 256], F32)
    rt2 = salloc("rt2", [128, 256], F32)

    btmp = salloc("btmp", [1, C], F32)
    bt1 = salloc("bt1", [1, 512], F32)
    bt2 = salloc("bt2", [1, 512], F32)

    p1ps = [palloc(f"p1ps{i}", [128, 512]) for i in range(2)]
    tpps = [palloc(f"tpps{i}", [128, 512], BF16) for i in range(2)]

    M = {}

    with nc.Block() as block:
        TE, VE, SE, GE, SY = block.tensor, block.vector, block.scalar, block.gpsimd, block.sync

        def W(eng, k, v):
            if v > 0:
                eng.wait_ge(sems[k], v)

        def Wd(eng, h):
            if isinstance(h, list):
                for hh in h:
                    Wd(eng, hh)
            elif isinstance(h, tuple):
                qn, i, v = h
                eng.wait_ge(dqs[qn]["sems"][i], v)

        def dma(eng, qn, instr_fn):
            dq_pre(eng, qn)
            return dq_post(instr_fn(), qn)

        @SY
        def _(q):
            dq_pre(q, "SYNQ")
            M["ident"] = dq_post(q.dma_start(out=identb[:], in_=identI), "SYNQ")
            dq_pre(q, "SYNQ")
            dq_post(q.dma_start(out=cos4[:].rearrange("p (s f) -> p s f", s=NS),
                               in_=cosT.rearrange("(s p) f -> p s f", p=128)), "SYNQ")
            dq_pre(q, "SYNQ")
            dq_post(q.dma_start(out=sin4[:].rearrange("p (s f) -> p s f", s=NS),
                               in_=sinT.rearrange("(s p) f -> p s f", p=128)), "SYNQ")
            dq_pre(q, "SYNQ")
            dq_post(q.dma_start(out=btmp[:], in_=bq), "SYNQ")
            M["p0"] = snapq("SYNQ")

        @GE
        def _(g):
            dq_pre(g, "POOLQ")
            dq_post(g.dma_start(out=Wk_sb[:].rearrange("p (c n) -> p c n", c=16),
                               in_=Wk.rearrange("(c p) n -> p c n", p=128)), "POOLQ")
            dq_pre(g, "POOLQ")
            dq_post(g.dma_start(out=Wv_sb[:].rearrange("p (c n) -> p c n", c=16),
                               in_=Wv.rearrange("(c p) n -> p c n", p=128)), "POOLQ")
            dq_pre(g, "POOLQ")
            dq_post(g.dma_start(out=bk_b[:], in_=bk), "POOLQ")
            dq_pre(g, "POOLQ")
            dq_post(g.dma_start(out=bv_b[:], in_=bv), "POOLQ")
            M["wkv"] = snapq("POOLQ")

        @VE
        def _(v):
            v.memset(ones_bf[:], 1.0)
            v.memset(ones_f[:], 1.0)
            v.memset(ones_c[:], 1.0)
            v.memset(ones64[:], 1.0)
            inc(v.memset(zerop[:], 0.0), "VEC")
            M["ones"] = cnt["VEC"]
            Wd(v, M["ident"])
            inc(v.tensor_copy(ident32[:], identb[:]), "VEC")
            Wd(v, M["p0"])
            bqv = btmp[:].rearrange("o (h j i) -> o h j i", h=4, j=4)
            v.tensor_tensor(bt1[:].rearrange("o (h i) -> o h i", h=4),
                            bqv[:, :, 0, :], bqv[:, :, 1, :], op=ALU.add)
            inc(v.tensor_tensor(bt2[:].rearrange("o (h i) -> o h i", h=4),
                                bqv[:, :, 2, :], bqv[:, :, 3, :], op=ALU.add), "VEC")
            W(v, "VEC", cnt["VEC"])
            inc(v.tensor_tensor(bq_eff[:], bt1[:], bt2[:], op=ALU.add), "VEC")
            M["bqe"] = cnt["VEC"]

        for i in range(16):
            @GE
            def _(g, i=i):
                W(g, "VEC", M.get(f"wqe{i - 2}", 0))
                dq_pre(g, "POOLQ")
                M[f"wql{i}"] = dq_post(g.dma_start(out=Wq_tmp[i % 2][:],
                                   in_=Wq[i * 128:(i + 1) * 128, :]), "POOLQ")

            @VE
            def _(v, i=i):
                Wd(v, M[f"wql{i}"])
                wv = Wq_tmp[i % 2][:].rearrange("p (h j i) -> p h j i", h=4, j=4)
                eff = Wq_eff[:, i * 512:(i + 1) * 512].rearrange("p (h i) -> p h i", h=4)
                inc(v.tensor_tensor(eff, wv[:, :, 0, :], wv[:, :, 1, :], op=ALU.add), "VEC")
                W(v, "VEC", cnt["VEC"])
                inc(v.tensor_tensor(eff, eff, wv[:, :, 2, :], op=ALU.add), "VEC")
                W(v, "VEC", cnt["VEC"])
                inc(v.tensor_tensor(eff, eff, wv[:, :, 3, :], op=ALU.add), "VEC")
                M[f"wqe{i}"] = cnt["VEC"]
        M["wqeff"] = cnt["VEC"]
        free("bt2", "bt1", "btmp")

        for s in range(NS):
            @SY
            def _(q, s=s):
                W(q, "VEC", M.get(f"xc{s - 2}", 0))
                dq_pre(q, "SYNQ")
                M[f"xl{s}"] = dq_post(q.dma_start(out=xbuf[s % 2][:], in_=x_loc[s * 128:(s + 1) * 128, :]), "SYNQ")

            @VE
            def _(v, s=s):
                Wd(v, M[f"xl{s}"])
                W(v, "PE", M.get(f"xT{s - 2}", 0))
                inc(v.tensor_copy(xb16[s % 2][:], xbuf[s % 2][:]), "VEC")
                M[f"xc{s}"] = cnt["VEC"]

            for b in range(4):
                @TE
                def _(t, s=s, b=b):
                    W(t, "VEC", M[f"xc{s}"])
                    W(t, "VEC", M.get(f"xEb{s}_{b - 2}", M.get(f"xEb{s - 1}_{b + 2}", 0)))
                    for i in range(4):
                        mm = t.transpose(tpps[b % 2][:, i * 128:(i + 1) * 128],
                                         xb16[s % 2][:, (4 * b + i) * 128:(4 * b + i + 1) * 128],
                                         identb[:])
                    inc(mm, "PE")
                    M[f"xTb{s}_{b}"] = cnt["PE"]
                    if b == 3:
                        M[f"xT{s}"] = cnt["PE"]

                @VE
                def _(v, s=s, b=b):
                    W(v, "PE", M[f"xTb{s}_{b}"])
                    inc(v.tensor_copy(
                        xfm_all[:].rearrange("p (fc t) -> p fc t", fc=16)[:, 4 * b:4 * b + 4,
                                                                         s * 128:(s + 1) * 128],
                        tpps[b % 2][:].rearrange("p (fc t) -> p fc t", fc=4)), "VEC")
                    M[f"xEb{s}_{b}"] = cnt["VEC"]
                    if b == 3:
                        M[f"xE{s}"] = cnt["VEC"]

            @TE
            def _(t, s=s):
                if s == 0:
                    Wd(t, M["wkv"])
                    W(t, "VEC", M["ones"])
                W(t, "VEC", M[f"xE{s}"])
                W(t, "VEC", M.get(f"rpk{s - 1}", 0))
                for fc in range(16):
                    t.matmul(p1ps[0][:], xfm_all[:, fc * 1024 + s * 128: fc * 1024 + (s + 1) * 128],
                             Wk_sb[:, fc * 512:(fc + 1) * 512], start=(fc == 0), stop=False)
                inc(t.matmul(p1ps[0][:], ones_bf[:, 0:128], bk_b[:], start=False, stop=True), "PE")
                M[f"kp{s}"] = cnt["PE"]
                W(t, "VEC", M.get(f"vev{s - 1}", 0))
                for fc in range(16):
                    t.matmul(p1ps[1][:], xfm_all[:, fc * 1024 + s * 128: fc * 1024 + (s + 1) * 128],
                             Wv_sb[:, fc * 512:(fc + 1) * 512], start=(fc == 0), stop=False)
                inc(t.matmul(p1ps[1][:], ones_bf[:, 0:128], bv_b[:], start=False, stop=True), "PE")
                M[f"vp{s}"] = cnt["PE"]

            @VE
            def _(v, s=s):
                W(v, "PE", M[f"kp{s}"])
                W(v, "PE", M.get(f"ktT{s - 1}", 0))
                ke = p1ps[0][:, 0:512:2]
                ko = p1ps[0][:, 1:512:2]
                cs = cos4[:, s * 256:(s + 1) * 256]
                sn = sin4[:, s * 256:(s + 1) * 256]
                v.tensor_tensor(rt1[:], ke, cs, op=ALU.mult)
                inc(v.tensor_tensor(rt2[:], ko, sn, op=ALU.mult), "VEC")
                W(v, "VEC", cnt["VEC"])
                krv = kr[:].rearrange("p (h two i) -> p h two i", h=4, two=2)
                inc(v.tensor_tensor(krv[:, :, 0, :], rt1[:], rt2[:], op=ALU.subtract), "VEC")
                W(v, "VEC", cnt["VEC"])
                v.tensor_tensor(rt1[:], ke, sn, op=ALU.mult)
                inc(v.tensor_tensor(rt2[:], ko, cs, op=ALU.mult), "VEC")
                W(v, "VEC", cnt["VEC"])
                inc(v.tensor_tensor(krv[:, :, 1, :], rt1[:], rt2[:], op=ALU.add), "VEC")
                M[f"rpk{s}"] = cnt["VEC"]
                W(v, "PE", M[f"vp{s}"])
                Wd(v, M.get(f"vst{s - 1}", 0))
                inc(v.tensor_copy(vr[:], p1ps[1][:]), "VEC")
                M[f"vev{s}"] = cnt["VEC"]

            @TE
            def _(t, s=s):
                W(t, "VEC", M[f"rpk{s}"])
                W(t, "VEC", M.get(f"ktE{s - 1}", 0))
                for h in range(4):
                    mm = t.transpose(tpps[0][:, h * 128:(h + 1) * 128],
                                     kr[:, h * 128:(h + 1) * 128], identb[:])
                inc(mm, "PE")
                M[f"ktT{s}"] = cnt["PE"]

            @VE
            def _(v, s=s):
                W(v, "PE", M[f"ktT{s}"])
                Wd(v, M.get(f"ktst{s - 1}", 0))
                inc(v.tensor_copy(ktT[:], tpps[0][:]), "VEC")
                M[f"ktE{s}"] = cnt["VEC"]

            @SY
            def _(q, s=s):
                W(q, "VEC", M[f"ktE{s}"])
                dq_pre(q, "SYNQ")
                M[f"ktst{s}"] = dq_post(q.dma_start(
                    out=k_in.rearrange("(h d) t -> d h t", h=4)[:, :, s * 128:(s + 1) * 128],
                    in_=ktT[:].rearrange("d (h t) -> d h t", h=4)), "SYNQ")
                W(q, "VEC", M[f"vev{s}"])
                dq_pre(q, "SYNQ")
                M[f"vst{s}"] = dq_post(q.dma_start(out=v_in[s * 128:(s + 1) * 128, :], in_=vr[:]), "SYNQ")

        M["kvst"] = snapq("SYNQ")

        @GE
        def _(g):
            Wd(g, M["kvst"])
            inc(g.collective_compute("AllGather", ALU.bypass, replica_groups=RG,
                                     ins=[k_in], outs=[k_out]), "CC")
            inc(g.collective_compute("AllGather", ALU.bypass, replica_groups=RG,
                                     ins=[v_in], outs=[v_out]), "CC")
            M["agkv"] = cnt["CC"]

        for s in range(NS):
            @TE
            def _(t, s=s):
                if s == 0:
                    W(t, "VEC", M["wqeff"])
                    W(t, "VEC", M["bqe"])
                if s <= 1:
                    W(t, "VEC", M[f"vev{NS - 1}"])
                W(t, "VEC", M.get(f"rpq{s - 2}", 0))
                for fc in range(16):
                    t.matmul(p1ps[s % 2][:], xfm_all[:, fc * 1024 + s * 128: fc * 1024 + (s + 1) * 128],
                             Wq_eff[:, fc * 512:(fc + 1) * 512], start=(fc == 0), stop=False)
                inc(t.matmul(p1ps[s % 2][:], ones_bf[:, 0:128], bq_eff[:], start=False, stop=True), "PE")
                M[f"qp{s}"] = cnt["PE"]

            @VE
            def _(v, s=s):
                W(v, "PE", M[f"qp{s}"])
                W(v, "PE", M.get(f"qT{s - 1}", 0))
                qe = p1ps[s % 2][:, 0:512:2]
                qo = p1ps[s % 2][:, 1:512:2]
                cs = cos4[:, s * 256:(s + 1) * 256]
                sn = sin4[:, s * 256:(s + 1) * 256]
                v.tensor_tensor(rt1[:], qe, cs, op=ALU.mult)
                inc(v.tensor_tensor(rt2[:], qo, sn, op=ALU.mult), "VEC")
                W(v, "VEC", cnt["VEC"])
                qrv = qr[:].rearrange("p (h two i) -> p h two i", h=4, two=2)
                inc(v.tensor_tensor(qrv[:, :, 0, :], rt1[:], rt2[:], op=ALU.subtract), "VEC")
                W(v, "VEC", cnt["VEC"])
                v.tensor_tensor(rt1[:], qe, sn, op=ALU.mult)
                inc(v.tensor_tensor(rt2[:], qo, cs, op=ALU.mult), "VEC")
                W(v, "VEC", cnt["VEC"])
                inc(v.tensor_tensor(qrv[:, :, 1, :], rt1[:], rt2[:], op=ALU.add), "VEC")
                M[f"rpq{s}"] = cnt["VEC"]

            @TE
            def _(t, s=s):
                W(t, "VEC", M[f"rpq{s}"])
                if s == 0:
                    W(t, "VEC", M[f"xE{NS - 1}"])
                W(t, "VEC", M.get(f"qE{s - 1}", 0))
                for h in range(4):
                    mm = t.transpose(tpps[1][:, h * 128:(h + 1) * 128],
                                     qr[:, h * 128:(h + 1) * 128], identb[:])
                inc(mm, "PE")
                M[f"qT{s}"] = cnt["PE"]

            @VE
            def _(v, s=s):
                W(v, "PE", M[f"qT{s}"])
                for h in range(4):
                    mm = v.tensor_copy(qTa[:, h * 1024 + s * 128: h * 1024 + (s + 1) * 128],
                                       tpps[1][:, h * 128:(h + 1) * 128])
                inc(mm, "VEC")
                M[f"qE{s}"] = cnt["VEC"]

        M["qT"] = cnt["VEC"]

        free("rt2", "rt1", "ktT", "vr", "kr", "qr",
             "xb161", "xb160", "xbuf1", "xbuf0", "Wq_tmp1", "Wq_tmp0",
             "Wq_eff", "Wv_sb", "Wk_sb", "sin4", "cos4", "xfm_all")
        free("tpps1", "tpps0", "p1ps1", "p1ps0")

        mask_sb = salloc("mask_sb", [128, NS * 1024], BF16)

        @SY
        def _(q):
            W(q, "VEC", M["qT"])
            dq_pre(q, "SYNQ")
            M["maskl"] = dq_post(q.dma_start(out=mask_sb[:], in_=masksI), "SYNQ")

        @GE
        def _(g):
            dq_pre(g, "POOLQ")
            dq_post(g.dma_start(out=bo_b[:], in_=bo), "POOLQ")
            dq_pre(g, "POOLQ")
            dq_post(g.dma_start(out=Wo_sb[:].rearrange("p (c n) -> p c n", c=4),
                               in_=Wo.rearrange("(c p) n -> p c n", p=128)), "POOLQ")
            M["wo"] = snapq("POOLQ")

        Kh = [salloc(f"Kh{i}", [128, NC * 1024], BF16) for i in range(2)]
        Vh = [salloc(f"Vh{i}", [128, 64 * 128], BF16) for i in range(2)]
        pT = [salloc(f"pT{i}", [128, 1024], BF16) for i in range(2)]
        dnm = salloc("dnm", [1, 1024], F32)
        rb_sb = salloc("rb_sb", [128, 1024], F32)

        sT_ps = [palloc(f"sT_ps{i}", [128, 1024]) for i in range(2)]
        yA_ps = palloc("yA_ps", [128, 512])
        yB_ps = palloc("yB_ps", [128, 512])
        dn_ps = palloc("dn_ps", [128, 512])

        exp_m, mask_m, sco_m, av_m = {}, {}, {}, {}

        def kv_load(h):
            @SE
            def _(sc, h=h):
                if h == 0:
                    W(sc, "CC", M["agkv"])
                if h <= 1:
                    W(sc, "VEC", M["qT"])
                    W(sc, "PE", M[f"qT{NS - 1}"])
                W(sc, "VEC", M.get(f"ynm{h - 2}", 0))
                dq_pre(sc, "SCLQ")
                dq_post(sc.dma_start(
                    out=Kh[h % 2][:].rearrange("d (r t) -> d r t", r=NC),
                    in_=k_out.rearrange("(r h d) t -> h d r t", r=NC, h=4)[h]), "SCLQ")
                for r in range(NC):
                    dq_pre(sc, "SCLQ")
                    dq_post(sc.dma_start(
                        out=Vh[h % 2][:].rearrange("p (r t d) -> p r t d", r=NC, t=8)[:, r],
                        in_=v_out[r * 1024:(r + 1) * 1024, h * 128:(h + 1) * 128]
                        .rearrange("(t p) d -> p t d", p=128)), "SCLQ")
                M[f"kvh{h}"] = snapq("SCLQ")

        kv_load(0)
        kv_load(1)

        def attn_scores(t, h, j):
            W(t, "ACT", exp_m.get((h, j - 2), 0))
            rj, sj = kchunk_owner(j)
            qs = 128 * (j // 8)
            lhs = Kh[h % 2][:, rj * 1024 + sj * 128: rj * 1024 + (sj + 1) * 128]
            if qs < 512:
                t.matmul(sT_ps[j % 2][:, qs:512], lhs,
                         qTa[:, h * 1024 + qs: h * 1024 + 512], start=True, stop=True)
                mm = t.matmul(sT_ps[j % 2][:, 512:1024], lhs,
                              qTa[:, h * 1024 + 512: h * 1024 + 1024], start=True, stop=True)
            else:
                mm = t.matmul(sT_ps[j % 2][:, qs:1024], lhs,
                              qTa[:, h * 1024 + qs: h * 1024 + 1024], start=True, stop=True)
            sco_m[(h, j)] = inc(mm, "PE")

        def attn_av(t, h, j):
            W(t, "VEC", mask_m[(h, j)])
            rj, sj = kchunk_owner(j)
            qs = 128 * (j // 8)
            vlhs = Vh[h % 2][:, (rj * 8 + sj) * 128:(rj * 8 + sj + 1) * 128]
            if qs < 512:
                t.matmul(yA_ps[:, qs:512], vlhs, pT[j % 2][:, qs:512],
                         start=(j == 0), stop=(j == 31))
                t.matmul(yB_ps[:], vlhs, pT[j % 2][:, 512:1024],
                         start=(j == 0), stop=False)
                t.matmul(dn_ps[0:1, qs:512], ones_c[:], pT[j % 2][:, qs:512],
                         start=(j == 0), stop=(j == 31))
                mm = t.matmul(dn_ps[32:33, :], ones_c[:], pT[j % 2][:, 512:1024],
                              start=(j == 0), stop=False)
            else:
                t.matmul(yB_ps[:, qs - 512:512], vlhs, pT[j % 2][:, qs:1024],
                         start=False, stop=(j == 63))
                mm = t.matmul(dn_ps[32:33, qs - 512:512], ones_c[:],
                              pT[j % 2][:, qs:1024], start=False, stop=(j == 63))
            av_m[(h, j)] = inc(mm, "PE")

        for h in range(4):
            @TE
            def _(t, h=h):
                if h == 0:
                    W(t, "VEC", M["qT"])
                Wd(t, M[f"kvh{h}"])
                if h > 0:
                    W(t, "VEC", M[f"ynm{h - 1}"])
                attn_scores(t, h, 0)

            for j in range(64):
                @SE
                def _(sc, h=h, j=j):
                    W(sc, "PE", sco_m[(h, j)])
                    W(sc, "PE", av_m.get((h, j - 2), 0))
                    qs = 128 * (j // 8)
                    exp_m[(h, j)] = inc(sc.activation(
                        pT[j % 2][:, qs:1024], sT_ps[j % 2][:, qs:1024], AF.Exp,
                        bias=f32(-SHIFT), scale=f32(INVSQD)), "ACT")

                @VE
                def _(v, h=h, j=j):
                    if (h, j) == (0, 0):
                        Wd(v, M["maskl"])
                    W(v, "ACT", exp_m[(h, j)])
                    s0 = j // 8
                    qs = 128 * s0
                    mask_m[(h, j)] = inc(v.tensor_tensor(
                        pT[j % 2][:, qs:qs + 128], pT[j % 2][:, qs:qs + 128],
                        mask_sb[:, s0 * 1024 + (j % 8) * 128: s0 * 1024 + (j % 8 + 1) * 128],
                        op=ALU.mult), "VEC")

                if "pdbg" in taps and (h, j) == (0, 2):
                    @VE
                    def _(v):
                        W(v, "VEC", cnt["VEC"])
                        inc(v.tensor_copy(rb_sb[:], pT[0][:]), "VEC")
                        M["pdbg"] = cnt["VEC"]
                        mask_m[(0, 2)] = cnt["VEC"]

                    @GE
                    def _(g):
                        W(g, "VEC", M["pdbg"])
                        dq_pre(g, "POOLQ")
                        M["pdbgd"] = dq_post(g.dma_start(out=tap_out["pdbg"], in_=rb_sb[:]), "POOLQ")

                    @TE
                    def _(t):
                        Wd(t, M["pdbgd"])
                        t.nop()

                @TE
                def _(t, h=h, j=j):
                    if j < 63:
                        attn_scores(t, h, j + 1)
                    attn_av(t, h, j)

            @VE
            def _(v, h=h):
                W(v, "PE", av_m[(h, 63)])
                v.reciprocal(dnm[:, 0:512], dn_ps[0:1, :])
                inc(v.reciprocal(dnm[:, 512:1024], dn_ps[32:33, :]), "VEC")
                M[f"rcp{h}"] = cnt["VEC"]

            @TE
            def _(t, h=h):
                W(t, "VEC", M[f"rcp{h}"])
                W(t, "ACT", exp_m[(h, 63)])
                t.matmul(sT_ps[0][:, 0:512], ones_f[:], dnm[:, 0:512], start=True, stop=True)
                inc(t.matmul(sT_ps[0][:, 512:1024], ones_f[:], dnm[:, 512:1024],
                             start=True, stop=True), "PE")
                M[f"rbm{h}"] = cnt["PE"]

            @VE
            def _(v, h=h):
                W(v, "PE", M[f"rbm{h}"])
                inc(v.tensor_copy(rb_sb[:], sT_ps[0][:]), "VEC")
                W(v, "VEC", cnt["VEC"])
                v.tensor_tensor(ynT[:, h * 1024: h * 1024 + 512], yA_ps[:],
                                rb_sb[:, 0:512], op=ALU.mult)
                inc(v.tensor_tensor(ynT[:, h * 1024 + 512: h * 1024 + 1024], yB_ps[:],
                                    rb_sb[:, 512:1024], op=ALU.mult), "VEC")
                M[f"ynm{h}"] = cnt["VEC"]

            if h + 2 < 4:
                kv_load(h + 2)

        M["attn"] = cnt["VEC"]

        free("rb_sb", "dnm", "pT1", "pT0", "Vh1", "Vh0", "Kh1", "Kh0", "mask_sb")
        free("dn_ps", "yB_ps", "yA_ps", "sT_ps1", "sT_ps0")

        wr_bc = salloc("wr_bc", [128, C], F32)
        br_bc = salloc("br_bc", [128, 1], F32)
        sjunk = salloc("sjunk", [128, C], BF16)
        sc_row = salloc("sc_row", [NS, 128], F32)
        oPs = palloc("oPs", [128, 2048])
        tp3f = palloc("tp3f", [128, 512])

        @GE
        def _(g):
            W(g, "VEC", M["attn"])
            dq_pre(g, "POOLQ")
            dq_post(g.dma_start(out=wr_bc[0:1, :], in_=wr), "POOLQ")
            dq_pre(g, "POOLQ")
            dq_post(g.dma_start(out=br_bc[0:1, 0:1], in_=br), "POOLQ")
            M["wrl"] = snapq("POOLQ")

        for i in range(4):
            @TE
            def _(t, i=i):
                if i == 0:
                    Wd(t, M["wrl"])
                    W(t, "VEC", M["attn"])
                else:
                    W(t, "VEC", M[f"wrbe{i - 1}"])
                inc(t.matmul(tp3f[:], ones_f[:], wr_bc[0:1, i * 512:(i + 1) * 512],
                             start=True, stop=True), "PE")
                M[f"wrb{i}"] = cnt["PE"]

            @VE
            def _(v, i=i):
                W(v, "PE", M[f"wrb{i}"])
                inc(v.tensor_copy(wr_bc[:, i * 512:(i + 1) * 512], tp3f[:]), "VEC")
                M[f"wrbe{i}"] = cnt["VEC"]

        @TE
        def _(t):
            W(t, "VEC", M["wrbe3"])
            inc(t.matmul(tp3f[:, 0:1], ones_f[:], br_bc[0:1, 0:1], start=True, stop=True), "PE")
            M["brb"] = cnt["PE"]

        @VE
        def _(v):
            W(v, "PE", M["brb"])
            inc(v.tensor_copy(br_bc[:], tp3f[:, 0:1]), "VEC")
            M["brbe"] = cnt["VEC"]

        for s in range(NS):
            @SY
            def _(q, s=s):
                W(q, "VEC", M.get(f"oev{s - 1}", 0))
                dq_pre(q, "SYNQ")
                M[f"xr{s}"] = dq_post(q.dma_start(out=v1t[:], in_=x_loc[s * 128:(s + 1) * 128, :]), "SYNQ")

            @TE
            def _(t, s=s):
                if s == 0:
                    Wd(t, M["wo"])
                    W(t, "VEC", M["brbe"])
                W(t, "VEC", M.get(f"oev{s - 1}", 0))
                for ncol in range(4):
                    for h in range(4):
                        t.matmul(oPs[:, ncol * 512:(ncol + 1) * 512],
                                 ynT[:, h * 1024 + s * 128: h * 1024 + (s + 1) * 128],
                                 Wo_sb[:, h * 2048 + ncol * 512: h * 2048 + (ncol + 1) * 512],
                                 start=(h == 0), stop=False)
                    mm = t.matmul(oPs[:, ncol * 512:(ncol + 1) * 512], ones_bf[:, 0:128],
                                  bo_b[:, ncol * 512:(ncol + 1) * 512], start=False, stop=True)
                inc(mm, "PE")
                M[f"op{s}"] = cnt["PE"]

            @VE
            def _(v, s=s):
                W(v, "PE", M[f"op{s}"])
                Wd(v, M[f"xr{s}"])
                Wd(v, M.get(f"x1s{s - 1}", 0))
                for i in range(4):
                    mm = v.tensor_tensor(v1t[:, i * 512:(i + 1) * 512],
                                         v1t[:, i * 512:(i + 1) * 512],
                                         oPs[:, i * 512:(i + 1) * 512], op=ALU.add)
                inc(mm, "VEC")
                M[f"oev{s}"] = cnt["VEC"]
                W(v, "VEC", cnt["VEC"])
                for i in range(4):
                    mm = v.bn_stats(bnst[:, i * 6:(i + 1) * 6], v1t[:, i * 512:(i + 1) * 512])
                inc(mm, "VEC")
                W(v, "VEC", cnt["VEC"])
                inc(v.bn_aggr(mv[:], bnst[:]), "VEC")
                M[f"st1_{s}"] = cnt["VEC"]

            @SE
            def _(sc, s=s):
                W(sc, "VEC", M[f"st1_{s}"])
                inc(sc.activation(lnc1[:], mv[:, 1:2], AF.Ln, bias=f32(EPS), scale=f32(1.0)), "ACT")
                W(sc, "ACT", cnt["ACT"])
                inc(sc.activation(lnc1[:], lnc1[:], AF.Exp, bias=f32(0.0), scale=f32(-0.5)), "ACT")
                M[f"rs1_{s}"] = cnt["ACT"]

            @VE
            def _(v, s=s):
                W(v, "ACT", M[f"rs1_{s}"])
                inc(v.tensor_tensor(lnc2[:], mv[:, 0:1], lnc1[:], op=ALU.mult), "VEC")
                W(v, "VEC", cnt["VEC"])
                inc(v.tensor_scalar(lnc2[:], lnc2[:], -1.0, None, op0=ALU.mult), "VEC")
                W(v, "VEC", cnt["VEC"])
                inc(v.tensor_scalar(x1t[:], v1t[:], lnc1[:], lnc2[:],
                                    op0=ALU.mult, op1=ALU.add), "VEC")
                M[f"x1_{s}"] = cnt["VEC"]
                W(v, "VEC", cnt["VEC"])
                inc(v.scalar_tensor_tensor(sjunk[:], x1t[:], 1.0, wr_bc[:],
                                           op0=ALU.mult, op1=ALU.mult,
                                           accum_out=lnc2[:]), "VEC")
                W(v, "VEC", cnt["VEC"])
                inc(v.tensor_tensor(s_loc[:, s:s + 1], lnc2[:], br_bc[:], op=ALU.add), "VEC")
                M[f"sl{s}"] = cnt["VEC"]

            @SY
            def _(q, s=s):
                W(q, "VEC", M[f"x1_{s}"])
                dq_pre(q, "SYNQ")
                M[f"x1s{s}"] = dq_post(q.dma_start(out=x1_dram[s * 128:(s + 1) * 128, :], in_=x1t[:]), "SYNQ")

        M["x1d"] = snapq("SYNQ")

        sc_all = salloc("sc_all", [64, 128], F32)
        cmp_t = salloc("cmp_t", [64, 128], F32)
        rowcnt = salloc("rowcnt", [64, 1], F32)
        lo_ = salloc("lo_", [1, 1], F32)
        hi_ = salloc("hi_", [1, 1], F32)
        mid_ = salloc("mid_", [1, 1], F32)
        bbit = salloc("bbit", [1, 1], F32)
        dtt = salloc("dtt", [1, 1], F32)
        et1 = salloc("et1", [1, 1], F32)
        thr_ps = palloc("thr_ps", [64, 1])
        th128 = palloc("th128", [128, 1])
        cnt_ps = palloc("cnt_ps", [1, 1])

        @TE
        def _(t):
            W(t, "VEC", M[f"sl{NS - 1}"])
            inc(t.transpose(tp3f[0:8, 0:128], s_loc[:, 0:8], ident32[:]), "PE")
            M["slT"] = cnt["PE"]

        @VE
        def _(v):
            W(v, "PE", M["slT"])
            inc(v.tensor_copy(sc_row[:], tp3f[0:8, 0:128]), "VEC")
            M["srow"] = cnt["VEC"]

        @SY
        def _(q):
            W(q, "VEC", M["srow"])
            dq_pre(q, "SYNQ")
            M["scs"] = dq_post(q.dma_start(out=sc_in, in_=sc_row[:]), "SYNQ")

        @GE
        def _(g):
            Wd(g, M["scs"])
            inc(g.collective_compute("AllGather", ALU.bypass, replica_groups=RG,
                                     ins=[sc_in], outs=[sc_out]), "CC")
            M["agsc"] = cnt["CC"]

        @SY
        def _(q):
            W(q, "CC", M["agsc"])
            dq_pre(q, "SYNQ")
            M["scl"] = dq_post(q.dma_start(out=sc_all[:], in_=sc_out), "SYNQ")

        @VE
        def _(v):
            Wd(v, M["scl"])
            v.memset(lo_[:], -128.0)
            v.memset(hi_[:], 128.0)
            inc(v.memset(mid_[:], 0.0), "VEC")
            M["bi-1"] = cnt["VEC"]

        for it in range(BISECT):
            @TE
            def _(t, it=it):
                W(t, "VEC", M[f"bi{it - 1}"])
                inc(t.matmul(thr_ps[:], ones_f[:, 0:64], mid_[:], start=True, stop=True), "PE")
                M[f"bt{it}"] = cnt["PE"]

            @VE
            def _(v, it=it):
                W(v, "PE", M[f"bt{it}"])
                inc(v.tensor_scalar(cmp_t[:], sc_all[:], thr_ps[:], None,
                                    op0=ALU.is_ge, op1=ALU.add, accum_out=rowcnt[:]), "VEC")
                M[f"bc{it}"] = cnt["VEC"]

            @TE
            def _(t, it=it):
                W(t, "VEC", M[f"bc{it}"])
                inc(t.matmul(cnt_ps[:], rowcnt[:], ones64[:], start=True, stop=True), "PE")
                M[f"bn{it}"] = cnt["PE"]

            @VE
            def _(v, it=it):
                W(v, "PE", M[f"bn{it}"])
                inc(v.tensor_scalar(bbit[:], cnt_ps[:], 4096.0, None, op0=ALU.is_ge), "VEC")
                v.tensor_tensor(dtt[:], mid_[:], lo_[:], op=ALU.subtract)
                inc(v.tensor_tensor(et1[:], hi_[:], mid_[:], op=ALU.subtract), "VEC")
                W(v, "VEC", cnt["VEC"])
                v.scalar_tensor_tensor(lo_[:], bbit[:], dtt[:], lo_[:],
                                       op0=ALU.mult, op1=ALU.add)
                inc(v.scalar_tensor_tensor(hi_[:], bbit[:], et1[:], mid_[:],
                                           op0=ALU.mult, op1=ALU.add), "VEC")
                W(v, "VEC", cnt["VEC"])
                inc(v.tensor_tensor(mid_[:], lo_[:], hi_[:], op=ALU.add), "VEC")
                W(v, "VEC", cnt["VEC"])
                inc(v.tensor_scalar(mid_[:], mid_[:], 0.5, None, op0=ALU.mult), "VEC")
                M[f"bi{it}"] = cnt["VEC"]

        pos_i = salloc("pos_i", [128, NS], F32)
        posu = salloc("posu", [128, NS], U32)
        rc128 = salloc("rc128", [128, 1], F32)
        prow = salloc("prow", [1, 128], F32)
        poff = salloc("poff", [128, 1], F32)
        zero1 = salloc("zero1", [1, 1], F32)
        trow = salloc("trow", [128, 128], F32)
        tcol = salloc("tcol", [128, 1], U32)
        bigt = salloc("bigt", [128, 128], F32)

        @TE
        def _(t):
            W(t, "VEC", M[f"bi{BISECT - 1}"])
            inc(t.matmul(th128[:], ones_f[:], lo_[:], start=True, stop=True), "PE")
            M["th128"] = cnt["PE"]

        @VE
        def _(v):
            W(v, "PE", M["th128"])
            inc(v.tensor_scalar(m_loc[:], s_loc[:], th128[:], None, op0=ALU.is_ge), "VEC")
            W(v, "VEC", cnt["VEC"])
            inc(v.tensor_reduce(rc128[:], m_loc[:], axis=mybir.AxisListType.X, op=ALU.add), "VEC")
            M["rc"] = cnt["VEC"]

        @TE
        def _(t):
            W(t, "VEC", M["rc"])
            inc(t.transpose(tp3f[0:1, 0:128], rc128[:], ident32[:]), "PE")
            M["rcT"] = cnt["PE"]

        @VE
        def _(v):
            W(v, "PE", M["rcT"])
            v.memset(zero1[:], 0.0)
            inc(v.tensor_copy(prow[:], tp3f[0:1, 0:128]), "VEC")
            W(v, "VEC", cnt["VEC"])
            inc(v.tensor_tensor_scan(prow[:], prow[:], prow[:], zero1[:],
                                     op0=ALU.add, op1=ALU.bypass), "VEC")
            M["scan"] = cnt["VEC"]

        @TE
        def _(t):
            W(t, "VEC", M["scan"])
            inc(t.matmul(tp3f[:, 0:1], prow[:], ones_f[:, 0:1], start=True, stop=True), "PE")
            M["scanT"] = cnt["PE"]

        @VE
        def _(v):
            W(v, "PE", M["scanT"])
            inc(v.tensor_copy(poff[:], tp3f[:, 0:1]), "VEC")
            W(v, "VEC", cnt["VEC"])
            inc(v.tensor_tensor(poff[:], poff[:], rc128[:], op=ALU.subtract), "VEC")
            W(v, "VEC", cnt["VEC"])
            inc(v.tensor_tensor_scan(pos_i[:], m_loc[:], m_loc[:], zerop[:],
                                     op0=ALU.add, op1=ALU.bypass), "VEC")
            W(v, "VEC", cnt["VEC"])
            inc(v.scalar_tensor_tensor(pos_i[:], pos_i[:], poff[:], m_loc[:],
                                       op0=ALU.add, op1=ALU.subtract), "VEC")
            W(v, "VEC", cnt["VEC"])
            inc(v.tensor_scalar(bigt[:, 0:NS], m_loc[:], -1.0e9, 1.0e9,
                                op0=ALU.mult, op1=ALU.add), "VEC")
            W(v, "VEC", cnt["VEC"])
            inc(v.tensor_tensor(pos_i[:], pos_i[:], bigt[:, 0:NS], op=ALU.add), "VEC")
            W(v, "VEC", cnt["VEC"])
            inc(v.tensor_copy(posu[:], pos_i[:]), "VEC")
            W(v, "VEC", cnt["VEC"])
            v.memset(trow[:], 0.0)
            inc(v.memset(bigt[:], 2.0e9), "VEC")
            M["posu"] = cnt["VEC"]

        @GE
        def _(g):
            W(g, "VEC", M["posu"])
            for s in range(NS):
                dq_pre(g, "POOLQ")
                dq_post(g.dma_start(out=gbuf[s * 128:(s + 1) * 128, :], in_=bigt[:]), "POOLQ")
            M["gfill"] = snapq("POOLQ")

        for s in range(NS):
            @GE
            def _(g, s=s):
                if s == 0:
                    Wd(g, M["gfill"])
                W(g, "VEC", M.get(f"tcast{s - 1}", 0))
                inc(g.iota(tcol[:], pattern=[[0, 1]], base=s * 128, channel_multiplier=1), "POOL")
                M[f"iot{s}"] = cnt["POOL"]

            @VE
            def _(v, s=s):
                W(v, "POOL", M[f"iot{s}"])
                Wd(v, M.get(f"gsc{s - 1}", 0))
                inc(v.tensor_copy(trow[:, 0:1], tcol[:]), "VEC")
                M[f"tcast{s}"] = cnt["VEC"]

            @GE
            def _(g, s=s):
                W(g, "VEC", M[f"tcast{s}"])
                dq_pre(g, "POOLQ")
                M[f"gsc{s}"] = dq_post(g.indirect_dma_start(
                    out=gbuf, out_offset=IndirectOffsetOnAxis(ap=posu[:, s:s + 1], axis=0),
                    in_=trow[:], in_offset=None,
                    bounds_check=LT - 1, oob_is_err=False), "POOLQ")

        @SY
        def _(q):
            Wd(q, M[f"gsc{NS - 1}"])
            dq_pre(q, "SYNQ")
            M["gld"] = dq_post(q.dma_start(out=gf32[:].rearrange("p (i o) -> p i o", o=1),
                               in_=gbuf[0:CAP, 0:1].rearrange("(i p) o -> p i o", p=128),
                               allow_slow_non_contiguous=True), "SYNQ")

        @VE
        def _(v):
            Wd(v, M["gld"])
            inc(v.tensor_copy(gidx[:], gf32[:]), "VEC")
            M["gidx"] = cnt["VEC"]

        if taps:
            @GE
            def _(g):
                if "x1" in taps:
                    Wd(g, M["x1d"])
                    dq_pre(g, "POOLQ")
                    dq_post(g.dma_start(out=tap_out["x1"], in_=x1_dram), "POOLQ")
                if "thr" in taps:
                    W(g, "VEC", M[f"bi{BISECT - 1}"])
                    dq_pre(g, "POOLQ")
                    dq_post(g.dma_start(out=tap_out["thr"], in_=lo_[:]), "POOLQ")
                if "mask" in taps:
                    W(g, "VEC", M["posu"])
                    dq_pre(g, "POOLQ")
                    dq_post(g.dma_start(out=tap_out["mask"], in_=m_loc[:]), "POOLQ")
                if "scores" in taps:
                    Wd(g, M["scl"])
                    dq_pre(g, "POOLQ")
                    dq_post(g.dma_start(out=tap_out["scores"], in_=sc_all[:]), "POOLQ")
                if "gidx" in taps:
                    W(g, "VEC", M["gidx"])
                    dq_pre(g, "POOLQ")
                    dq_post(g.dma_start(out=tap_out["gidx"], in_=gf32[:]), "POOLQ")
                if "ynT" in taps:
                    W(g, "VEC", M["attn"])
                    dq_pre(g, "POOLQ")
                    dq_post(g.dma_start(out=tap_out["ynT"], in_=ynT[:]), "POOLQ")
                M["tapd"] = snapq("POOLQ")

        free("bigt", "tcol", "trow", "zero1", "poff", "prow", "rc128", "posu",
             "pos_i", "et1", "dtt", "bbit", "mid_", "hi_", "lo_", "rowcnt",
             "cmp_t", "sc_all", "sc_row", "sjunk", "br_bc", "wr_bc")
        free("cnt_ps", "th128", "thr_ps", "tp3f", "oPs")
        free("bo_b", "Wo_sb", "ynT", "qTa")

        xselb = [salloc(f"xselb{i}", [128, C], BF16) for i in range(5)]
        b2_b = salloc("b2_b", [1, C], BF16)
        hT = salloc("hT", [128, 64 * CAP], BF16)
        b1_b = salloc("b1_b", [1, DF], BF16)
        xsT = salloc("xsT", [128, 16 * CAP], BF16)
        w1b = [salloc(f"w1b{i}", [128, 16 * 256], BF16) for i in range(2)]
        tp5 = [palloc(f"tp5{i}", [128, 512], BF16) for i in range(2)]
        psU = [palloc(f"psU{i}", [128, CAP]) for i in range(2)]

        @GE
        def _(g):
            W(g, "VEC", M["gidx"])
            Wd(g, snapq("POOLQ"))
            dq_pre(g, "POOLQ")
            dq_post(g.dma_start(out=b1_b[:], in_=b1), "POOLQ")
            dq_pre(g, "POOLQ")
            dq_post(g.dma_start(out=b2_b[:], in_=b2), "POOLQ")
            for i in range(2):
                dq_pre(g, "POOLQ")
                M[f"w1l{i}"] = dq_post(g.dma_start(
                    out=w1b[i][:].rearrange("p (c n) -> p c n", c=16),
                    in_=w1[:, i * 256:(i + 1) * 256].rearrange("(c p) n -> p c n", p=128)), "POOLQ")

        gat = {}
        for i in range(5):
            @VE
            def _(v, i=i):
                if i == 0:
                    W(v, "VEC", M["gidx"])
                    Wd(v, M.get("tapd", 0))
                gat[("z", i)] = inc(v.memset(xselb[i][:], 0.0), "VEC")

            @GE
            def _(g, i=i):
                if i == 0:
                    Wd(g, M["x1d"])
                W(g, "VEC", gat[("z", i)])
                dq_pre(g, "POOLQ")
                gat[("g", i)] = dq_post(g.indirect_dma_start(
                    out=xselb[i][:], out_offset=None,
                    in_=x1_dram, in_offset=IndirectOffsetOnAxis(ap=gidx[:, i:i + 1], axis=0),
                    bounds_check=LT - 1, oob_is_err=False), "POOLQ")

        for i in range(5):
            for b in range(4):
                @TE
                def _(t, i=i, b=b):
                    Wd(t, gat[("g", i)])
                    W(t, "VEC", gat.get(("e", i, b - 2), gat.get(("e", i - 1, b + 2), 0)))
                    for k in range(4):
                        mm = t.transpose(tp5[b % 2][:, k * 128:(k + 1) * 128],
                                         xselb[i][:, (4 * b + k) * 128:(4 * b + k + 1) * 128],
                                         identb[:])
                    inc(mm, "PE")
                    gat[("t", i, b)] = cnt["PE"]

                @VE
                def _(v, i=i, b=b):
                    W(v, "PE", gat[("t", i, b)])
                    mm = v.tensor_copy(
                        xsT[:].rearrange("p (fc t) -> p fc t", fc=16)[:, 4 * b:4 * b + 4,
                                                                     i * 128:(i + 1) * 128],
                        tp5[b % 2][:].rearrange("p (fc t) -> p fc t", fc=4))
                    gat[("e", i, b)] = inc(mm, "VEC")
        M["xsT"] = cnt["VEC"]

        mm1 = {}
        for ds in range(32):
            @TE
            def _(t, ds=ds):
                if ds == 0:
                    W(t, "VEC", M["xsT"])
                Wd(t, M[f"w1l{ds}"] if ds < 2 else mm1[("w1l", ds)])
                for dfc in range(2):
                    g = 2 * ds + dfc
                    W(t, "ACT", mm1.get(("gel", g - 2), 0))
                    for fc in range(16):
                        lhs = w1b[ds % 2][:, fc * 256 + dfc * 128: fc * 256 + (dfc + 1) * 128]
                        t.matmul(psU[g % 2][:, 0:512], lhs, xsT[:, fc * CAP: fc * CAP + 512],
                                 start=(fc == 0), stop=False)
                        t.matmul(psU[g % 2][:, 512:CAP], lhs,
                                 xsT[:, fc * CAP + 512: fc * CAP + CAP],
                                 start=(fc == 0), stop=False)
                    boff = ds * 256 + dfc * 128
                    t.matmul(psU[g % 2][:, 0:512], b1_b[:, boff:boff + 128],
                             ones_bf[:, 0:512], start=False, stop=True)
                    mm = t.matmul(psU[g % 2][:, 512:CAP], b1_b[:, boff:boff + 128],
                                  ones_bf[:, 0:128], start=False, stop=True)
                    mm1[("u", g)] = inc(mm, "PE")

            @SE
            def _(sc, ds=ds):
                for dfc in range(2):
                    g = 2 * ds + dfc
                    W(sc, "PE", mm1[("u", g)])
                    mm1[("gel", g)] = inc(sc.activation(
                        hT[:, g * CAP:(g + 1) * CAP], psU[g % 2][:], AF.Gelu_apprx_tanh), "ACT")

            if ds + 2 < 32:
                @GE
                def _(g, ds=ds):
                    W(g, "PE", mm1[("u", 2 * ds + 1)])
                    dq_pre(g, "POOLQ")
                    mm = g.dma_start(
                        out=w1b[ds % 2][:].rearrange("p (c n) -> p c n", c=16),
                        in_=w1[:, (ds + 2) * 256:(ds + 3) * 256].rearrange("(c p) n -> p c n", p=128))
                    mm1[("w1l", ds + 2)] = dq_post(mm, "POOLQ")

        M["gelu"] = cnt["ACT"]

        free("w1b1", "w1b0", "xsT", "b1_b")
        free("psU1", "psU0", "tp51", "tp50")
        w2cb = [salloc(f"w2cb{i}", [128, 64 * 128], BF16) for i in range(2)]
        psO2 = [palloc(f"psO2{i}", [128, 128]) for i in range(2)]

        mm2 = {}
        for nb in range(2):
            @GE
            def _(g, nb=nb):
                W(g, "ACT", M["gelu"] if nb == 0 else 0)
                dq_pre(g, "POOLQ")
                mm = g.dma_start(
                    out=w2cb[nb % 2][:].rearrange("p (c n) -> p c n", c=64),
                    in_=w2[:, nb * 128:(nb + 1) * 128].rearrange("(c p) n -> p c n", p=128))
                mm2[("w2l", nb)] = dq_post(mm, "POOLQ")

        for nb in range(16):
            for sl in range(5):
                @TE
                def _(t, nb=nb, sl=sl):
                    g = nb * 5 + sl
                    if sl == 0:
                        Wd(t, mm2[("w2l", nb)])
                    if g >= 2:
                        W(t, "VEC", mm2[("ev", g - 2)])
                    for dfc in range(64):
                        t.matmul(psO2[g % 2][:],
                                 hT[:, dfc * CAP + sl * 128: dfc * CAP + (sl + 1) * 128],
                                 w2cb[nb % 2][:, dfc * 128:(dfc + 1) * 128],
                                 start=(dfc == 0), stop=False)
                    mm = t.matmul(psO2[g % 2][:], ones_bf[0:1, 0:128],
                                  b2_b[:, nb * 128:(nb + 1) * 128], start=False, stop=True)
                    mm2[("mm", g)] = inc(mm, "PE")
                    if sl == 4:
                        mm2[("mmn", nb)] = cnt["PE"]

                @VE
                def _(v, nb=nb, sl=sl):
                    g = nb * 5 + sl
                    W(v, "PE", mm2[("mm", g)])
                    mm2[("ev", g)] = inc(v.tensor_tensor(
                        xselb[sl][:, nb * 128:(nb + 1) * 128], psO2[g % 2][:],
                        xselb[sl][:, nb * 128:(nb + 1) * 128], op=ALU.add), "VEC")

            if nb + 2 < 16:
                @GE
                def _(g, nb=nb):
                    W(g, "PE", mm2[("mmn", nb)])
                    dq_pre(g, "POOLQ")
                    mm = g.dma_start(
                        out=w2cb[nb % 2][:].rearrange("p (c n) -> p c n", c=64),
                        in_=w2[:, (nb + 2) * 128:(nb + 3) * 128].rearrange("(c p) n -> p c n", p=128))
                    mm2[("w2l", nb + 2)] = dq_post(mm, "POOLQ")

        M["mm2e"] = cnt["VEC"]

        ln = {}
        for s in range(NS):
            @SY
            def _(q, s=s):
                W(q, "VEC", ln.get(("ub", s - 1), 0))
                Wd(q, ln.get(("st", s - 1), 0))
                dq_pre(q, "SYNQ")
                ln[("xr", s)] = dq_post(q.dma_start(out=v1t[:], in_=x1_dram[s * 128:(s + 1) * 128, :]), "SYNQ")

            @VE
            def _(v, s=s):
                Wd(v, ln[("xr", s)])
                for i in range(4):
                    mm = v.bn_stats(bnst[:, i * 6:(i + 1) * 6], v1t[:, i * 512:(i + 1) * 512])
                inc(mm, "VEC")
                W(v, "VEC", cnt["VEC"])
                inc(v.bn_aggr(mv[:], bnst[:]), "VEC")
                ln[("stat", s)] = cnt["VEC"]
                ln[("ub", s)] = cnt["VEC"]

            @SE
            def _(sc, s=s):
                W(sc, "VEC", ln[("stat", s)])
                inc(sc.activation(lnc1[:], mv[:, 1:2], AF.Ln, bias=f32(EPS), scale=f32(1.0)), "ACT")
                W(sc, "ACT", cnt["ACT"])
                inc(sc.activation(lnc1[:], lnc1[:], AF.Exp, bias=f32(0.0), scale=f32(-0.5)), "ACT")
                ln[("rs", s)] = cnt["ACT"]

            @VE
            def _(v, s=s):
                W(v, "ACT", ln[("rs", s)])
                inc(v.tensor_tensor(lnc2[:], mv[:, 0:1], lnc1[:], op=ALU.mult), "VEC")
                W(v, "VEC", cnt["VEC"])
                inc(v.tensor_scalar(lnc2[:], lnc2[:], -1.0, None, op0=ALU.mult), "VEC")
                W(v, "VEC", cnt["VEC"])
                inc(v.tensor_scalar(v1t[:], v1t[:], lnc1[:], lnc2[:],
                                    op0=ALU.mult, op1=ALU.add), "VEC")
                ln[("y", s)] = cnt["VEC"]

            @SY
            def _(q, s=s):
                W(q, "VEC", ln[("y", s)])
                dq_pre(q, "SYNQ")
                ln[("st", s)] = dq_post(q.dma_start(out=out[s * 128:(s + 1) * 128, :], in_=v1t[:]), "SYNQ")

        M["base_out"] = snapq("SYNQ")

        sel = {}
        for sl in range(5):
            @VE
            def _(v, sl=sl):
                if sl == 0:
                    W(v, "VEC", M["mm2e"])
                Wd(v, sel.get(("sc", sl - 1), 0))
                for i in range(4):
                    mm = v.bn_stats(bnst[:, i * 6:(i + 1) * 6], xselb[sl][:, i * 512:(i + 1) * 512])
                inc(mm, "VEC")
                W(v, "VEC", cnt["VEC"])
                inc(v.bn_aggr(mv[:], bnst[:]), "VEC")
                sel[("stat", sl)] = cnt["VEC"]

            @SE
            def _(sc, sl=sl):
                W(sc, "VEC", sel[("stat", sl)])
                inc(sc.activation(lnc1[:], mv[:, 1:2], AF.Ln, bias=f32(EPS), scale=f32(1.0)), "ACT")
                W(sc, "ACT", cnt["ACT"])
                inc(sc.activation(lnc1[:], lnc1[:], AF.Exp, bias=f32(0.0), scale=f32(-0.5)), "ACT")
                sel[("rs", sl)] = cnt["ACT"]

            @VE
            def _(v, sl=sl):
                W(v, "ACT", sel[("rs", sl)])
                inc(v.tensor_tensor(lnc2[:], mv[:, 0:1], lnc1[:], op=ALU.mult), "VEC")
                W(v, "VEC", cnt["VEC"])
                inc(v.tensor_scalar(lnc2[:], lnc2[:], -1.0, None, op0=ALU.mult), "VEC")
                W(v, "VEC", cnt["VEC"])
                inc(v.tensor_scalar(x1t[:], xselb[sl][:], lnc1[:], lnc2[:],
                                    op0=ALU.mult, op1=ALU.add), "VEC")
                sel[("y", sl)] = cnt["VEC"]

            @GE
            def _(g, sl=sl):
                if sl == 0:
                    Wd(g, M["base_out"])
                W(g, "VEC", sel[("y", sl)])
                dq_pre(g, "POOLQ")
                sel[("sc", sl)] = dq_post(g.indirect_dma_start(
                    out=out, out_offset=IndirectOffsetOnAxis(ap=gidx[:, sl:sl + 1], axis=0),
                    in_=x1t[:], in_offset=None,
                    bounds_check=LT - 1, oob_is_err=False), "POOLQ")

        @GE
        def _(g):
            Wd(g, snapq("SYNQ"))
            Wd(g, snapq("POOLQ"))
            Wd(g, snapq("SCLQ"))
            W(g, "PE", cnt["PE"])
            W(g, "ACT", cnt["ACT"])
            W(g, "VEC", cnt["VEC"])
            W(g, "POOL", cnt["POOL"])

    for n in reversed(list(opens)):
        opens.pop(n).__exit__(None, None, None)
    return nc



def _host_consts(c):
    tiles = [q_tile(c, s) for s in range(NS)]
    rows = np.concatenate([np.arange(g * 128, (g + 1) * 128) for g in tiles])
    inv = 10000.0 ** (-(np.arange(64, dtype=np.float64)) / 64.0)
    ang = rows[:, None].astype(np.float64) * inv[None, :]
    cosr = np.tile(np.cos(ang).astype(np.float32), (1, 4))
    sinr = np.tile(np.sin(ang).astype(np.float32), (1, 4))
    masks = np.zeros((128, NS * 1024), dtype=ml_dtypes.bfloat16)
    for s in range(NS):
        pi = (s + c) % NC
        kk = np.arange(128)[:, None]
        jq = np.arange(1024)[None, :]
        jj, ql = jq // 128, jq % 128
        allowed = (jj * 128 + kk) <= (pi * 128 + ql)
        masks[:, s * 1024:(s + 1) * 1024] = allowed.astype(ml_dtypes.bfloat16)
    return rows, cosr, sinr, masks


def kernel(L0_attn_x, L0_attn_Wq, L0_attn_bq, L0_attn_Wk, L0_attn_bk,
           L0_attn_Wv, L0_attn_bv, L0_attn_Wo, L0_attn_bo,
           L0_ffn_w_r, L0_ffn_b_r, L0_ffn_w1, L0_ffn_b1, L0_ffn_w2, L0_ffn_b2):
    x = np.ascontiguousarray(np.asarray(L0_attn_x, dtype=np.float32).reshape(T, C))
    com = dict(
        Wq=np.ascontiguousarray(np.asarray(L0_attn_Wq, np.float32)),
        bq=np.asarray(L0_attn_bq, np.float32).reshape(1, C),
        Wk=np.ascontiguousarray(np.asarray(L0_attn_Wk, np.float32)),
        bk=np.asarray(L0_attn_bk, np.float32).reshape(1, DKV),
        Wv=np.ascontiguousarray(np.asarray(L0_attn_Wv, np.float32)),
        bv=np.asarray(L0_attn_bv, np.float32).reshape(1, DKV),
        Wo=np.ascontiguousarray(np.asarray(L0_attn_Wo, np.float32)),
        bo=np.asarray(L0_attn_bo, np.float32).reshape(1, C),
        w1=np.ascontiguousarray(np.asarray(L0_ffn_w1, np.float32)),
        b1=np.asarray(L0_ffn_b1, np.float32).reshape(1, DF),
        w2=np.ascontiguousarray(np.asarray(L0_ffn_w2, np.float32)),
        b2=np.asarray(L0_ffn_b2, np.float32).reshape(1, C),
        wr=np.asarray(L0_ffn_w_r, np.float32).reshape(1, C),
        br=np.asarray(L0_ffn_b_r, np.float32).reshape(1, 1),
        ident=np.eye(128, dtype=ml_dtypes.bfloat16),
    )
    in_maps = []
    row_lists = []
    for c in range(NC):
        rows, cosr, sinr, masks = _host_consts(c)
        row_lists.append(rows)
        m = dict(com)
        m["x_loc"] = np.ascontiguousarray(x[rows])
        m["cosT"] = np.ascontiguousarray(cosr)
        m["sinT"] = np.ascontiguousarray(sinr)
        m["masks"] = np.ascontiguousarray(masks)
        in_maps.append(m)

    nc_prog = build_program(taps=_TAPS)
    res = run_bass_kernel_spmd(nc_prog, in_maps, core_ids=list(range(NC)))

    full = np.empty((T, C), dtype=np.float32)
    for c in range(NC):
        full[row_lists[c]] = res.results[c]["out"]
    kernel._last_results = res
    return full.reshape(1, T, C)


# version 02
# speedup vs baseline: 1.0090x; 1.0090x over previous
import numpy as np
import ml_dtypes

import concourse.bass as bass
import concourse.mybir as mybir
from concourse.bass import IndirectOffsetOnAxis
from concourse.bass_utils import run_bass_kernel_spmd

F32 = mybir.dt.float32
BF16 = mybir.dt.bfloat16
U32 = mybir.dt.uint32
AF = mybir.ActivationFunctionType
ALU = mybir.AluOpType
f32 = np.float32

T, C, NKV, DH, DKV, DF = 8192, 2048, 4, 128, 512, 8192
NC, NS = 8, 8
LT = NS * 128
CAP = 640
EPS = 1e-5
SHIFT = 8.0
INVSQD = float(1.0 / np.sqrt(128.0))
BISECT = 48
RG = [list(range(NC))]

_TAPS = ()


def q_tile(c, s):
    return 8 * s + ((s + c) % NC)


def kchunk_owner(j):
    s = j // 8
    return (j % 8 - s) % NC, s


def build_program(taps=()):
    nc = bass.Bass("TRN2", target_bir_lowering=False)

    dI = lambda n, s, d=F32: nc.dram_tensor(n, s, d, kind="ExternalInput").ap()
    x_loc = dI("x_loc", [LT, C])
    Wq = dI("Wq", [C, C]); bq = dI("bq", [1, C])
    Wk = dI("Wk", [C, DKV]); bk = dI("bk", [1, DKV])
    Wv = dI("Wv", [C, DKV]); bv = dI("bv", [1, DKV])
    Wo = dI("Wo", [DKV, C]); bo = dI("bo", [1, C])
    w1 = dI("w1", [C, DF]); b1 = dI("b1", [1, DF])
    w2 = dI("w2", [DF, C]); b2 = dI("b2", [1, C])
    wr = dI("wr", [1, C]); br = dI("br", [1, 1])
    cosT = dI("cosT", [LT, 256])
    sinT = dI("sinT", [LT, 256])
    masksI = dI("masks", [128, NS * 1024], BF16)
    identI = dI("ident", [128, 128], BF16)

    out = nc.dram_tensor("out", [LT, C], F32, kind="ExternalOutput").ap()
    tap_shapes = {"x1": [LT, C], "thr": [1, 1], "mask": [128, NS],
                  "scores": [64, 128], "gidx": [128, 5], "ynT": [128, NKV * 1024],
                  "pdbg": [128, 1024]}
    tap_out = {t: nc.dram_tensor("tap_" + t, tap_shapes[t], F32, kind="ExternalOutput").ap()
               for t in taps}

    k_in = nc.dram_tensor("k_in", [DKV, LT], BF16).ap()
    k_out = nc.dram_tensor("k_out", [NC * DKV, LT], BF16, addr_space="Shared").ap()
    v_in = nc.dram_tensor("v_in", [LT, DKV], BF16).ap()
    v_out = nc.dram_tensor("v_out", [NC * LT, DKV], BF16, addr_space="Shared").ap()
    sc_in = nc.dram_tensor("sc_in", [NS, 128], F32).ap()
    sc_out = nc.dram_tensor("sc_out", [NC * NS, 128], F32, addr_space="Shared").ap()
    x1_dram = nc.dram_tensor("x1_dram", [LT, C], F32).ap()
    gbuf = nc.dram_tensor("gbuf", [LT, NS], F32).ap()

    sems = {}
    cnt = {}
    opens = {}

    def semalloc(k):
        cm = nc.semaphore("sem_" + k)
        opens["sem_" + k] = cm
        sems[k] = cm.__enter__()
        cnt[k] = 0

    for k in ("PE", "VEC", "ACT", "POOL", "CC"):
        semalloc(k)

    def inc(instr, k, n=1):
        instr.then_inc(sems[k], n)
        cnt[k] += n
        return cnt[k]

    dqs = {}
    for qn, dep in (("SYNQ", 4), ("POOLQ", 4), ("SCLQ", 4)):
        slots = []
        for i in range(dep):
            cm = nc.semaphore(f"q_{qn}{i}")
            opens[f"q_{qn}{i}"] = cm
            slots.append(cm.__enter__())
        dqs[qn] = {"sems": slots, "val": [0] * dep, "n": 0, "d": dep}

    def dq_pre(eng, qn):
        q = dqs[qn]
        i = q["n"] % q["d"]
        if q["val"][i] > 0:
            eng.wait_ge(q["sems"][i], q["val"][i])

    def dq_post(instr, qn):
        q = dqs[qn]
        i = q["n"] % q["d"]
        q["val"][i] += 16
        instr.then_inc(q["sems"][i], 16)
        q["n"] += 1
        return (qn, i, q["val"][i])

    def snapq(qn):
        q = dqs[qn]
        return [(qn, i, q["val"][i]) for i in range(q["d"]) if q["val"][i] > 0]

    def salloc(n, s, d):
        cm = nc.sbuf_tensor(n, s, d)
        opens[n] = cm
        return cm.__enter__()

    def palloc(n, s, d=F32):
        cm = nc.psum_tensor(n, s, d)
        opens[n] = cm
        return cm.__enter__()

    def free(*names):
        for n in names:
            opens.pop(n).__exit__(None, None, None)

    identb = salloc("identb", [128, 128], BF16)
    ident32 = salloc("ident32", [128, 128], F32)
    ones_bf = salloc("ones_bf", [1, 512], BF16)
    ones_f = salloc("ones_f", [1, 128], F32)
    ones_c = salloc("ones_c", [128, 1], BF16)
    ones64 = salloc("ones64", [64, 1], F32)
    zerop = salloc("zerop", [128, 1], F32)
    bq_eff = salloc("bq_eff", [1, DKV], BF16)
    bk_b = salloc("bk_b", [1, DKV], BF16)
    bv_b = salloc("bv_b", [1, DKV], BF16)
    s_loc = salloc("s_loc", [128, NS], F32)
    m_loc = salloc("m_loc", [128, NS], F32)
    bnst = salloc("bnst", [128, 24], F32)
    mv = salloc("mv", [128, 2], F32)
    lnc1 = salloc("lnc1", [128, 1], F32)
    lnc2 = salloc("lnc2", [128, 1], F32)
    gidx = salloc("gidx", [128, 5], U32)
    gf32 = salloc("gf32", [128, 5], F32)
    x1t = salloc("x1t", [128, C], F32)
    v1t = salloc("v1t", [128, C], F32)

    qTa = salloc("qTa", [128, NKV * 1024], BF16)
    ynT = salloc("ynT", [128, NKV * 1024], BF16)
    Wo_sb = salloc("Wo_sb", [128, NKV * C], BF16)
    bo_b = salloc("bo_b", [1, C], BF16)

    xfm_all = salloc("xfm_all", [128, 16 * 1024], BF16)
    cos4 = salloc("cos4", [128, NS * 256], F32)
    sin4 = salloc("sin4", [128, NS * 256], F32)
    Wk_sb = salloc("Wk_sb", [128, 16 * 512], BF16)
    Wv_sb = salloc("Wv_sb", [128, 16 * 512], BF16)
    Wq_eff = salloc("Wq_eff", [128, 16 * 512], BF16)
    Wq_tmp = [salloc(f"Wq_tmp{i}", [128, C], BF16) for i in range(2)]
    xbuf = [salloc(f"xbuf{i}", [128, C], F32) for i in range(2)]
    xb16 = [salloc(f"xb16{i}", [128, C], BF16) for i in range(2)]
    qr = salloc("qr", [128, DKV], BF16)
    kr = salloc("kr", [128, DKV], BF16)
    vr = salloc("vr", [128, DKV], BF16)
    ktT = salloc("ktT", [128, NKV * 128], BF16)
    rt1 = salloc("rt1", [128, 256], F32)
    rt2 = salloc("rt2", [128, 256], F32)

    btmp = salloc("btmp", [1, C], F32)
    bt1 = salloc("bt1", [1, 512], F32)
    bt2 = salloc("bt2", [1, 512], F32)

    p1ps = [palloc(f"p1ps{i}", [128, 512]) for i in range(2)]
    tpps = [palloc(f"tpps{i}", [128, 512], BF16) for i in range(2)]

    M = {}

    _scope_state = {"cm": None}

    def _scope_switch(name):
        if _scope_state["cm"] is not None:
            _scope_state["cm"].__exit__(None, None, None)
        cm = nc.named_scope(name)
        cm.__enter__()
        _scope_state["cm"] = cm

    with nc.Block() as block:
        TE, VE, SE, GE, SY = block.tensor, block.vector, block.scalar, block.gpsimd, block.sync

        def W(eng, k, v):
            if v > 0:
                eng.wait_ge(sems[k], v)

        def Wd(eng, h):
            if isinstance(h, list):
                for hh in h:
                    Wd(eng, hh)
            elif isinstance(h, tuple):
                qn, i, v = h
                eng.wait_ge(dqs[qn]["sems"][i], v)

        def dma(eng, qn, instr_fn):
            dq_pre(eng, qn)
            return dq_post(instr_fn(), qn)

        @SY
        def _(q):
            dq_pre(q, "SYNQ")
            M["ident"] = dq_post(q.dma_start(out=identb[:], in_=identI), "SYNQ")
            dq_pre(q, "SYNQ")
            dq_post(q.dma_start(out=cos4[:].rearrange("p (s f) -> p s f", s=NS),
                               in_=cosT.rearrange("(s p) f -> p s f", p=128)), "SYNQ")
            dq_pre(q, "SYNQ")
            dq_post(q.dma_start(out=sin4[:].rearrange("p (s f) -> p s f", s=NS),
                               in_=sinT.rearrange("(s p) f -> p s f", p=128)), "SYNQ")
            dq_pre(q, "SYNQ")
            dq_post(q.dma_start(out=btmp[:], in_=bq), "SYNQ")
            M["p0"] = snapq("SYNQ")

        @GE
        def _(g):
            dq_pre(g, "POOLQ")
            dq_post(g.dma_start(out=Wk_sb[:].rearrange("p (c n) -> p c n", c=16),
                               in_=Wk.rearrange("(c p) n -> p c n", p=128)), "POOLQ")
            dq_pre(g, "POOLQ")
            dq_post(g.dma_start(out=Wv_sb[:].rearrange("p (c n) -> p c n", c=16),
                               in_=Wv.rearrange("(c p) n -> p c n", p=128)), "POOLQ")
            dq_pre(g, "POOLQ")
            dq_post(g.dma_start(out=bk_b[:], in_=bk), "POOLQ")
            dq_pre(g, "POOLQ")
            dq_post(g.dma_start(out=bv_b[:], in_=bv), "POOLQ")
            M["wkv"] = snapq("POOLQ")

        @VE
        def _(v):
            v.memset(ones_bf[:], 1.0)
            v.memset(ones_f[:], 1.0)
            v.memset(ones_c[:], 1.0)
            v.memset(ones64[:], 1.0)
            inc(v.memset(zerop[:], 0.0), "VEC")
            M["ones"] = cnt["VEC"]
            Wd(v, M["ident"])
            inc(v.tensor_copy(ident32[:], identb[:]), "VEC")
            Wd(v, M["p0"])
            bqv = btmp[:].rearrange("o (h j i) -> o h j i", h=4, j=4)
            v.tensor_tensor(bt1[:].rearrange("o (h i) -> o h i", h=4),
                            bqv[:, :, 0, :], bqv[:, :, 1, :], op=ALU.add)
            inc(v.tensor_tensor(bt2[:].rearrange("o (h i) -> o h i", h=4),
                                bqv[:, :, 2, :], bqv[:, :, 3, :], op=ALU.add), "VEC")
            W(v, "VEC", cnt["VEC"])
            inc(v.tensor_tensor(bq_eff[:], bt1[:], bt2[:], op=ALU.add), "VEC")
            M["bqe"] = cnt["VEC"]

        for i in range(16):
            @GE
            def _(g, i=i):
                W(g, "VEC", M.get(f"wqe{i - 2}", 0))
                dq_pre(g, "POOLQ")
                M[f"wql{i}"] = dq_post(g.dma_start(out=Wq_tmp[i % 2][:],
                                   in_=Wq[i * 128:(i + 1) * 128, :]), "POOLQ")

            @VE
            def _(v, i=i):
                Wd(v, M[f"wql{i}"])
                wv = Wq_tmp[i % 2][:].rearrange("p (h j i) -> p h j i", h=4, j=4)
                eff = Wq_eff[:, i * 512:(i + 1) * 512].rearrange("p (h i) -> p h i", h=4)
                inc(v.tensor_tensor(eff, wv[:, :, 0, :], wv[:, :, 1, :], op=ALU.add), "VEC")
                W(v, "VEC", cnt["VEC"])
                inc(v.tensor_tensor(eff, eff, wv[:, :, 2, :], op=ALU.add), "VEC")
                W(v, "VEC", cnt["VEC"])
                inc(v.tensor_tensor(eff, eff, wv[:, :, 3, :], op=ALU.add), "VEC")
                M[f"wqe{i}"] = cnt["VEC"]
        M["wqeff"] = cnt["VEC"]
        free("bt2", "bt1", "btmp")

        _scope_switch('P1')
        for s in range(NS):
            @SY
            def _(q, s=s):
                W(q, "VEC", M.get(f"xc{s - 2}", 0))
                dq_pre(q, "SYNQ")
                M[f"xl{s}"] = dq_post(q.dma_start(out=xbuf[s % 2][:], in_=x_loc[s * 128:(s + 1) * 128, :]), "SYNQ")

            @VE
            def _(v, s=s):
                Wd(v, M[f"xl{s}"])
                W(v, "PE", M.get(f"xT{s - 2}", 0))
                inc(v.tensor_copy(xb16[s % 2][:], xbuf[s % 2][:]), "VEC")
                M[f"xc{s}"] = cnt["VEC"]

            for b in range(4):
                @TE
                def _(t, s=s, b=b):
                    W(t, "VEC", M[f"xc{s}"])
                    W(t, "VEC", M.get(f"xEb{s}_{b - 2}", M.get(f"xEb{s - 1}_{b + 2}", 0)))
                    for i in range(4):
                        mm = t.transpose(tpps[b % 2][:, i * 128:(i + 1) * 128],
                                         xb16[s % 2][:, (4 * b + i) * 128:(4 * b + i + 1) * 128],
                                         identb[:])
                    inc(mm, "PE")
                    M[f"xTb{s}_{b}"] = cnt["PE"]
                    if b == 3:
                        M[f"xT{s}"] = cnt["PE"]

                @VE
                def _(v, s=s, b=b):
                    W(v, "PE", M[f"xTb{s}_{b}"])
                    inc(v.tensor_copy(
                        xfm_all[:].rearrange("p (fc t) -> p fc t", fc=16)[:, 4 * b:4 * b + 4,
                                                                         s * 128:(s + 1) * 128],
                        tpps[b % 2][:].rearrange("p (fc t) -> p fc t", fc=4)), "VEC")
                    M[f"xEb{s}_{b}"] = cnt["VEC"]
                    if b == 3:
                        M[f"xE{s}"] = cnt["VEC"]

            @TE
            def _(t, s=s):
                if s == 0:
                    Wd(t, M["wkv"])
                    W(t, "VEC", M["ones"])
                W(t, "VEC", M[f"xE{s}"])
                W(t, "VEC", M.get(f"rpk{s - 1}", 0))
                for fc in range(16):
                    t.matmul(p1ps[0][:], xfm_all[:, fc * 1024 + s * 128: fc * 1024 + (s + 1) * 128],
                             Wk_sb[:, fc * 512:(fc + 1) * 512], start=(fc == 0), stop=False)
                inc(t.matmul(p1ps[0][:], ones_bf[:, 0:128], bk_b[:], start=False, stop=True), "PE")
                M[f"kp{s}"] = cnt["PE"]
                W(t, "VEC", M.get(f"vev{s - 1}", 0))
                for fc in range(16):
                    t.matmul(p1ps[1][:], xfm_all[:, fc * 1024 + s * 128: fc * 1024 + (s + 1) * 128],
                             Wv_sb[:, fc * 512:(fc + 1) * 512], start=(fc == 0), stop=False)
                inc(t.matmul(p1ps[1][:], ones_bf[:, 0:128], bv_b[:], start=False, stop=True), "PE")
                M[f"vp{s}"] = cnt["PE"]

            @VE
            def _(v, s=s):
                W(v, "PE", M[f"kp{s}"])
                W(v, "PE", M.get(f"ktT{s - 1}", 0))
                ke = p1ps[0][:, 0:512:2]
                ko = p1ps[0][:, 1:512:2]
                cs = cos4[:, s * 256:(s + 1) * 256]
                sn = sin4[:, s * 256:(s + 1) * 256]
                v.tensor_tensor(rt1[:], ke, cs, op=ALU.mult)
                inc(v.tensor_tensor(rt2[:], ko, sn, op=ALU.mult), "VEC")
                W(v, "VEC", cnt["VEC"])
                krv = kr[:].rearrange("p (h two i) -> p h two i", h=4, two=2)
                inc(v.tensor_tensor(krv[:, :, 0, :], rt1[:], rt2[:], op=ALU.subtract), "VEC")
                W(v, "VEC", cnt["VEC"])
                v.tensor_tensor(rt1[:], ke, sn, op=ALU.mult)
                inc(v.tensor_tensor(rt2[:], ko, cs, op=ALU.mult), "VEC")
                W(v, "VEC", cnt["VEC"])
                inc(v.tensor_tensor(krv[:, :, 1, :], rt1[:], rt2[:], op=ALU.add), "VEC")
                M[f"rpk{s}"] = cnt["VEC"]
                W(v, "PE", M[f"vp{s}"])
                Wd(v, M.get(f"vst{s - 1}", 0))
                inc(v.tensor_copy(vr[:], p1ps[1][:]), "VEC")
                M[f"vev{s}"] = cnt["VEC"]

            @TE
            def _(t, s=s):
                W(t, "VEC", M[f"rpk{s}"])
                W(t, "VEC", M.get(f"ktE{s - 1}", 0))
                for h in range(4):
                    mm = t.transpose(tpps[0][:, h * 128:(h + 1) * 128],
                                     kr[:, h * 128:(h + 1) * 128], identb[:])
                inc(mm, "PE")
                M[f"ktT{s}"] = cnt["PE"]

            @VE
            def _(v, s=s):
                W(v, "PE", M[f"ktT{s}"])
                Wd(v, M.get(f"ktst{s - 1}", 0))
                inc(v.tensor_copy(ktT[:], tpps[0][:]), "VEC")
                M[f"ktE{s}"] = cnt["VEC"]

            @SY
            def _(q, s=s):
                W(q, "VEC", M[f"ktE{s}"])
                dq_pre(q, "SYNQ")
                M[f"ktst{s}"] = dq_post(q.dma_start(
                    out=k_in.rearrange("(h d) t -> d h t", h=4)[:, :, s * 128:(s + 1) * 128],
                    in_=ktT[:].rearrange("d (h t) -> d h t", h=4)), "SYNQ")
                W(q, "VEC", M[f"vev{s}"])
                dq_pre(q, "SYNQ")
                M[f"vst{s}"] = dq_post(q.dma_start(out=v_in[s * 128:(s + 1) * 128, :], in_=vr[:]), "SYNQ")

        M["kvst"] = snapq("SYNQ")

        @GE
        def _(g):
            Wd(g, M["kvst"])
            inc(g.collective_compute("AllGather", ALU.bypass, replica_groups=RG,
                                     ins=[k_in], outs=[k_out]), "CC")
            inc(g.collective_compute("AllGather", ALU.bypass, replica_groups=RG,
                                     ins=[v_in], outs=[v_out]), "CC")
            M["agkv"] = cnt["CC"]

        for s in range(NS):
            @TE
            def _(t, s=s):
                if s == 0:
                    W(t, "VEC", M["wqeff"])
                    W(t, "VEC", M["bqe"])
                if s <= 1:
                    W(t, "VEC", M[f"vev{NS - 1}"])
                W(t, "VEC", M.get(f"rpq{s - 2}", 0))
                for fc in range(16):
                    t.matmul(p1ps[s % 2][:], xfm_all[:, fc * 1024 + s * 128: fc * 1024 + (s + 1) * 128],
                             Wq_eff[:, fc * 512:(fc + 1) * 512], start=(fc == 0), stop=False)
                inc(t.matmul(p1ps[s % 2][:], ones_bf[:, 0:128], bq_eff[:], start=False, stop=True), "PE")
                M[f"qp{s}"] = cnt["PE"]

            @VE
            def _(v, s=s):
                W(v, "PE", M[f"qp{s}"])
                W(v, "PE", M.get(f"qT{s - 1}", 0))
                qe = p1ps[s % 2][:, 0:512:2]
                qo = p1ps[s % 2][:, 1:512:2]
                cs = cos4[:, s * 256:(s + 1) * 256]
                sn = sin4[:, s * 256:(s + 1) * 256]
                v.tensor_tensor(rt1[:], qe, cs, op=ALU.mult)
                inc(v.tensor_tensor(rt2[:], qo, sn, op=ALU.mult), "VEC")
                W(v, "VEC", cnt["VEC"])
                qrv = qr[:].rearrange("p (h two i) -> p h two i", h=4, two=2)
                inc(v.tensor_tensor(qrv[:, :, 0, :], rt1[:], rt2[:], op=ALU.subtract), "VEC")
                W(v, "VEC", cnt["VEC"])
                v.tensor_tensor(rt1[:], qe, sn, op=ALU.mult)
                inc(v.tensor_tensor(rt2[:], qo, cs, op=ALU.mult), "VEC")
                W(v, "VEC", cnt["VEC"])
                inc(v.tensor_tensor(qrv[:, :, 1, :], rt1[:], rt2[:], op=ALU.add), "VEC")
                M[f"rpq{s}"] = cnt["VEC"]

            @TE
            def _(t, s=s):
                W(t, "VEC", M[f"rpq{s}"])
                if s == 0:
                    W(t, "VEC", M[f"xE{NS - 1}"])
                W(t, "VEC", M.get(f"qE{s - 1}", 0))
                for h in range(4):
                    mm = t.transpose(tpps[1][:, h * 128:(h + 1) * 128],
                                     qr[:, h * 128:(h + 1) * 128], identb[:])
                inc(mm, "PE")
                M[f"qT{s}"] = cnt["PE"]

            @VE
            def _(v, s=s):
                W(v, "PE", M[f"qT{s}"])
                for h in range(4):
                    mm = v.tensor_copy(qTa[:, h * 1024 + s * 128: h * 1024 + (s + 1) * 128],
                                       tpps[1][:, h * 128:(h + 1) * 128])
                inc(mm, "VEC")
                M[f"qE{s}"] = cnt["VEC"]

        M["qT"] = cnt["VEC"]

        _scope_switch('P2attn')
        free("rt2", "rt1", "ktT", "vr", "kr", "qr",
             "xb161", "xb160", "xbuf1", "xbuf0", "Wq_tmp1", "Wq_tmp0",
             "Wq_eff", "Wv_sb", "Wk_sb", "sin4", "cos4", "xfm_all")
        free("tpps1", "tpps0", "p1ps1", "p1ps0")

        mask_sb = salloc("mask_sb", [128, NS * 1024], BF16)

        @SY
        def _(q):
            W(q, "VEC", M["qT"])
            dq_pre(q, "SYNQ")
            M["maskl"] = dq_post(q.dma_start(out=mask_sb[:], in_=masksI), "SYNQ")

        @GE
        def _(g):
            dq_pre(g, "POOLQ")
            dq_post(g.dma_start(out=bo_b[:], in_=bo), "POOLQ")
            dq_pre(g, "POOLQ")
            dq_post(g.dma_start(out=Wo_sb[:].rearrange("p (c n) -> p c n", c=4),
                               in_=Wo.rearrange("(c p) n -> p c n", p=128)), "POOLQ")
            M["wo"] = snapq("POOLQ")

        Kh = [salloc(f"Kh{i}", [128, NC * 1024], BF16) for i in range(2)]
        Vh = [salloc(f"Vh{i}", [128, 64 * 128], BF16) for i in range(2)]
        pT = [salloc(f"pT{i}", [128, 1024], BF16) for i in range(2)]
        dnm = salloc("dnm", [1, 1024], F32)
        rb_sb = salloc("rb_sb", [128, 1024], F32)

        sT_ps = [palloc(f"sT_ps{i}", [128, 1024]) for i in range(2)]
        yA_ps = palloc("yA_ps", [128, 512])
        yB_ps = palloc("yB_ps", [128, 512])
        dn_ps = palloc("dn_ps", [128, 512])

        exp_m, mask_m, sco_m, av_m = {}, {}, {}, {}

        def kv_load(h):
            @SE
            def _(sc, h=h):
                if h == 0:
                    W(sc, "CC", M["agkv"])
                if h <= 1:
                    W(sc, "VEC", M["qT"])
                    W(sc, "PE", M[f"qT{NS - 1}"])
                W(sc, "VEC", M.get(f"ynm{h - 2}", 0))
                dq_pre(sc, "SCLQ")
                dq_post(sc.dma_start(
                    out=Kh[h % 2][:].rearrange("d (r t) -> d r t", r=NC),
                    in_=k_out.rearrange("(r h d) t -> h d r t", r=NC, h=4)[h]), "SCLQ")
                for r in range(NC):
                    dq_pre(sc, "SCLQ")
                    dq_post(sc.dma_start(
                        out=Vh[h % 2][:].rearrange("p (r t d) -> p r t d", r=NC, t=8)[:, r],
                        in_=v_out[r * 1024:(r + 1) * 1024, h * 128:(h + 1) * 128]
                        .rearrange("(t p) d -> p t d", p=128)), "SCLQ")
                M[f"kvh{h}"] = snapq("SCLQ")

        kv_load(0)
        kv_load(1)

        def attn_scores(t, h, j):
            W(t, "ACT", exp_m.get((h, j - 2), 0))
            rj, sj = kchunk_owner(j)
            qs = 128 * (j // 8)
            lhs = Kh[h % 2][:, rj * 1024 + sj * 128: rj * 1024 + (sj + 1) * 128]
            if qs < 512:
                t.matmul(sT_ps[j % 2][:, qs:512], lhs,
                         qTa[:, h * 1024 + qs: h * 1024 + 512], start=True, stop=True)
                mm = t.matmul(sT_ps[j % 2][:, 512:1024], lhs,
                              qTa[:, h * 1024 + 512: h * 1024 + 1024], start=True, stop=True)
            else:
                mm = t.matmul(sT_ps[j % 2][:, qs:1024], lhs,
                              qTa[:, h * 1024 + qs: h * 1024 + 1024], start=True, stop=True)
            sco_m[(h, j)] = inc(mm, "PE")

        def attn_av(t, h, j):
            W(t, "VEC", mask_m[(h, j)])
            rj, sj = kchunk_owner(j)
            qs = 128 * (j // 8)
            vlhs = Vh[h % 2][:, (rj * 8 + sj) * 128:(rj * 8 + sj + 1) * 128]
            if qs < 512:
                t.matmul(yA_ps[:, qs:512], vlhs, pT[j % 2][:, qs:512],
                         start=(j == 0), stop=(j == 31))
                t.matmul(yB_ps[:], vlhs, pT[j % 2][:, 512:1024],
                         start=(j == 0), stop=False)
                t.matmul(dn_ps[0:1, qs:512], ones_c[:], pT[j % 2][:, qs:512],
                         start=(j == 0), stop=(j == 31))
                mm = t.matmul(dn_ps[32:33, :], ones_c[:], pT[j % 2][:, 512:1024],
                              start=(j == 0), stop=False)
            else:
                t.matmul(yB_ps[:, qs - 512:512], vlhs, pT[j % 2][:, qs:1024],
                         start=False, stop=(j == 63))
                mm = t.matmul(dn_ps[32:33, qs - 512:512], ones_c[:],
                              pT[j % 2][:, qs:1024], start=False, stop=(j == 63))
            av_m[(h, j)] = inc(mm, "PE")

        for h in range(4):
            @TE
            def _(t, h=h):
                if h == 0:
                    W(t, "VEC", M["qT"])
                Wd(t, M[f"kvh{h}"])
                if h > 0:
                    W(t, "VEC", M[f"ynm{h - 1}"])
                attn_scores(t, h, 0)

            for j in range(64):
                @SE
                def _(sc, h=h, j=j):
                    W(sc, "PE", sco_m[(h, j)])
                    W(sc, "PE", av_m.get((h, j - 2), 0))
                    qs = 128 * (j // 8)
                    exp_m[(h, j)] = inc(sc.activation(
                        pT[j % 2][:, qs:1024], sT_ps[j % 2][:, qs:1024], AF.Exp,
                        bias=f32(-SHIFT), scale=f32(INVSQD)), "ACT")

                @VE
                def _(v, h=h, j=j):
                    if (h, j) == (0, 0):
                        Wd(v, M["maskl"])
                    W(v, "ACT", exp_m[(h, j)])
                    s0 = j // 8
                    qs = 128 * s0
                    mask_m[(h, j)] = inc(v.tensor_tensor(
                        pT[j % 2][:, qs:qs + 128], pT[j % 2][:, qs:qs + 128],
                        mask_sb[:, s0 * 1024 + (j % 8) * 128: s0 * 1024 + (j % 8 + 1) * 128],
                        op=ALU.mult), "VEC")

                if "pdbg" in taps and (h, j) == (0, 2):
                    @VE
                    def _(v):
                        W(v, "VEC", cnt["VEC"])
                        inc(v.tensor_copy(rb_sb[:], pT[0][:]), "VEC")
                        M["pdbg"] = cnt["VEC"]
                        mask_m[(0, 2)] = cnt["VEC"]

                    @GE
                    def _(g):
                        W(g, "VEC", M["pdbg"])
                        dq_pre(g, "POOLQ")
                        M["pdbgd"] = dq_post(g.dma_start(out=tap_out["pdbg"], in_=rb_sb[:]), "POOLQ")

                    @TE
                    def _(t):
                        Wd(t, M["pdbgd"])
                        t.nop()

                @TE
                def _(t, h=h, j=j):
                    if j < 63:
                        attn_scores(t, h, j + 1)
                    attn_av(t, h, j)

            @VE
            def _(v, h=h):
                W(v, "PE", av_m[(h, 63)])
                v.reciprocal(dnm[:, 0:512], dn_ps[0:1, :])
                inc(v.reciprocal(dnm[:, 512:1024], dn_ps[32:33, :]), "VEC")
                M[f"rcp{h}"] = cnt["VEC"]

            @TE
            def _(t, h=h):
                W(t, "VEC", M[f"rcp{h}"])
                W(t, "ACT", exp_m[(h, 63)])
                t.matmul(sT_ps[0][:, 0:512], ones_f[:], dnm[:, 0:512], start=True, stop=True)
                inc(t.matmul(sT_ps[0][:, 512:1024], ones_f[:], dnm[:, 512:1024],
                             start=True, stop=True), "PE")
                M[f"rbm{h}"] = cnt["PE"]

            @VE
            def _(v, h=h):
                W(v, "PE", M[f"rbm{h}"])
                inc(v.tensor_copy(rb_sb[:], sT_ps[0][:]), "VEC")
                W(v, "VEC", cnt["VEC"])
                v.tensor_tensor(ynT[:, h * 1024: h * 1024 + 512], yA_ps[:],
                                rb_sb[:, 0:512], op=ALU.mult)
                inc(v.tensor_tensor(ynT[:, h * 1024 + 512: h * 1024 + 1024], yB_ps[:],
                                    rb_sb[:, 512:1024], op=ALU.mult), "VEC")
                M[f"ynm{h}"] = cnt["VEC"]

            if h + 2 < 4:
                kv_load(h + 2)

        M["attn"] = cnt["VEC"]

        _scope_switch('P3')
        free("rb_sb", "dnm", "pT1", "pT0", "Vh1", "Vh0", "Kh1", "Kh0", "mask_sb")
        free("dn_ps", "yB_ps", "yA_ps", "sT_ps1", "sT_ps0")

        wr_bc = salloc("wr_bc", [128, C], F32)
        br_bc = salloc("br_bc", [128, 1], F32)
        sjunk = salloc("sjunk", [128, C], BF16)
        sc_row = salloc("sc_row", [NS, 128], F32)
        oPs = palloc("oPs", [128, 2048])
        tp3f = palloc("tp3f", [128, 512])

        @GE
        def _(g):
            W(g, "VEC", M["attn"])
            dq_pre(g, "POOLQ")
            dq_post(g.dma_start(out=wr_bc[0:1, :], in_=wr), "POOLQ")
            dq_pre(g, "POOLQ")
            dq_post(g.dma_start(out=br_bc[0:1, 0:1], in_=br), "POOLQ")
            M["wrl"] = snapq("POOLQ")

        for i in range(4):
            @TE
            def _(t, i=i):
                if i == 0:
                    Wd(t, M["wrl"])
                    W(t, "VEC", M["attn"])
                else:
                    W(t, "VEC", M[f"wrbe{i - 1}"])
                inc(t.matmul(tp3f[:], ones_f[:], wr_bc[0:1, i * 512:(i + 1) * 512],
                             start=True, stop=True), "PE")
                M[f"wrb{i}"] = cnt["PE"]

            @VE
            def _(v, i=i):
                W(v, "PE", M[f"wrb{i}"])
                inc(v.tensor_copy(wr_bc[:, i * 512:(i + 1) * 512], tp3f[:]), "VEC")
                M[f"wrbe{i}"] = cnt["VEC"]

        @TE
        def _(t):
            W(t, "VEC", M["wrbe3"])
            inc(t.matmul(tp3f[:, 0:1], ones_f[:], br_bc[0:1, 0:1], start=True, stop=True), "PE")
            M["brb"] = cnt["PE"]

        @VE
        def _(v):
            W(v, "PE", M["brb"])
            inc(v.tensor_copy(br_bc[:], tp3f[:, 0:1]), "VEC")
            M["brbe"] = cnt["VEC"]

        for s in range(NS):
            @SY
            def _(q, s=s):
                W(q, "VEC", M.get(f"oev{s - 1}", 0))
                dq_pre(q, "SYNQ")
                M[f"xr{s}"] = dq_post(q.dma_start(out=v1t[:], in_=x_loc[s * 128:(s + 1) * 128, :]), "SYNQ")

            @TE
            def _(t, s=s):
                if s == 0:
                    Wd(t, M["wo"])
                    W(t, "VEC", M["brbe"])
                W(t, "VEC", M.get(f"oev{s - 1}", 0))
                for ncol in range(4):
                    for h in range(4):
                        t.matmul(oPs[:, ncol * 512:(ncol + 1) * 512],
                                 ynT[:, h * 1024 + s * 128: h * 1024 + (s + 1) * 128],
                                 Wo_sb[:, h * 2048 + ncol * 512: h * 2048 + (ncol + 1) * 512],
                                 start=(h == 0), stop=False)
                    mm = t.matmul(oPs[:, ncol * 512:(ncol + 1) * 512], ones_bf[:, 0:128],
                                  bo_b[:, ncol * 512:(ncol + 1) * 512], start=False, stop=True)
                inc(mm, "PE")
                M[f"op{s}"] = cnt["PE"]

            @VE
            def _(v, s=s):
                W(v, "PE", M[f"op{s}"])
                Wd(v, M[f"xr{s}"])
                Wd(v, M.get(f"x1s{s - 1}", 0))
                for i in range(4):
                    mm = v.tensor_tensor(v1t[:, i * 512:(i + 1) * 512],
                                         v1t[:, i * 512:(i + 1) * 512],
                                         oPs[:, i * 512:(i + 1) * 512], op=ALU.add)
                inc(mm, "VEC")
                M[f"oev{s}"] = cnt["VEC"]
                W(v, "VEC", cnt["VEC"])
                for i in range(4):
                    mm = v.bn_stats(bnst[:, i * 6:(i + 1) * 6], v1t[:, i * 512:(i + 1) * 512])
                inc(mm, "VEC")
                W(v, "VEC", cnt["VEC"])
                inc(v.bn_aggr(mv[:], bnst[:]), "VEC")
                M[f"st1_{s}"] = cnt["VEC"]

            @SE
            def _(sc, s=s):
                W(sc, "VEC", M[f"st1_{s}"])
                inc(sc.activation(lnc1[:], mv[:, 1:2], AF.Ln, bias=f32(EPS), scale=f32(1.0)), "ACT")
                W(sc, "ACT", cnt["ACT"])
                inc(sc.activation(lnc1[:], lnc1[:], AF.Exp, bias=f32(0.0), scale=f32(-0.5)), "ACT")
                M[f"rs1_{s}"] = cnt["ACT"]

            @VE
            def _(v, s=s):
                W(v, "ACT", M[f"rs1_{s}"])
                inc(v.tensor_tensor(lnc2[:], mv[:, 0:1], lnc1[:], op=ALU.mult), "VEC")
                W(v, "VEC", cnt["VEC"])
                inc(v.tensor_scalar(lnc2[:], lnc2[:], -1.0, None, op0=ALU.mult), "VEC")
                W(v, "VEC", cnt["VEC"])
                inc(v.tensor_scalar(x1t[:], v1t[:], lnc1[:], lnc2[:],
                                    op0=ALU.mult, op1=ALU.add), "VEC")
                M[f"x1_{s}"] = cnt["VEC"]
                W(v, "VEC", cnt["VEC"])
                inc(v.scalar_tensor_tensor(sjunk[:], x1t[:], 1.0, wr_bc[:],
                                           op0=ALU.mult, op1=ALU.mult,
                                           accum_out=lnc2[:]), "VEC")
                W(v, "VEC", cnt["VEC"])
                inc(v.tensor_tensor(s_loc[:, s:s + 1], lnc2[:], br_bc[:], op=ALU.add), "VEC")
                M[f"sl{s}"] = cnt["VEC"]

            @SY
            def _(q, s=s):
                W(q, "VEC", M[f"x1_{s}"])
                dq_pre(q, "SYNQ")
                M[f"x1s{s}"] = dq_post(q.dma_start(out=x1_dram[s * 128:(s + 1) * 128, :], in_=x1t[:]), "SYNQ")

        M["x1d"] = snapq("SYNQ")

        _scope_switch('P4bisect')
        sc_flat = salloc("sc_flat", [1, 8192], F32)
        srep = salloc("srep", [64, 8192], F32)
        bjunk = salloc("bjunk", [64, 8192], BF16)
        rowcnt = salloc("rowcnt", [64, 1], F32)
        lo_ = salloc("lo_", [1, 1], F32)
        d64_ = salloc("d64_", [1, 1], F32)
        pv_i = salloc("pv_i", [64, 1], U32)
        pvec1 = salloc("pvec1", [64, 1], F32)
        tcol1 = salloc("tcol1", [64, 1], F32)
        thr_col = salloc("thr_col", [64, 1], F32)
        ps4b = palloc("ps4b", [64, 2])
        th128 = palloc("th128", [128, 1])
        cnt_ps = palloc("cnt_ps", [1, 1])

        @TE
        def _(t):
            W(t, "VEC", M[f"sl{NS - 1}"])
            inc(t.transpose(tp3f[0:8, 0:128], s_loc[:, 0:8], ident32[:]), "PE")
            M["slT"] = cnt["PE"]

        @VE
        def _(v):
            W(v, "PE", M["slT"])
            inc(v.tensor_copy(sc_row[:], tp3f[0:8, 0:128]), "VEC")
            M["srow"] = cnt["VEC"]

        @SY
        def _(q):
            W(q, "VEC", M["srow"])
            dq_pre(q, "SYNQ")
            M["scs"] = dq_post(q.dma_start(out=sc_in, in_=sc_row[:]), "SYNQ")

        @GE
        def _(g):
            Wd(g, M["scs"])
            inc(g.collective_compute("AllGather", ALU.bypass, replica_groups=RG,
                                     ins=[sc_in], outs=[sc_out]), "CC")
            M["agsc"] = cnt["CC"]


        @SY
        def _(q):
            W(q, "CC", M["agsc"])
            dq_pre(q, "SYNQ")
            M["sclf"] = dq_post(q.dma_start(out=sc_flat[:], in_=sc_out.rearrange("a b -> (a b)").rearrange("(o n) -> o n", o=1)), "SYNQ")

        @GE
        def _(g):
            inc(g.iota(pv_i[:], pattern=[[0, 1]], base=1, channel_multiplier=1), "POOL")
            M["pvi"] = cnt["POOL"]

        @VE
        def _(v):
            W(v, "POOL", M["pvi"])
            inc(v.tensor_copy(pvec1[:], pv_i[:]), "VEC")
            v.memset(lo_[:], -128.0)
            inc(v.memset(d64_[:], 4.0), "VEC")
            M["bisinit"] = cnt["VEC"]

        for i in range(16):
            @TE
            def _(t, i=i):
                if i == 0:
                    Wd(t, M["sclf"])
                    W(t, "VEC", M["srow"])
                else:
                    W(t, "VEC", M[f"srepc{i - 1}"])
                inc(t.matmul(tp3f[0:64, :], ones_f[:, 0:64],
                             sc_flat[:, i * 512:(i + 1) * 512], start=True, stop=True), "PE")
                M[f"srep{i}"] = cnt["PE"]

            @VE
            def _(v, i=i):
                W(v, "PE", M[f"srep{i}"])
                inc(v.tensor_copy(srep[:, i * 512:(i + 1) * 512], tp3f[0:64, :]), "VEC")
                M[f"srepc{i}"] = cnt["VEC"]
                if i == 15:
                    M["srep"] = cnt["VEC"]

        for it in range(6):
            @TE
            def _(t, it=it):
                W(t, "VEC", M["bisinit"] if it == 0 else M[f"b64u{it - 1}"])
                t.matmul(ps4b[:, 0:1], ones_f[:, 0:64], d64_[:], start=True, stop=True)
                inc(t.matmul(ps4b[:, 1:2], ones_f[:, 0:64], lo_[:], start=True, stop=True), "PE")
                M[f"b64b{it}"] = cnt["PE"]

            @VE
            def _(v, it=it):
                W(v, "PE", M[f"b64b{it}"])
                if it == 0:
                    W(v, "VEC", M["srep"])
                inc(v.tensor_scalar(tcol1[:], pvec1[:], ps4b[:, 0:1], None, op0=ALU.mult), "VEC")
                W(v, "VEC", cnt["VEC"])
                inc(v.tensor_scalar(thr_col[:], tcol1[:], ps4b[:, 1:2], None, op0=ALU.add), "VEC")
                W(v, "VEC", cnt["VEC"])
                inc(v.tensor_scalar(bjunk[:], srep[:], thr_col[:], None,
                                    op0=ALU.is_ge, op1=ALU.add, accum_out=rowcnt[:]), "VEC")
                W(v, "VEC", cnt["VEC"])
                inc(v.tensor_scalar(rowcnt[:], rowcnt[:], 4096.0, None, op0=ALU.is_ge), "VEC")
                M[f"b64c{it}"] = cnt["VEC"]

            @TE
            def _(t, it=it):
                W(t, "VEC", M[f"b64c{it}"])
                inc(t.matmul(cnt_ps[:], rowcnt[:], ones64[:], start=True, stop=True), "PE")
                M[f"b64s{it}"] = cnt["PE"]

            @VE
            def _(v, it=it):
                W(v, "PE", M[f"b64s{it}"])
                inc(v.scalar_tensor_tensor(lo_[:], cnt_ps[:], d64_[:], lo_[:],
                                           op0=ALU.mult, op1=ALU.add), "VEC")
                W(v, "VEC", cnt["VEC"])
                inc(v.tensor_scalar(d64_[:], d64_[:], 1.0 / 64.0, None, op0=ALU.mult), "VEC")
                M[f"b64u{it}"] = cnt["VEC"]
        M[f"bi{BISECT - 1}"] = M["b64u5"]

        _scope_switch('P4idx')
        pos_i = salloc("pos_i", [128, NS], F32)
        posu = salloc("posu", [128, NS], U32)
        rc128 = salloc("rc128", [128, 1], F32)
        prow = salloc("prow", [1, 128], F32)
        poff = salloc("poff", [128, 1], F32)
        zero1 = salloc("zero1", [1, 1], F32)
        trow = salloc("trow", [128, 2 * NS], F32)
        tcol = salloc("tcol", [128, 2 * NS], U32)
        bigt = salloc("bigt", [128, NS], F32)
        gf32b = salloc("gf32b", [128, 5 * NS], F32)

        @TE
        def _(t):
            W(t, "VEC", M[f"bi{BISECT - 1}"])
            inc(t.matmul(th128[:], ones_f[:], lo_[:], start=True, stop=True), "PE")
            M["th128"] = cnt["PE"]

        @VE
        def _(v):
            W(v, "PE", M["th128"])
            inc(v.tensor_scalar(m_loc[:], s_loc[:], th128[:], None, op0=ALU.is_ge), "VEC")
            W(v, "VEC", cnt["VEC"])
            inc(v.tensor_reduce(rc128[:], m_loc[:], axis=mybir.AxisListType.X, op=ALU.add), "VEC")
            M["rc"] = cnt["VEC"]

        @TE
        def _(t):
            W(t, "VEC", M["rc"])
            inc(t.transpose(tp3f[0:1, 0:128], rc128[:], ident32[:]), "PE")
            M["rcT"] = cnt["PE"]

        @VE
        def _(v):
            W(v, "PE", M["rcT"])
            v.memset(zero1[:], 0.0)
            inc(v.tensor_copy(prow[:], tp3f[0:1, 0:128]), "VEC")
            W(v, "VEC", cnt["VEC"])
            inc(v.tensor_tensor_scan(prow[:], prow[:], prow[:], zero1[:],
                                     op0=ALU.add, op1=ALU.bypass), "VEC")
            M["scan"] = cnt["VEC"]

        @TE
        def _(t):
            W(t, "VEC", M["scan"])
            inc(t.matmul(tp3f[:, 0:1], prow[:], ones_f[:, 0:1], start=True, stop=True), "PE")
            M["scanT"] = cnt["PE"]

        @VE
        def _(v):
            W(v, "PE", M["scanT"])
            inc(v.tensor_copy(poff[:], tp3f[:, 0:1]), "VEC")
            W(v, "VEC", cnt["VEC"])
            inc(v.tensor_tensor(poff[:], poff[:], rc128[:], op=ALU.subtract), "VEC")
            W(v, "VEC", cnt["VEC"])
            inc(v.tensor_tensor_scan(pos_i[:], m_loc[:], m_loc[:], zerop[:],
                                     op0=ALU.add, op1=ALU.bypass), "VEC")
            W(v, "VEC", cnt["VEC"])
            inc(v.scalar_tensor_tensor(pos_i[:], pos_i[:], poff[:], m_loc[:],
                                       op0=ALU.add, op1=ALU.subtract), "VEC")
            W(v, "VEC", cnt["VEC"])
            inc(v.tensor_scalar(bigt[:, 0:NS], m_loc[:], -1.0e9, 1.0e9,
                                op0=ALU.mult, op1=ALU.add), "VEC")
            W(v, "VEC", cnt["VEC"])
            inc(v.tensor_tensor(pos_i[:], pos_i[:], bigt[:, 0:NS], op=ALU.add), "VEC")
            W(v, "VEC", cnt["VEC"])
            inc(v.tensor_copy(posu[:], pos_i[:]), "VEC")
            M["posu"] = cnt["VEC"]

        @GE
        def _(g):
            inc(g.iota(tcol[:], pattern=[[128, 2 * NS]], base=0, channel_multiplier=1), "POOL")
            M["iot8"] = cnt["POOL"]

        @VE
        def _(v):
            W(v, "POOL", M["iot8"])
            inc(v.tensor_copy(trow[:], tcol[:]), "VEC")
            inc(v.memset(bigt[:, 0:NS], 2.0e9), "VEC")
            M["trow8"] = cnt["VEC"]

        @GE
        def _(g):
            W(g, "VEC", M["trow8"])
            for s in range(NS):
                dq_pre(g, "POOLQ")
                dq_post(g.dma_start(out=gbuf[s * 128:(s + 1) * 128, 0:NS], in_=bigt[:, 0:NS]), "POOLQ")
            M["gfill"] = snapq("POOLQ")

        for s in range(NS):
            @GE
            def _(g, s=s):
                if s == 0:
                    Wd(g, M["gfill"])
                    W(g, "VEC", M["posu"])
                dq_pre(g, "POOLQ")
                M[f"gsc{s}"] = dq_post(g.indirect_dma_start(
                    out=gbuf[:, 0:NS], out_offset=IndirectOffsetOnAxis(ap=posu[:, s:s + 1], axis=0),
                    in_=trow[:, s:s + NS], in_offset=None,
                    bounds_check=LT - 1, oob_is_err=False), "POOLQ")

        M["gscall"] = snapq("POOLQ")

        @SY
        def _(q):
            Wd(q, M["gscall"])
            dq_pre(q, "SYNQ")
            M["gld"] = dq_post(q.dma_start(
                out=gf32b[:].rearrange("p (i f) -> p i f", f=NS),
                in_=gbuf[0:CAP, :].rearrange("(i p) f -> p i f", p=128)), "SYNQ")

        @VE
        def _(v):
            Wd(v, M["gld"])
            inc(v.tensor_copy(gf32[:], gf32b[:].rearrange("p (i f) -> p i f", f=NS)[:, :, 0]), "VEC")
            W(v, "VEC", cnt["VEC"])
            inc(v.tensor_copy(gidx[:], gf32[:]), "VEC")
            M["gidx"] = cnt["VEC"]

        if taps:
            @GE
            def _(g):
                if "x1" in taps:
                    Wd(g, M["x1d"])
                    dq_pre(g, "POOLQ")
                    dq_post(g.dma_start(out=tap_out["x1"], in_=x1_dram), "POOLQ")
                if "thr" in taps:
                    W(g, "VEC", M[f"bi{BISECT - 1}"])
                    dq_pre(g, "POOLQ")
                    dq_post(g.dma_start(out=tap_out["thr"], in_=lo_[:]), "POOLQ")
                if "mask" in taps:
                    W(g, "VEC", M["posu"])
                    dq_pre(g, "POOLQ")
                    dq_post(g.dma_start(out=tap_out["mask"], in_=m_loc[:]), "POOLQ")
                if "gidx" in taps:
                    W(g, "VEC", M["gidx"])
                    dq_pre(g, "POOLQ")
                    dq_post(g.dma_start(out=tap_out["gidx"], in_=gf32[:]), "POOLQ")
                if "ynT" in taps:
                    W(g, "VEC", M["attn"])
                    dq_pre(g, "POOLQ")
                    dq_post(g.dma_start(out=tap_out["ynT"], in_=ynT[:]), "POOLQ")
                M["tapd"] = snapq("POOLQ")

        _scope_switch('P5gather')
        free("gf32b", "bigt", "tcol", "trow", "zero1", "poff", "prow", "rc128", "posu",
             "pos_i", "thr_col", "tcol1", "pvec1", "pv_i", "d64_", "lo_",
             "rowcnt", "bjunk", "srep", "sc_flat", "sc_row", "sjunk", "br_bc", "wr_bc")
        free("cnt_ps", "th128", "ps4b", "tp3f", "oPs")
        free("bo_b", "Wo_sb", "ynT", "qTa")

        xselb = [salloc(f"xselb{i}", [128, C], BF16) for i in range(5)]
        b2_b = salloc("b2_b", [1, C], BF16)
        hT = salloc("hT", [128, 64 * CAP], BF16)
        b1_b = salloc("b1_b", [1, DF], BF16)
        xsT = salloc("xsT", [128, 16 * CAP], BF16)
        w1b = [salloc(f"w1b{i}", [128, 16 * 256], BF16) for i in range(3)]
        tp5 = [palloc(f"tp5{i}", [128, 512], BF16) for i in range(2)]
        psU = [palloc(f"psU{i}", [128, CAP]) for i in range(2)]

        @GE
        def _(g):
            W(g, "VEC", M["gidx"])
            Wd(g, snapq("POOLQ"))
            dq_pre(g, "POOLQ")
            dq_post(g.dma_start(out=b1_b[:], in_=b1), "POOLQ")
            dq_pre(g, "POOLQ")
            dq_post(g.dma_start(out=b2_b[:], in_=b2), "POOLQ")
            for i in range(3):
                dq_pre(g, "POOLQ")
                M[f"w1l{i}"] = dq_post(g.dma_start(
                    out=w1b[i][:].rearrange("p (c n) -> p c n", c=16),
                    in_=w1[:, i * 256:(i + 1) * 256].rearrange("(c p) n -> p c n", p=128)), "POOLQ")

        gat = {}
        for i in range(5):
            @VE
            def _(v, i=i):
                if i == 0:
                    W(v, "VEC", M["gidx"])
                    Wd(v, M.get("tapd", 0))
                gat[("z", i)] = inc(v.memset(xselb[i][:], 0.0), "VEC")

            @GE
            def _(g, i=i):
                if i == 0:
                    Wd(g, M["x1d"])
                W(g, "VEC", gat[("z", i)])
                dq_pre(g, "POOLQ")
                gat[("g", i)] = dq_post(g.indirect_dma_start(
                    out=xselb[i][:], out_offset=None,
                    in_=x1_dram, in_offset=IndirectOffsetOnAxis(ap=gidx[:, i:i + 1], axis=0),
                    bounds_check=LT - 1, oob_is_err=False), "POOLQ")

        for i in range(5):
            for b in range(4):
                @TE
                def _(t, i=i, b=b):
                    Wd(t, gat[("g", i)])
                    W(t, "VEC", gat.get(("e", i, b - 2), gat.get(("e", i - 1, b + 2), 0)))
                    for k in range(4):
                        mm = t.transpose(tp5[b % 2][:, k * 128:(k + 1) * 128],
                                         xselb[i][:, (4 * b + k) * 128:(4 * b + k + 1) * 128],
                                         identb[:])
                    inc(mm, "PE")
                    gat[("t", i, b)] = cnt["PE"]

                @VE
                def _(v, i=i, b=b):
                    W(v, "PE", gat[("t", i, b)])
                    mm = v.tensor_copy(
                        xsT[:].rearrange("p (fc t) -> p fc t", fc=16)[:, 4 * b:4 * b + 4,
                                                                     i * 128:(i + 1) * 128],
                        tp5[b % 2][:].rearrange("p (fc t) -> p fc t", fc=4))
                    gat[("e", i, b)] = inc(mm, "VEC")
        M["xsT"] = cnt["VEC"]

        _scope_switch('lnout')
        ln = {}
        for s in range(NS):
            @SY
            def _(q, s=s):
                if s == 0:
                    Wd(q, M["x1d"])
                W(q, "VEC", ln.get(("ub", s - 1), 0))
                Wd(q, ln.get(("st", s - 1), 0))
                dq_pre(q, "SYNQ")
                ln[("xr", s)] = dq_post(q.dma_start(out=v1t[:], in_=x1_dram[s * 128:(s + 1) * 128, :]), "SYNQ")

            @VE
            def _(v, s=s):
                Wd(v, ln[("xr", s)])
                for i in range(4):
                    mm = v.bn_stats(bnst[:, i * 6:(i + 1) * 6], v1t[:, i * 512:(i + 1) * 512])
                inc(mm, "VEC")
                W(v, "VEC", cnt["VEC"])
                inc(v.bn_aggr(mv[:], bnst[:]), "VEC")
                ln[("stat", s)] = cnt["VEC"]
                ln[("ub", s)] = cnt["VEC"]

            @SE
            def _(sc, s=s):
                W(sc, "VEC", ln[("stat", s)])
                inc(sc.activation(lnc1[:], mv[:, 1:2], AF.Ln, bias=f32(EPS), scale=f32(1.0)), "ACT")
                W(sc, "ACT", cnt["ACT"])
                inc(sc.activation(lnc1[:], lnc1[:], AF.Exp, bias=f32(0.0), scale=f32(-0.5)), "ACT")
                ln[("rs", s)] = cnt["ACT"]

            @VE
            def _(v, s=s):
                W(v, "ACT", ln[("rs", s)])
                inc(v.tensor_tensor(lnc2[:], mv[:, 0:1], lnc1[:], op=ALU.mult), "VEC")
                W(v, "VEC", cnt["VEC"])
                inc(v.tensor_scalar(lnc2[:], lnc2[:], -1.0, None, op0=ALU.mult), "VEC")
                W(v, "VEC", cnt["VEC"])
                inc(v.tensor_scalar(v1t[:], v1t[:], lnc1[:], lnc2[:],
                                    op0=ALU.mult, op1=ALU.add), "VEC")
                ln[("y", s)] = cnt["VEC"]

            @SY
            def _(q, s=s):
                W(q, "VEC", ln[("y", s)])
                dq_pre(q, "SYNQ")
                ln[("st", s)] = dq_post(q.dma_start(out=out[s * 128:(s + 1) * 128, :], in_=v1t[:]), "SYNQ")

        M["base_out"] = snapq("SYNQ")


        _scope_switch('mm1')
        mm1 = {}
        for ds in range(32):
            @TE
            def _(t, ds=ds):
                if ds == 0:
                    W(t, "VEC", M["xsT"])
                Wd(t, M[f"w1l{ds}"] if ds < 3 else mm1[("w1l", ds)])
                for dfc in range(2):
                    g = 2 * ds + dfc
                    W(t, "ACT", mm1.get(("gel", g - 2), 0))
                    for fc in range(16):
                        lhs = w1b[ds % 3][:, fc * 256 + dfc * 128: fc * 256 + (dfc + 1) * 128]
                        t.matmul(psU[g % 2][:, 0:512], lhs, xsT[:, fc * CAP: fc * CAP + 512],
                                 start=(fc == 0), stop=False)
                        t.matmul(psU[g % 2][:, 512:CAP], lhs,
                                 xsT[:, fc * CAP + 512: fc * CAP + CAP],
                                 start=(fc == 0), stop=False)
                    boff = ds * 256 + dfc * 128
                    t.matmul(psU[g % 2][:, 0:512], b1_b[:, boff:boff + 128],
                             ones_bf[:, 0:512], start=False, stop=True)
                    mm = t.matmul(psU[g % 2][:, 512:CAP], b1_b[:, boff:boff + 128],
                                  ones_bf[:, 0:128], start=False, stop=True)
                    mm1[("u", g)] = inc(mm, "PE")

            @SE
            def _(sc, ds=ds):
                for dfc in range(2):
                    g = 2 * ds + dfc
                    W(sc, "PE", mm1[("u", g)])
                    mm1[("gel", g)] = inc(sc.activation(
                        hT[:, g * CAP:(g + 1) * CAP], psU[g % 2][:], AF.Gelu_apprx_tanh), "ACT")

            if ds + 3 < 32:
                @GE
                def _(g, ds=ds):
                    W(g, "PE", mm1[("u", 2 * ds + 1)])
                    dq_pre(g, "POOLQ")
                    mm = g.dma_start(
                        out=w1b[ds % 3][:].rearrange("p (c n) -> p c n", c=16),
                        in_=w1[:, (ds + 3) * 256:(ds + 4) * 256].rearrange("(c p) n -> p c n", p=128))
                    mm1[("w1l", ds + 3)] = dq_post(mm, "POOLQ")

        M["gelu"] = cnt["ACT"]

        _scope_switch('mm2')
        free("w1b2", "w1b1", "w1b0", "xsT", "b1_b")
        free("psU1", "psU0", "tp51", "tp50")
        w2cb = [salloc(f"w2cb{i}", [128, 64 * 128], BF16) for i in range(2)]
        psO2 = [palloc(f"psO2{i}", [128, 128]) for i in range(4)]

        mm2 = {}
        for nb in range(2):
            @GE
            def _(g, nb=nb):
                W(g, "ACT", M["gelu"] if nb == 0 else 0)
                dq_pre(g, "POOLQ")
                mm = g.dma_start(
                    out=w2cb[nb % 2][:].rearrange("p (c n) -> p c n", c=64),
                    in_=w2[:, nb * 128:(nb + 1) * 128].rearrange("(c p) n -> p c n", p=128))
                mm2[("w2l", nb)] = dq_post(mm, "POOLQ")

        for nb in range(16):
            for sl in range(5):
                @TE
                def _(t, nb=nb, sl=sl):
                    g = nb * 5 + sl
                    if sl == 0:
                        Wd(t, mm2[("w2l", nb)])
                    if g >= 4:
                        W(t, "VEC", mm2[("ev", g - 4)])
                    for dfc in range(64):
                        t.matmul(psO2[g % 4][:],
                                 hT[:, dfc * CAP + sl * 128: dfc * CAP + (sl + 1) * 128],
                                 w2cb[nb % 2][:, dfc * 128:(dfc + 1) * 128],
                                 start=(dfc == 0), stop=False)
                    mm = t.matmul(psO2[g % 4][:], ones_bf[0:1, 0:128],
                                  b2_b[:, nb * 128:(nb + 1) * 128], start=False, stop=True)
                    mm2[("mm", g)] = inc(mm, "PE")
                    if sl == 4:
                        mm2[("mmn", nb)] = cnt["PE"]

                @VE
                def _(v, nb=nb, sl=sl):
                    g = nb * 5 + sl
                    W(v, "PE", mm2[("mm", g)])
                    mm2[("ev", g)] = inc(v.tensor_tensor(
                        xselb[sl][:, nb * 128:(nb + 1) * 128], psO2[g % 4][:],
                        xselb[sl][:, nb * 128:(nb + 1) * 128], op=ALU.add), "VEC")

            if nb + 2 < 16:
                @GE
                def _(g, nb=nb):
                    W(g, "PE", mm2[("mmn", nb)])
                    dq_pre(g, "POOLQ")
                    mm = g.dma_start(
                        out=w2cb[nb % 2][:].rearrange("p (c n) -> p c n", c=64),
                        in_=w2[:, (nb + 2) * 128:(nb + 3) * 128].rearrange("(c p) n -> p c n", p=128))
                    mm2[("w2l", nb + 2)] = dq_post(mm, "POOLQ")

        M["mm2e"] = cnt["VEC"]

        sel = {}
        for sl in range(5):
            @VE
            def _(v, sl=sl):
                if sl == 0:
                    W(v, "VEC", M["mm2e"])
                Wd(v, sel.get(("sc", sl - 1), 0))
                for i in range(4):
                    mm = v.bn_stats(bnst[:, i * 6:(i + 1) * 6], xselb[sl][:, i * 512:(i + 1) * 512])
                inc(mm, "VEC")
                W(v, "VEC", cnt["VEC"])
                inc(v.bn_aggr(mv[:], bnst[:]), "VEC")
                sel[("stat", sl)] = cnt["VEC"]

            @SE
            def _(sc, sl=sl):
                W(sc, "VEC", sel[("stat", sl)])
                inc(sc.activation(lnc1[:], mv[:, 1:2], AF.Ln, bias=f32(EPS), scale=f32(1.0)), "ACT")
                W(sc, "ACT", cnt["ACT"])
                inc(sc.activation(lnc1[:], lnc1[:], AF.Exp, bias=f32(0.0), scale=f32(-0.5)), "ACT")
                sel[("rs", sl)] = cnt["ACT"]

            @VE
            def _(v, sl=sl):
                W(v, "ACT", sel[("rs", sl)])
                inc(v.tensor_tensor(lnc2[:], mv[:, 0:1], lnc1[:], op=ALU.mult), "VEC")
                W(v, "VEC", cnt["VEC"])
                inc(v.tensor_scalar(lnc2[:], lnc2[:], -1.0, None, op0=ALU.mult), "VEC")
                W(v, "VEC", cnt["VEC"])
                inc(v.tensor_scalar(x1t[:], xselb[sl][:], lnc1[:], lnc2[:],
                                    op0=ALU.mult, op1=ALU.add), "VEC")
                sel[("y", sl)] = cnt["VEC"]

            @GE
            def _(g, sl=sl):
                if sl == 0:
                    Wd(g, M["base_out"])
                W(g, "VEC", sel[("y", sl)])
                dq_pre(g, "POOLQ")
                sel[("sc", sl)] = dq_post(g.indirect_dma_start(
                    out=out, out_offset=IndirectOffsetOnAxis(ap=gidx[:, sl:sl + 1], axis=0),
                    in_=x1t[:], in_offset=None,
                    bounds_check=LT - 1, oob_is_err=False), "POOLQ")

        if _scope_state["cm"] is not None:
            _scope_state["cm"].__exit__(None, None, None)
            _scope_state["cm"] = None

        @GE
        def _(g):
            Wd(g, snapq("SYNQ"))
            Wd(g, snapq("POOLQ"))
            Wd(g, snapq("SCLQ"))
            W(g, "PE", cnt["PE"])
            W(g, "ACT", cnt["ACT"])
            W(g, "VEC", cnt["VEC"])
            W(g, "POOL", cnt["POOL"])

    for n in reversed(list(opens)):
        opens.pop(n).__exit__(None, None, None)
    return nc



def _host_consts(c):
    tiles = [q_tile(c, s) for s in range(NS)]
    rows = np.concatenate([np.arange(g * 128, (g + 1) * 128) for g in tiles])
    inv = 10000.0 ** (-(np.arange(64, dtype=np.float64)) / 64.0)
    ang = rows[:, None].astype(np.float64) * inv[None, :]
    cosr = np.tile(np.cos(ang).astype(np.float32), (1, 4))
    sinr = np.tile(np.sin(ang).astype(np.float32), (1, 4))
    masks = np.zeros((128, NS * 1024), dtype=ml_dtypes.bfloat16)
    for s in range(NS):
        pi = (s + c) % NC
        kk = np.arange(128)[:, None]
        jq = np.arange(1024)[None, :]
        jj, ql = jq // 128, jq % 128
        allowed = (jj * 128 + kk) <= (pi * 128 + ql)
        masks[:, s * 1024:(s + 1) * 1024] = allowed.astype(ml_dtypes.bfloat16)
    return rows, cosr, sinr, masks


def kernel(L0_attn_x, L0_attn_Wq, L0_attn_bq, L0_attn_Wk, L0_attn_bk,
           L0_attn_Wv, L0_attn_bv, L0_attn_Wo, L0_attn_bo,
           L0_ffn_w_r, L0_ffn_b_r, L0_ffn_w1, L0_ffn_b1, L0_ffn_w2, L0_ffn_b2):
    x = np.ascontiguousarray(np.asarray(L0_attn_x, dtype=np.float32).reshape(T, C))
    com = dict(
        Wq=np.ascontiguousarray(np.asarray(L0_attn_Wq, np.float32)),
        bq=np.asarray(L0_attn_bq, np.float32).reshape(1, C),
        Wk=np.ascontiguousarray(np.asarray(L0_attn_Wk, np.float32)),
        bk=np.asarray(L0_attn_bk, np.float32).reshape(1, DKV),
        Wv=np.ascontiguousarray(np.asarray(L0_attn_Wv, np.float32)),
        bv=np.asarray(L0_attn_bv, np.float32).reshape(1, DKV),
        Wo=np.ascontiguousarray(np.asarray(L0_attn_Wo, np.float32)),
        bo=np.asarray(L0_attn_bo, np.float32).reshape(1, C),
        w1=np.ascontiguousarray(np.asarray(L0_ffn_w1, np.float32)),
        b1=np.asarray(L0_ffn_b1, np.float32).reshape(1, DF),
        w2=np.ascontiguousarray(np.asarray(L0_ffn_w2, np.float32)),
        b2=np.asarray(L0_ffn_b2, np.float32).reshape(1, C),
        wr=np.asarray(L0_ffn_w_r, np.float32).reshape(1, C),
        br=np.asarray(L0_ffn_b_r, np.float32).reshape(1, 1),
        ident=np.eye(128, dtype=ml_dtypes.bfloat16),
    )
    in_maps = []
    row_lists = []
    for c in range(NC):
        rows, cosr, sinr, masks = _host_consts(c)
        row_lists.append(rows)
        m = dict(com)
        m["x_loc"] = np.ascontiguousarray(x[rows])
        m["cosT"] = np.ascontiguousarray(cosr)
        m["sinT"] = np.ascontiguousarray(sinr)
        m["masks"] = np.ascontiguousarray(masks)
        in_maps.append(m)

    nc_prog = build_program(taps=_TAPS)
    res = run_bass_kernel_spmd(nc_prog, in_maps, core_ids=list(range(NC)))

    full = np.empty((T, C), dtype=np.float32)
    for c in range(NC):
        full[row_lists[c]] = res.results[c]["out"]
    kernel._last_results = res
    return full.reshape(1, T, C)


# version 04
# speedup vs baseline: 1.1138x; 1.1039x over previous
import numpy as np
import ml_dtypes

import concourse.bass as bass
import concourse.mybir as mybir
from concourse.bass import IndirectOffsetOnAxis
from concourse.bass_utils import run_bass_kernel_spmd

F32 = mybir.dt.float32
BF16 = mybir.dt.bfloat16
U32 = mybir.dt.uint32
AF = mybir.ActivationFunctionType
ALU = mybir.AluOpType
f32 = np.float32

T, C, NKV, DH, DKV, DF = 8192, 2048, 4, 128, 512, 8192
NC, NS = 8, 8
LT = NS * 128
CAP = 640
EPS = 1e-5
SHIFT = 8.0
INVSQD = float(1.0 / np.sqrt(128.0))
BISECT = 48
RG = [list(range(NC))]

_TAPS = ()


def q_tile(c, s):
    return 8 * s + ((s + c) % NC)


def kchunk_owner(j):
    s = j // 8
    return (j % 8 - s) % NC, s


def build_program(taps=()):
    nc = bass.Bass("TRN2", target_bir_lowering=False)

    dI = lambda n, s, d=F32: nc.dram_tensor(n, s, d, kind="ExternalInput").ap()
    x_loc = dI("x_loc", [LT, C])
    Wq = dI("Wq", [C, C]); bq = dI("bq", [1, C])
    Wk = dI("Wk", [C, DKV]); bk = dI("bk", [1, DKV])
    Wv = dI("Wv", [C, DKV]); bv = dI("bv", [1, DKV])
    Wo = dI("Wo", [DKV, C]); bo = dI("bo", [1, C])
    w1 = dI("w1", [C, DF]); b1 = dI("b1", [1, DF])
    w2 = dI("w2", [DF, C]); b2 = dI("b2", [1, C])
    wr = dI("wr", [1, C]); br = dI("br", [1, 1])
    cosT = dI("cosT", [LT, 256])
    sinT = dI("sinT", [LT, 256])
    masksI = dI("masks", [128, NS * 1024], BF16)
    identI = dI("ident", [128, 128], BF16)

    out = nc.dram_tensor("out", [LT, C], F32, kind="ExternalOutput").ap()
    tap_shapes = {"x1": [LT, C], "thr": [1, 1], "mask": [128, NS],
                  "scores": [64, 128], "gidx": [128, 5], "ynT": [128, NKV * 1024]}
    tap_out = {t: nc.dram_tensor("tap_" + t, tap_shapes[t], F32, kind="ExternalOutput").ap()
               for t in taps}

    k_in = nc.dram_tensor("k_in", [DKV, LT], BF16).ap()
    k_out = nc.dram_tensor("k_out", [NC * DKV, LT], BF16, addr_space="Shared").ap()
    v_in = nc.dram_tensor("v_in", [LT, DKV], BF16).ap()
    v_out = nc.dram_tensor("v_out", [NC * LT, DKV], BF16, addr_space="Shared").ap()
    sc_in = nc.dram_tensor("sc_in", [NS, 128], F32).ap()
    sc_out = nc.dram_tensor("sc_out", [NC * NS, 128], F32, addr_space="Shared").ap()
    x1_dram = nc.dram_tensor("x1_dram", [LT, C], F32).ap()
    gbuf = nc.dram_tensor("gbuf", [LT, NS], F32).ap()

    sems = {}
    cnt = {}
    opens = {}

    def semalloc(k):
        cm = nc.semaphore("sem_" + k)
        opens["sem_" + k] = cm
        sems[k] = cm.__enter__()
        cnt[k] = 0

    for k in ("PE", "VEC", "ACT", "POOL", "CC"):
        semalloc(k)

    def inc(instr, k, n=1):
        instr.then_inc(sems[k], n)
        cnt[k] += n
        return cnt[k]

    dqs = {}
    for qn, dep in (("SYNQ", 4), ("POOLQ", 4), ("SCLQ", 4)):
        slots = []
        for i in range(dep):
            cm = nc.semaphore(f"q_{qn}{i}")
            opens[f"q_{qn}{i}"] = cm
            slots.append(cm.__enter__())
        dqs[qn] = {"sems": slots, "val": [0] * dep, "n": 0, "d": dep}

    def dq_pre(eng, qn):
        q = dqs[qn]
        i = q["n"] % q["d"]
        if q["val"][i] > 0:
            eng.wait_ge(q["sems"][i], q["val"][i])

    def dq_post(instr, qn):
        q = dqs[qn]
        i = q["n"] % q["d"]
        q["val"][i] += 16
        instr.then_inc(q["sems"][i], 16)
        q["n"] += 1
        return (qn, i, q["val"][i])

    def snapq(qn):
        q = dqs[qn]
        return [(qn, i, q["val"][i]) for i in range(q["d"]) if q["val"][i] > 0]

    def salloc(n, s, d):
        cm = nc.sbuf_tensor(n, s, d)
        opens[n] = cm
        return cm.__enter__()

    def palloc(n, s, d=F32):
        cm = nc.psum_tensor(n, s, d)
        opens[n] = cm
        return cm.__enter__()

    def free(*names):
        for n in names:
            opens.pop(n).__exit__(None, None, None)

    identb = salloc("identb", [128, 128], BF16)
    ident32 = salloc("ident32", [128, 128], F32)
    ones_bf = salloc("ones_bf", [1, 512], BF16)
    ones_f = salloc("ones_f", [1, 128], F32)
    ones_c = salloc("ones_c", [128, 1], BF16)
    ones64 = salloc("ones64", [64, 1], F32)
    zerop = salloc("zerop", [128, 1], F32)
    bq_eff = salloc("bq_eff", [1, DKV], BF16)
    bk_b = salloc("bk_b", [1, DKV], BF16)
    bv_b = salloc("bv_b", [1, DKV], BF16)
    s_loc = salloc("s_loc", [128, NS], F32)
    m_loc = salloc("m_loc", [128, NS], F32)
    bnst = salloc("bnst", [128, 24], F32)
    mv = salloc("mv", [128, 2], F32)
    lnc1 = salloc("lnc1", [128, 1], F32)
    lnc2 = salloc("lnc2", [128, 1], F32)
    gidx = salloc("gidx", [128, 5], U32)
    gf32 = salloc("gf32", [128, 5], F32)
    x1t = salloc("x1t", [128, C], F32)
    v1t = salloc("v1t", [128, C], F32)

    qTa = salloc("qTa", [128, NKV * 1024], BF16)
    ynT = salloc("ynT", [128, NKV * 1024], BF16)
    Wo_sb = salloc("Wo_sb", [128, NKV * C], BF16)
    bo_b = salloc("bo_b", [1, C], BF16)

    xfm_all = salloc("xfm_all", [128, 16 * 1024], BF16)
    cos4 = salloc("cos4", [128, NS * 256], F32)
    sin4 = salloc("sin4", [128, NS * 256], F32)
    Wk_sb = salloc("Wk_sb", [128, 16 * 512], BF16)
    Wv_sb = salloc("Wv_sb", [128, 16 * 512], BF16)
    Wq_eff = salloc("Wq_eff", [128, 16 * 512], BF16)
    Wq_tmp = [salloc(f"Wq_tmp{i}", [128, C], BF16) for i in range(2)]
    xbuf = [salloc(f"xbuf{i}", [128, C], F32) for i in range(2)]
    xb16 = [salloc(f"xb16{i}", [128, C], BF16) for i in range(2)]
    qr = salloc("qr", [128, DKV], BF16)
    kr = salloc("kr", [128, DKV], BF16)
    vr = salloc("vr", [128, DKV], BF16)
    ktT = salloc("ktT", [128, NKV * 128], BF16)
    rt1 = salloc("rt1", [128, 256], F32)
    rt2 = salloc("rt2", [128, 256], F32)

    btmp = salloc("btmp", [1, C], F32)
    bt1 = salloc("bt1", [1, 512], F32)
    bt2 = salloc("bt2", [1, 512], F32)

    p1ps = [palloc(f"p1ps{i}", [128, 512]) for i in range(2)]
    tpps = [palloc(f"tpps{i}", [128, 512], BF16) for i in range(2)]

    M = {}

    _scope_state = {"cm": None}

    def _scope_switch(name):
        if _scope_state["cm"] is not None:
            _scope_state["cm"].__exit__(None, None, None)
        cm = nc.named_scope(name)
        cm.__enter__()
        _scope_state["cm"] = cm

    with nc.Block() as block:
        TE, VE, SE, GE, SY = block.tensor, block.vector, block.scalar, block.gpsimd, block.sync

        def W(eng, k, v):
            if v > 0:
                eng.wait_ge(sems[k], v)

        def Wd(eng, h):
            if isinstance(h, list):
                for hh in h:
                    Wd(eng, hh)
            elif isinstance(h, tuple):
                qn, i, v = h
                eng.wait_ge(dqs[qn]["sems"][i], v)

        def dma(eng, qn, instr_fn):
            dq_pre(eng, qn)
            return dq_post(instr_fn(), qn)

        @SY
        def _(q):
            dq_pre(q, "SYNQ")
            M["ident"] = dq_post(q.dma_start(out=identb[:], in_=identI), "SYNQ")
            dq_pre(q, "SYNQ")
            dq_post(q.dma_start(out=cos4[:].rearrange("p (s f) -> p s f", s=NS),
                               in_=cosT.rearrange("(s p) f -> p s f", p=128)), "SYNQ")
            dq_pre(q, "SYNQ")
            dq_post(q.dma_start(out=sin4[:].rearrange("p (s f) -> p s f", s=NS),
                               in_=sinT.rearrange("(s p) f -> p s f", p=128)), "SYNQ")
            dq_pre(q, "SYNQ")
            dq_post(q.dma_start(out=btmp[:], in_=bq), "SYNQ")
            M["p0"] = snapq("SYNQ")

        @GE
        def _(g):
            dq_pre(g, "POOLQ")
            dq_post(g.dma_start(out=Wk_sb[:].rearrange("p (c n) -> p c n", c=16),
                               in_=Wk.rearrange("(c p) n -> p c n", p=128)), "POOLQ")
            dq_pre(g, "POOLQ")
            dq_post(g.dma_start(out=Wv_sb[:].rearrange("p (c n) -> p c n", c=16),
                               in_=Wv.rearrange("(c p) n -> p c n", p=128)), "POOLQ")
            dq_pre(g, "POOLQ")
            dq_post(g.dma_start(out=bk_b[:], in_=bk), "POOLQ")
            dq_pre(g, "POOLQ")
            dq_post(g.dma_start(out=bv_b[:], in_=bv), "POOLQ")
            M["wkv"] = snapq("POOLQ")

        @VE
        def _(v):
            v.memset(ones_bf[:], 1.0)
            v.memset(ones_f[:], 1.0)
            v.memset(ones_c[:], 1.0)
            v.memset(ones64[:], 1.0)
            inc(v.memset(zerop[:], 0.0), "VEC")
            M["ones"] = cnt["VEC"]
            Wd(v, M["ident"])
            inc(v.tensor_copy(ident32[:], identb[:]), "VEC")
            Wd(v, M["p0"])
            bqv = btmp[:].rearrange("o (h j i) -> o h j i", h=4, j=4)
            v.tensor_tensor(bt1[:].rearrange("o (h i) -> o h i", h=4),
                            bqv[:, :, 0, :], bqv[:, :, 1, :], op=ALU.add)
            inc(v.tensor_tensor(bt2[:].rearrange("o (h i) -> o h i", h=4),
                                bqv[:, :, 2, :], bqv[:, :, 3, :], op=ALU.add), "VEC")
            W(v, "VEC", cnt["VEC"])
            inc(v.tensor_tensor(bq_eff[:], bt1[:], bt2[:], op=ALU.add), "VEC")
            M["bqe"] = cnt["VEC"]

        for i in range(16):
            @GE
            def _(g, i=i):
                W(g, "VEC", M.get(f"wqe{i - 2}", 0))
                dq_pre(g, "POOLQ")
                M[f"wql{i}"] = dq_post(g.dma_start(out=Wq_tmp[i % 2][:],
                                   in_=Wq[i * 128:(i + 1) * 128, :]), "POOLQ")

            @VE
            def _(v, i=i):
                Wd(v, M[f"wql{i}"])
                wv = Wq_tmp[i % 2][:].rearrange("p (h j i) -> p h j i", h=4, j=4)
                eff = Wq_eff[:, i * 512:(i + 1) * 512].rearrange("p (h i) -> p h i", h=4)
                inc(v.tensor_tensor(eff, wv[:, :, 0, :], wv[:, :, 1, :], op=ALU.add), "VEC")
                W(v, "VEC", cnt["VEC"])
                inc(v.tensor_tensor(eff, eff, wv[:, :, 2, :], op=ALU.add), "VEC")
                W(v, "VEC", cnt["VEC"])
                inc(v.tensor_tensor(eff, eff, wv[:, :, 3, :], op=ALU.add), "VEC")
                M[f"wqe{i}"] = cnt["VEC"]
        M["wqeff"] = cnt["VEC"]
        free("bt2", "bt1", "btmp")

        _scope_switch('P1')
        for s in range(NS):
            @SY
            def _(q, s=s):
                W(q, "VEC", M.get(f"xc{s - 2}", 0))
                dq_pre(q, "SYNQ")
                M[f"xl{s}"] = dq_post(q.dma_start(out=xbuf[s % 2][:], in_=x_loc[s * 128:(s + 1) * 128, :]), "SYNQ")

            @VE
            def _(v, s=s):
                Wd(v, M[f"xl{s}"])
                W(v, "PE", M.get(f"xT{s - 2}", 0))
                inc(v.tensor_copy(xb16[s % 2][:], xbuf[s % 2][:]), "VEC")
                M[f"xc{s}"] = cnt["VEC"]

            for b in range(4):
                @TE
                def _(t, s=s, b=b):
                    W(t, "VEC", M[f"xc{s}"])
                    W(t, "VEC", M.get(f"xEb{s}_{b - 2}", M.get(f"xEb{s - 1}_{b + 2}", 0)))
                    for i in range(4):
                        mm = t.transpose(tpps[b % 2][:, i * 128:(i + 1) * 128],
                                         xb16[s % 2][:, (4 * b + i) * 128:(4 * b + i + 1) * 128],
                                         identb[:])
                    inc(mm, "PE")
                    M[f"xTb{s}_{b}"] = cnt["PE"]
                    if b == 3:
                        M[f"xT{s}"] = cnt["PE"]

                @VE
                def _(v, s=s, b=b):
                    W(v, "PE", M[f"xTb{s}_{b}"])
                    inc(v.tensor_copy(
                        xfm_all[:].rearrange("p (fc t) -> p fc t", fc=16)[:, 4 * b:4 * b + 4,
                                                                         s * 128:(s + 1) * 128],
                        tpps[b % 2][:].rearrange("p (fc t) -> p fc t", fc=4)), "VEC")
                    M[f"xEb{s}_{b}"] = cnt["VEC"]
                    if b == 3:
                        M[f"xE{s}"] = cnt["VEC"]

            @TE
            def _(t, s=s):
                if s == 0:
                    Wd(t, M["wkv"])
                    W(t, "VEC", M["ones"])
                W(t, "VEC", M[f"xE{s}"])
                W(t, "VEC", M.get(f"rpk{s - 1}", 0))
                for fc in range(16):
                    t.matmul(p1ps[0][:], xfm_all[:, fc * 1024 + s * 128: fc * 1024 + (s + 1) * 128],
                             Wk_sb[:, fc * 512:(fc + 1) * 512], start=(fc == 0), stop=False)
                inc(t.matmul(p1ps[0][:], ones_bf[:, 0:128], bk_b[:], start=False, stop=True), "PE")
                M[f"kp{s}"] = cnt["PE"]
                W(t, "VEC", M.get(f"vev{s - 1}", 0))
                for fc in range(16):
                    t.matmul(p1ps[1][:], xfm_all[:, fc * 1024 + s * 128: fc * 1024 + (s + 1) * 128],
                             Wv_sb[:, fc * 512:(fc + 1) * 512], start=(fc == 0), stop=False)
                inc(t.matmul(p1ps[1][:], ones_bf[:, 0:128], bv_b[:], start=False, stop=True), "PE")
                M[f"vp{s}"] = cnt["PE"]

            @VE
            def _(v, s=s):
                W(v, "PE", M[f"kp{s}"])
                W(v, "PE", M.get(f"ktT{s - 1}", 0))
                ke = p1ps[0][:, 0:512:2]
                ko = p1ps[0][:, 1:512:2]
                cs = cos4[:, s * 256:(s + 1) * 256]
                sn = sin4[:, s * 256:(s + 1) * 256]
                v.tensor_tensor(rt1[:], ke, cs, op=ALU.mult)
                inc(v.tensor_tensor(rt2[:], ko, sn, op=ALU.mult), "VEC")
                W(v, "VEC", cnt["VEC"])
                krv = kr[:].rearrange("p (h two i) -> p h two i", h=4, two=2)
                inc(v.tensor_tensor(krv[:, :, 0, :], rt1[:], rt2[:], op=ALU.subtract), "VEC")
                W(v, "VEC", cnt["VEC"])
                v.tensor_tensor(rt1[:], ke, sn, op=ALU.mult)
                inc(v.tensor_tensor(rt2[:], ko, cs, op=ALU.mult), "VEC")
                W(v, "VEC", cnt["VEC"])
                inc(v.tensor_tensor(krv[:, :, 1, :], rt1[:], rt2[:], op=ALU.add), "VEC")
                M[f"rpk{s}"] = cnt["VEC"]
                W(v, "PE", M[f"vp{s}"])
                Wd(v, M.get(f"vst{s - 1}", 0))
                inc(v.tensor_copy(vr[:], p1ps[1][:]), "VEC")
                M[f"vev{s}"] = cnt["VEC"]

            @TE
            def _(t, s=s):
                W(t, "VEC", M[f"rpk{s}"])
                W(t, "VEC", M.get(f"ktE{s - 1}", 0))
                for h in range(4):
                    mm = t.transpose(tpps[0][:, h * 128:(h + 1) * 128],
                                     kr[:, h * 128:(h + 1) * 128], identb[:])
                inc(mm, "PE")
                M[f"ktT{s}"] = cnt["PE"]

            @VE
            def _(v, s=s):
                W(v, "PE", M[f"ktT{s}"])
                Wd(v, M.get(f"ktst{s - 1}", 0))
                inc(v.tensor_copy(ktT[:], tpps[0][:]), "VEC")
                M[f"ktE{s}"] = cnt["VEC"]

            @SY
            def _(q, s=s):
                W(q, "VEC", M[f"ktE{s}"])
                dq_pre(q, "SYNQ")
                M[f"ktst{s}"] = dq_post(q.dma_start(
                    out=k_in.rearrange("(h d) t -> d h t", h=4)[:, :, s * 128:(s + 1) * 128],
                    in_=ktT[:].rearrange("d (h t) -> d h t", h=4)), "SYNQ")
                W(q, "VEC", M[f"vev{s}"])
                dq_pre(q, "SYNQ")
                M[f"vst{s}"] = dq_post(q.dma_start(out=v_in[s * 128:(s + 1) * 128, :], in_=vr[:]), "SYNQ")

        M["kvst"] = snapq("SYNQ")

        @GE
        def _(g):
            Wd(g, M["kvst"])
            inc(g.collective_compute("AllGather", ALU.bypass, replica_groups=RG,
                                     ins=[k_in], outs=[k_out]), "CC")
            inc(g.collective_compute("AllGather", ALU.bypass, replica_groups=RG,
                                     ins=[v_in], outs=[v_out]), "CC")
            M["agkv"] = cnt["CC"]

        for s in range(NS):
            @TE
            def _(t, s=s):
                if s == 0:
                    W(t, "VEC", M["wqeff"])
                    W(t, "VEC", M["bqe"])
                if s <= 1:
                    W(t, "VEC", M[f"vev{NS - 1}"])
                W(t, "VEC", M.get(f"rpq{s - 2}", 0))
                for fc in range(16):
                    t.matmul(p1ps[s % 2][:], xfm_all[:, fc * 1024 + s * 128: fc * 1024 + (s + 1) * 128],
                             Wq_eff[:, fc * 512:(fc + 1) * 512], start=(fc == 0), stop=False)
                inc(t.matmul(p1ps[s % 2][:], ones_bf[:, 0:128], bq_eff[:], start=False, stop=True), "PE")
                M[f"qp{s}"] = cnt["PE"]

            @VE
            def _(v, s=s):
                W(v, "PE", M[f"qp{s}"])
                W(v, "PE", M.get(f"qT{s - 1}", 0))
                qe = p1ps[s % 2][:, 0:512:2]
                qo = p1ps[s % 2][:, 1:512:2]
                cs = cos4[:, s * 256:(s + 1) * 256]
                sn = sin4[:, s * 256:(s + 1) * 256]
                v.tensor_tensor(rt1[:], qe, cs, op=ALU.mult)
                inc(v.tensor_tensor(rt2[:], qo, sn, op=ALU.mult), "VEC")
                W(v, "VEC", cnt["VEC"])
                qrv = qr[:].rearrange("p (h two i) -> p h two i", h=4, two=2)
                inc(v.tensor_tensor(qrv[:, :, 0, :], rt1[:], rt2[:], op=ALU.subtract), "VEC")
                W(v, "VEC", cnt["VEC"])
                v.tensor_tensor(rt1[:], qe, sn, op=ALU.mult)
                inc(v.tensor_tensor(rt2[:], qo, cs, op=ALU.mult), "VEC")
                W(v, "VEC", cnt["VEC"])
                inc(v.tensor_tensor(qrv[:, :, 1, :], rt1[:], rt2[:], op=ALU.add), "VEC")
                M[f"rpq{s}"] = cnt["VEC"]

            @TE
            def _(t, s=s):
                W(t, "VEC", M[f"rpq{s}"])
                if s == 0:
                    W(t, "VEC", M[f"xE{NS - 1}"])
                W(t, "VEC", M.get(f"qE{s - 1}", 0))
                for h in range(4):
                    mm = t.transpose(tpps[1][:, h * 128:(h + 1) * 128],
                                     qr[:, h * 128:(h + 1) * 128], identb[:])
                inc(mm, "PE")
                M[f"qT{s}"] = cnt["PE"]

            @VE
            def _(v, s=s):
                W(v, "PE", M[f"qT{s}"])
                for h in range(4):
                    mm = v.tensor_copy(qTa[:, h * 1024 + s * 128: h * 1024 + (s + 1) * 128],
                                       tpps[1][:, h * 128:(h + 1) * 128])
                inc(mm, "VEC")
                M[f"qE{s}"] = cnt["VEC"]

        M["qT"] = cnt["VEC"]

        _scope_switch('P2attn')
        free("rt2", "rt1", "ktT", "vr", "kr", "qr",
             "xb161", "xb160", "xbuf1", "xbuf0", "Wq_tmp1", "Wq_tmp0",
             "Wq_eff", "Wv_sb", "Wk_sb", "sin4", "cos4", "xfm_all")
        free("tpps1", "tpps0", "p1ps1", "p1ps0")

        mask_sb = salloc("mask_sb", [128, NS * 1024], BF16)

        @SY
        def _(q):
            W(q, "VEC", M["qT"])
            dq_pre(q, "SYNQ")
            M["maskl"] = dq_post(q.dma_start(out=mask_sb[:], in_=masksI), "SYNQ")

        @GE
        def _(g):
            dq_pre(g, "POOLQ")
            dq_post(g.dma_start(out=bo_b[:], in_=bo), "POOLQ")
            dq_pre(g, "POOLQ")
            dq_post(g.dma_start(out=Wo_sb[:].rearrange("p (c n) -> p c n", c=4),
                               in_=Wo.rearrange("(c p) n -> p c n", p=128)), "POOLQ")
            M["wo"] = snapq("POOLQ")

        Kh = [salloc(f"Kh{i}", [128, NC * 1024], BF16) for i in range(2)]
        Vh = [salloc(f"Vh{i}", [128, 64 * 128], BF16) for i in range(2)]
        pT = [salloc(f"pT{i}", [128, 1024], BF16) for i in range(2)]
        dnm = salloc("dnm", [1, 1024], F32)
        rb_sb = salloc("rb_sb", [128, 1024], F32)

        sT_ps = [palloc(f"sT_ps{i}", [128, 1024]) for i in range(2)]
        yA_ps = palloc("yA_ps", [128, 512])
        yB_ps = palloc("yB_ps", [128, 512])
        dn_ps = palloc("dn_ps", [128, 512])

        exp_m, mask_m, sco_m, av_m = {}, {}, {}, {}
        A0_attn = cnt["ACT"]

        def kv_load(h):
            @SE
            def _(sc, h=h):
                if h == 0:
                    W(sc, "CC", M["agkv"])
                if h <= 1:
                    W(sc, "VEC", M["qT"])
                    W(sc, "PE", M[f"qT{NS - 1}"])
                W(sc, "VEC", M.get(f"ynm{h - 2}", 0))
                dq_pre(sc, "SCLQ")
                dq_post(sc.dma_start(
                    out=Kh[h % 2][:].rearrange("d (r t) -> d r t", r=NC),
                    in_=k_out.rearrange("(r h d) t -> h d r t", r=NC, h=4)[h]), "SCLQ")
                for r in range(NC):
                    dq_pre(sc, "SCLQ")
                    dq_post(sc.dma_start(
                        out=Vh[h % 2][:].rearrange("p (r t d) -> p r t d", r=NC, t=8)[:, r],
                        in_=v_out[r * 1024:(r + 1) * 1024, h * 128:(h + 1) * 128]
                        .rearrange("(t p) d -> p t d", p=128)), "SCLQ")
                M[f"kvh{h}"] = snapq("SCLQ")

        kv_load(0)
        kv_load(1)

        def attn_scores(t, h, j):
            W(t, "ACT", exp_m.get((h, j - 2), 0))
            rj, sj = kchunk_owner(j)
            qs = 128 * (j // 8)
            lhs = Kh[h % 2][:, rj * 1024 + sj * 128: rj * 1024 + (sj + 1) * 128]
            if qs < 512:
                t.matmul(sT_ps[j % 2][:, qs:512], lhs,
                         qTa[:, h * 1024 + qs: h * 1024 + 512], start=True, stop=True)
                mm = t.matmul(sT_ps[j % 2][:, 512:1024], lhs,
                              qTa[:, h * 1024 + 512: h * 1024 + 1024], start=True, stop=True)
            else:
                mm = t.matmul(sT_ps[j % 2][:, qs:1024], lhs,
                              qTa[:, h * 1024 + qs: h * 1024 + 1024], start=True, stop=True)
            sco_m[(h, j)] = inc(mm, "PE")

        def attn_av(t, h, j):
            W(t, "VEC", mask_m[(h, j)])
            rj, sj = kchunk_owner(j)
            qs = 128 * (j // 8)
            vlhs = Vh[h % 2][:, (rj * 8 + sj) * 128:(rj * 8 + sj + 1) * 128]
            if qs < 512:
                t.matmul(yA_ps[:, qs:512], vlhs, pT[j % 2][:, qs:512],
                         start=(j == 0), stop=(j == 31))
                t.matmul(yB_ps[:], vlhs, pT[j % 2][:, 512:1024],
                         start=(j == 0), stop=False)
                t.matmul(dn_ps[0:1, qs:512], ones_c[:], pT[j % 2][:, qs:512],
                         start=(j == 0), stop=(j == 31))
                mm = t.matmul(dn_ps[32:33, :], ones_c[:], pT[j % 2][:, 512:1024],
                              start=(j == 0), stop=False)
            else:
                t.matmul(yB_ps[:, qs - 512:512], vlhs, pT[j % 2][:, qs:1024],
                         start=False, stop=(j == 63))
                mm = t.matmul(dn_ps[32:33, qs - 512:512], ones_c[:],
                              pT[j % 2][:, qs:1024], start=False, stop=(j == 63))
            av_m[(h, j)] = inc(mm, "PE")

        for h in range(4):
            @TE
            def _(t, h=h):
                if h == 0:
                    W(t, "VEC", M["qT"])
                Wd(t, M[f"kvh{h}"])
                if h > 0:
                    W(t, "VEC", M[f"ynm{h - 1}"])
                attn_scores(t, h, 0)

            for j in range(64):
                exp_m[(h, j)] = A0_attn + 64 * h + j + 1

                @VE
                def _(v, h=h, j=j):
                    if (h, j) == (0, 0):
                        Wd(v, M["maskl"])
                    W(v, "ACT", exp_m[(h, j)])
                    s0 = j // 8
                    qs = 128 * s0
                    mask_m[(h, j)] = inc(v.tensor_tensor(
                        pT[j % 2][:, qs:qs + 128], pT[j % 2][:, qs:qs + 128],
                        mask_sb[:, s0 * 1024 + (j % 8) * 128: s0 * 1024 + (j % 8 + 1) * 128],
                        op=ALU.mult), "VEC")

                @TE
                def _(t, h=h, j=j):
                    if j < 63:
                        attn_scores(t, h, j + 1)
                    attn_av(t, h, j)

            @SE
            def _(sc, h=h):
                for j in range(64):
                    W(sc, "PE", sco_m[(h, j)])
                    W(sc, "PE", av_m.get((h, j - 2), 0))
                    qs = 128 * (j // 8)
                    got = inc(sc.activation(
                        pT[j % 2][:, qs:1024], sT_ps[j % 2][:, qs:1024], AF.Exp,
                        bias=f32(-SHIFT), scale=f32(INVSQD)), "ACT")
                    assert got == exp_m[(h, j)], (h, j, got)

            @VE
            def _(v, h=h):
                W(v, "PE", av_m[(h, 63)])
                v.reciprocal(dnm[:, 0:512], dn_ps[0:1, :])
                inc(v.reciprocal(dnm[:, 512:1024], dn_ps[32:33, :]), "VEC")
                M[f"rcp{h}"] = cnt["VEC"]

            @TE
            def _(t, h=h):
                W(t, "VEC", M[f"rcp{h}"])
                W(t, "ACT", exp_m[(h, 63)])
                t.matmul(sT_ps[0][:, 0:512], ones_f[:], dnm[:, 0:512], start=True, stop=True)
                inc(t.matmul(sT_ps[0][:, 512:1024], ones_f[:], dnm[:, 512:1024],
                             start=True, stop=True), "PE")
                M[f"rbm{h}"] = cnt["PE"]

            @VE
            def _(v, h=h):
                W(v, "PE", M[f"rbm{h}"])
                inc(v.tensor_copy(rb_sb[:], sT_ps[0][:]), "VEC")
                W(v, "VEC", cnt["VEC"])
                v.tensor_tensor(ynT[:, h * 1024: h * 1024 + 512], yA_ps[:],
                                rb_sb[:, 0:512], op=ALU.mult)
                inc(v.tensor_tensor(ynT[:, h * 1024 + 512: h * 1024 + 1024], yB_ps[:],
                                    rb_sb[:, 512:1024], op=ALU.mult), "VEC")
                M[f"ynm{h}"] = cnt["VEC"]

            if h + 2 < 4:
                kv_load(h + 2)

        M["attn"] = cnt["VEC"]

        _scope_switch('P3')
        free("rb_sb", "dnm", "pT1", "pT0", "Vh1", "Vh0", "Kh1", "Kh0", "mask_sb")
        free("dn_ps", "yB_ps", "yA_ps", "sT_ps1", "sT_ps0")

        wr_bc = salloc("wr_bc", [128, C], F32)
        br_bc = salloc("br_bc", [128, 1], F32)
        sjunk = salloc("sjunk", [128, C], BF16)
        sc_row = salloc("sc_row", [NS, 128], F32)
        oPs = palloc("oPs", [128, 2048])
        tp3f = palloc("tp3f", [128, 512])

        @GE
        def _(g):
            W(g, "VEC", M["attn"])
            dq_pre(g, "POOLQ")
            dq_post(g.dma_start(out=wr_bc[0:1, :], in_=wr), "POOLQ")
            dq_pre(g, "POOLQ")
            dq_post(g.dma_start(out=br_bc[0:1, 0:1], in_=br), "POOLQ")
            M["wrl"] = snapq("POOLQ")

        for i in range(4):
            @TE
            def _(t, i=i):
                if i == 0:
                    Wd(t, M["wrl"])
                    W(t, "VEC", M["attn"])
                else:
                    W(t, "VEC", M[f"wrbe{i - 1}"])
                inc(t.matmul(tp3f[:], ones_f[:], wr_bc[0:1, i * 512:(i + 1) * 512],
                             start=True, stop=True), "PE")
                M[f"wrb{i}"] = cnt["PE"]

            @VE
            def _(v, i=i):
                W(v, "PE", M[f"wrb{i}"])
                inc(v.tensor_copy(wr_bc[:, i * 512:(i + 1) * 512], tp3f[:]), "VEC")
                M[f"wrbe{i}"] = cnt["VEC"]

        @TE
        def _(t):
            W(t, "VEC", M["wrbe3"])
            inc(t.matmul(tp3f[:, 0:1], ones_f[:], br_bc[0:1, 0:1], start=True, stop=True), "PE")
            M["brb"] = cnt["PE"]

        @VE
        def _(v):
            W(v, "PE", M["brb"])
            inc(v.tensor_copy(br_bc[:], tp3f[:, 0:1]), "VEC")
            M["brbe"] = cnt["VEC"]

        A0_p3 = cnt["ACT"]
        for s in range(NS):
            M[f"rs1_{s}"] = A0_p3 + 2 * (s + 1)
        for s in range(NS):
            @SY
            def _(q, s=s):
                W(q, "VEC", M.get(f"oev{s - 1}", 0))
                dq_pre(q, "SYNQ")
                M[f"xr{s}"] = dq_post(q.dma_start(out=v1t[:], in_=x_loc[s * 128:(s + 1) * 128, :]), "SYNQ")

            @TE
            def _(t, s=s):
                if s == 0:
                    Wd(t, M["wo"])
                    W(t, "VEC", M["brbe"])
                W(t, "VEC", M.get(f"oev{s - 1}", 0))
                for ncol in range(4):
                    for h in range(4):
                        t.matmul(oPs[:, ncol * 512:(ncol + 1) * 512],
                                 ynT[:, h * 1024 + s * 128: h * 1024 + (s + 1) * 128],
                                 Wo_sb[:, h * 2048 + ncol * 512: h * 2048 + (ncol + 1) * 512],
                                 start=(h == 0), stop=False)
                    mm = t.matmul(oPs[:, ncol * 512:(ncol + 1) * 512], ones_bf[:, 0:128],
                                  bo_b[:, ncol * 512:(ncol + 1) * 512], start=False, stop=True)
                inc(mm, "PE")
                M[f"op{s}"] = cnt["PE"]

            @VE
            def _(v, s=s):
                W(v, "PE", M[f"op{s}"])
                Wd(v, M[f"xr{s}"])
                Wd(v, M.get(f"x1s{s - 1}", 0))
                for i in range(4):
                    mm = v.tensor_tensor(v1t[:, i * 512:(i + 1) * 512],
                                         v1t[:, i * 512:(i + 1) * 512],
                                         oPs[:, i * 512:(i + 1) * 512], op=ALU.add)
                inc(mm, "VEC")
                M[f"oev{s}"] = cnt["VEC"]
                W(v, "VEC", cnt["VEC"])
                for i in range(4):
                    mm = v.bn_stats(bnst[:, i * 6:(i + 1) * 6], v1t[:, i * 512:(i + 1) * 512])
                inc(mm, "VEC")
                W(v, "VEC", cnt["VEC"])
                inc(v.bn_aggr(mv[:], bnst[:]), "VEC")
                M[f"st1_{s}"] = cnt["VEC"]


            @VE
            def _(v, s=s):
                W(v, "ACT", M[f"rs1_{s}"])
                inc(v.tensor_tensor(lnc2[:], mv[:, 0:1], lnc1[:], op=ALU.mult), "VEC")
                W(v, "VEC", cnt["VEC"])
                inc(v.tensor_scalar(lnc2[:], lnc2[:], -1.0, None, op0=ALU.mult), "VEC")
                W(v, "VEC", cnt["VEC"])
                inc(v.tensor_scalar(x1t[:], v1t[:], lnc1[:], lnc2[:],
                                    op0=ALU.mult, op1=ALU.add), "VEC")
                M[f"x1_{s}"] = cnt["VEC"]
                W(v, "VEC", cnt["VEC"])
                inc(v.scalar_tensor_tensor(sjunk[:], x1t[:], 1.0, wr_bc[:],
                                           op0=ALU.mult, op1=ALU.mult,
                                           accum_out=lnc2[:]), "VEC")
                W(v, "VEC", cnt["VEC"])
                inc(v.tensor_tensor(s_loc[:, s:s + 1], lnc2[:], br_bc[:], op=ALU.add), "VEC")
                M[f"sl{s}"] = cnt["VEC"]

            @SY
            def _(q, s=s):
                W(q, "VEC", M[f"x1_{s}"])
                dq_pre(q, "SYNQ")
                M[f"x1s{s}"] = dq_post(q.dma_start(out=x1_dram[s * 128:(s + 1) * 128, :], in_=x1t[:]), "SYNQ")

        @SE
        def _(sc):
            for s in range(NS):
                W(sc, "VEC", M[f"st1_{s}"])
                inc(sc.activation(lnc1[:], mv[:, 1:2], AF.Ln, bias=f32(EPS), scale=f32(1.0)), "ACT")
                W(sc, "ACT", cnt["ACT"])
                got = inc(sc.activation(lnc1[:], lnc1[:], AF.Exp, bias=f32(0.0), scale=f32(-0.5)), "ACT")
                assert got == M[f"rs1_{s}"], (s, got)

        M["x1d"] = snapq("SYNQ")

        _scope_switch('P4bisect')
        sc_flat = salloc("sc_flat", [1, 8192], F32)
        srep = salloc("srep", [64, 8192], F32)
        bjunk = salloc("bjunk", [64, 8192], BF16)
        rowcnt = salloc("rowcnt", [64, 1], F32)
        lo_ = salloc("lo_", [1, 1], F32)
        d64_ = salloc("d64_", [1, 1], F32)
        pv_i = salloc("pv_i", [64, 1], U32)
        pvec1 = salloc("pvec1", [64, 1], F32)
        tcol1 = salloc("tcol1", [64, 1], F32)
        thr_col = salloc("thr_col", [64, 1], F32)
        ps4b = palloc("ps4b", [64, 2])
        th128 = palloc("th128", [128, 1])
        cnt_ps = palloc("cnt_ps", [1, 1])

        @TE
        def _(t):
            W(t, "VEC", M[f"sl{NS - 1}"])
            inc(t.transpose(tp3f[0:8, 0:128], s_loc[:, 0:8], ident32[:]), "PE")
            M["slT"] = cnt["PE"]

        @VE
        def _(v):
            W(v, "PE", M["slT"])
            inc(v.tensor_copy(sc_row[:], tp3f[0:8, 0:128]), "VEC")
            M["srow"] = cnt["VEC"]

        @SY
        def _(q):
            W(q, "VEC", M["srow"])
            dq_pre(q, "SYNQ")
            M["scs"] = dq_post(q.dma_start(out=sc_in, in_=sc_row[:]), "SYNQ")

        @GE
        def _(g):
            Wd(g, M["scs"])
            inc(g.collective_compute("AllGather", ALU.bypass, replica_groups=RG,
                                     ins=[sc_in], outs=[sc_out]), "CC")
            M["agsc"] = cnt["CC"]


        @SY
        def _(q):
            W(q, "CC", M["agsc"])
            dq_pre(q, "SYNQ")
            M["sclf"] = dq_post(q.dma_start(out=sc_flat[:], in_=sc_out.rearrange("a b -> (a b)").rearrange("(o n) -> o n", o=1)), "SYNQ")

        @GE
        def _(g):
            inc(g.iota(pv_i[:], pattern=[[0, 1]], base=1, channel_multiplier=1), "POOL")
            M["pvi"] = cnt["POOL"]

        @VE
        def _(v):
            W(v, "POOL", M["pvi"])
            inc(v.tensor_copy(pvec1[:], pv_i[:]), "VEC")
            v.memset(lo_[:], -128.0)
            inc(v.memset(d64_[:], 4.0), "VEC")
            M["bisinit"] = cnt["VEC"]

        for i in range(16):
            @TE
            def _(t, i=i):
                if i == 0:
                    Wd(t, M["sclf"])
                    W(t, "VEC", M["srow"])
                else:
                    W(t, "VEC", M[f"srepc{i - 1}"])
                inc(t.matmul(tp3f[0:64, :], ones_f[:, 0:64],
                             sc_flat[:, i * 512:(i + 1) * 512], start=True, stop=True), "PE")
                M[f"srep{i}"] = cnt["PE"]

            @VE
            def _(v, i=i):
                W(v, "PE", M[f"srep{i}"])
                inc(v.tensor_copy(srep[:, i * 512:(i + 1) * 512], tp3f[0:64, :]), "VEC")
                M[f"srepc{i}"] = cnt["VEC"]
                if i == 15:
                    M["srep"] = cnt["VEC"]

        for it in range(6):
            @TE
            def _(t, it=it):
                W(t, "VEC", M["bisinit"] if it == 0 else M[f"b64u{it - 1}"])
                t.matmul(ps4b[:, 0:1], ones_f[:, 0:64], d64_[:], start=True, stop=True)
                inc(t.matmul(ps4b[:, 1:2], ones_f[:, 0:64], lo_[:], start=True, stop=True), "PE")
                M[f"b64b{it}"] = cnt["PE"]

            @VE
            def _(v, it=it):
                W(v, "PE", M[f"b64b{it}"])
                if it == 0:
                    W(v, "VEC", M["srep"])
                inc(v.tensor_scalar(tcol1[:], pvec1[:], ps4b[:, 0:1], None, op0=ALU.mult), "VEC")
                W(v, "VEC", cnt["VEC"])
                inc(v.tensor_scalar(thr_col[:], tcol1[:], ps4b[:, 1:2], None, op0=ALU.add), "VEC")
                W(v, "VEC", cnt["VEC"])
                inc(v.tensor_scalar(bjunk[:], srep[:], thr_col[:], None,
                                    op0=ALU.is_ge, op1=ALU.add, accum_out=rowcnt[:]), "VEC")
                W(v, "VEC", cnt["VEC"])
                inc(v.tensor_scalar(rowcnt[:], rowcnt[:], 4096.0, None, op0=ALU.is_ge), "VEC")
                M[f"b64c{it}"] = cnt["VEC"]

            @TE
            def _(t, it=it):
                W(t, "VEC", M[f"b64c{it}"])
                inc(t.matmul(cnt_ps[:], rowcnt[:], ones64[:], start=True, stop=True), "PE")
                M[f"b64s{it}"] = cnt["PE"]

            @VE
            def _(v, it=it):
                W(v, "PE", M[f"b64s{it}"])
                inc(v.scalar_tensor_tensor(lo_[:], cnt_ps[:], d64_[:], lo_[:],
                                           op0=ALU.mult, op1=ALU.add), "VEC")
                W(v, "VEC", cnt["VEC"])
                inc(v.tensor_scalar(d64_[:], d64_[:], 1.0 / 64.0, None, op0=ALU.mult), "VEC")
                M[f"b64u{it}"] = cnt["VEC"]
        M[f"bi{BISECT - 1}"] = M["b64u5"]

        _scope_switch('P4idx')
        pos_i = salloc("pos_i", [128, NS], F32)
        posu = salloc("posu", [128, NS], U32)
        rc128 = salloc("rc128", [128, 1], F32)
        prow = salloc("prow", [1, 128], F32)
        poff = salloc("poff", [128, 1], F32)
        zero1 = salloc("zero1", [1, 1], F32)
        trow = salloc("trow", [128, 8 * NS], F32)
        tcol = salloc("tcol", [128, 8 * NS], U32)
        bigt = salloc("bigt", [128, NS], F32)
        gf32b = salloc("gf32b", [128, 5 * NS], F32)

        @TE
        def _(t):
            W(t, "VEC", M[f"bi{BISECT - 1}"])
            inc(t.matmul(th128[:], ones_f[:], lo_[:], start=True, stop=True), "PE")
            M["th128"] = cnt["PE"]

        @VE
        def _(v):
            W(v, "PE", M["th128"])
            inc(v.tensor_scalar(m_loc[:], s_loc[:], th128[:], None, op0=ALU.is_ge), "VEC")
            W(v, "VEC", cnt["VEC"])
            inc(v.tensor_reduce(rc128[:], m_loc[:], axis=mybir.AxisListType.X, op=ALU.add), "VEC")
            M["rc"] = cnt["VEC"]

        @TE
        def _(t):
            W(t, "VEC", M["rc"])
            inc(t.transpose(tp3f[0:1, 0:128], rc128[:], ident32[:]), "PE")
            M["rcT"] = cnt["PE"]

        @VE
        def _(v):
            W(v, "PE", M["rcT"])
            v.memset(zero1[:], 0.0)
            inc(v.tensor_copy(prow[:], tp3f[0:1, 0:128]), "VEC")
            W(v, "VEC", cnt["VEC"])
            inc(v.tensor_tensor_scan(prow[:], prow[:], prow[:], zero1[:],
                                     op0=ALU.add, op1=ALU.bypass), "VEC")
            M["scan"] = cnt["VEC"]

        @TE
        def _(t):
            W(t, "VEC", M["scan"])
            inc(t.matmul(tp3f[:, 0:1], prow[:], ones_f[:, 0:1], start=True, stop=True), "PE")
            M["scanT"] = cnt["PE"]

        @VE
        def _(v):
            W(v, "PE", M["scanT"])
            inc(v.tensor_copy(poff[:], tp3f[:, 0:1]), "VEC")
            W(v, "VEC", cnt["VEC"])
            inc(v.tensor_tensor(poff[:], poff[:], rc128[:], op=ALU.subtract), "VEC")
            W(v, "VEC", cnt["VEC"])
            inc(v.tensor_tensor_scan(pos_i[:], m_loc[:], m_loc[:], zerop[:],
                                     op0=ALU.add, op1=ALU.bypass), "VEC")
            W(v, "VEC", cnt["VEC"])
            inc(v.scalar_tensor_tensor(pos_i[:], pos_i[:], poff[:], m_loc[:],
                                       op0=ALU.add, op1=ALU.subtract), "VEC")
            W(v, "VEC", cnt["VEC"])
            inc(v.tensor_scalar(bigt[:, 0:NS], m_loc[:], -1.0e9, 1.0e9,
                                op0=ALU.mult, op1=ALU.add), "VEC")
            W(v, "VEC", cnt["VEC"])
            inc(v.tensor_tensor(pos_i[:], pos_i[:], bigt[:, 0:NS], op=ALU.add), "VEC")
            W(v, "VEC", cnt["VEC"])
            inc(v.tensor_copy(posu[:], pos_i[:]), "VEC")
            M["posu"] = cnt["VEC"]

        @GE
        def _(g):
            inc(g.iota(tcol[:], pattern=[[128, NS], [0, 8]], base=0, channel_multiplier=1), "POOL")
            M["iot8"] = cnt["POOL"]

        @VE
        def _(v):
            W(v, "POOL", M["iot8"])
            inc(v.tensor_copy(trow[:], tcol[:]), "VEC")
            inc(v.memset(bigt[:, 0:NS], 2.0e9), "VEC")
            M["trow8"] = cnt["VEC"]

        @GE
        def _(g):
            W(g, "VEC", M["trow8"])
            for s in range(NS):
                dq_pre(g, "POOLQ")
                dq_post(g.dma_start(out=gbuf[s * 128:(s + 1) * 128, 0:NS], in_=bigt[:, 0:NS]), "POOLQ")
            M["gfill"] = snapq("POOLQ")

        for s in range(NS):
            @GE
            def _(g, s=s):
                if s == 0:
                    Wd(g, M["gfill"])
                    W(g, "VEC", M["posu"])
                dq_pre(g, "POOLQ")
                M[f"gsc{s}"] = dq_post(g.indirect_dma_start(
                    out=gbuf[:, 0:NS], out_offset=IndirectOffsetOnAxis(ap=posu[:, s:s + 1], axis=0),
                    in_=trow[:, 8 * s:8 * s + NS], in_offset=None,
                    bounds_check=LT - 1, oob_is_err=False), "POOLQ")

        M["gscall"] = snapq("POOLQ")

        @SY
        def _(q):
            Wd(q, M["gscall"])
            dq_pre(q, "SYNQ")
            M["gld"] = dq_post(q.dma_start(
                out=gf32b[:].rearrange("p (i f) -> p i f", f=NS),
                in_=gbuf[0:CAP, :].rearrange("(i p) f -> p i f", p=128)), "SYNQ")

        @VE
        def _(v):
            Wd(v, M["gld"])
            inc(v.tensor_copy(gf32[:], gf32b[:].rearrange("p (i f) -> p i f", f=NS)[:, :, 0]), "VEC")
            W(v, "VEC", cnt["VEC"])
            inc(v.tensor_copy(gidx[:], gf32[:]), "VEC")
            M["gidx"] = cnt["VEC"]

        if taps:
            @GE
            def _(g):
                if "x1" in taps:
                    Wd(g, M["x1d"])
                    dq_pre(g, "POOLQ")
                    dq_post(g.dma_start(out=tap_out["x1"], in_=x1_dram), "POOLQ")
                if "thr" in taps:
                    W(g, "VEC", M[f"bi{BISECT - 1}"])
                    dq_pre(g, "POOLQ")
                    dq_post(g.dma_start(out=tap_out["thr"], in_=lo_[:]), "POOLQ")
                if "mask" in taps:
                    W(g, "VEC", M["posu"])
                    dq_pre(g, "POOLQ")
                    dq_post(g.dma_start(out=tap_out["mask"], in_=m_loc[:]), "POOLQ")
                if "gidx" in taps:
                    W(g, "VEC", M["gidx"])
                    dq_pre(g, "POOLQ")
                    dq_post(g.dma_start(out=tap_out["gidx"], in_=gf32[:]), "POOLQ")
                if "ynT" in taps:
                    W(g, "VEC", M["attn"])
                    dq_pre(g, "POOLQ")
                    dq_post(g.dma_start(out=tap_out["ynT"], in_=ynT[:]), "POOLQ")
                M["tapd"] = snapq("POOLQ")

        _scope_switch('P5gather')
        free("gf32b", "bigt", "tcol", "trow", "zero1", "poff", "prow", "rc128", "posu",
             "pos_i", "thr_col", "tcol1", "pvec1", "pv_i", "d64_", "lo_",
             "rowcnt", "bjunk", "srep", "sc_flat", "sc_row", "sjunk", "br_bc", "wr_bc")
        free("cnt_ps", "th128", "ps4b", "tp3f", "oPs")
        free("bo_b", "Wo_sb", "ynT", "qTa")

        xselb = [salloc(f"xselb{i}", [128, C], BF16) for i in range(5)]
        b2_b = salloc("b2_b", [1, C], BF16)
        hT = salloc("hT", [128, 64 * CAP], BF16)
        b1_b = salloc("b1_b", [1, DF], BF16)
        xsT = salloc("xsT", [128, 16 * CAP], BF16)
        w1b = [salloc(f"w1b{i}", [128, 16 * 256], BF16) for i in range(3)]
        tp5 = [palloc(f"tp5{i}", [128, 512], BF16) for i in range(2)]
        psU = [palloc(f"psU{i}", [128, CAP]) for i in range(2)]

        @GE
        def _(g):
            W(g, "VEC", M["gidx"])
            Wd(g, snapq("POOLQ"))
            dq_pre(g, "POOLQ")
            dq_post(g.dma_start(out=b1_b[:], in_=b1), "POOLQ")
            dq_pre(g, "POOLQ")
            dq_post(g.dma_start(out=b2_b[:], in_=b2), "POOLQ")
            for i in range(3):
                dq_pre(g, "POOLQ")
                M[f"w1l{i}"] = dq_post(g.dma_start(
                    out=w1b[i][:].rearrange("p (c n) -> p c n", c=16),
                    in_=w1[:, i * 256:(i + 1) * 256].rearrange("(c p) n -> p c n", p=128)), "POOLQ")

        gat = {}
        for i in range(5):
            @VE
            def _(v, i=i):
                if i == 0:
                    W(v, "VEC", M["gidx"])
                    Wd(v, M.get("tapd", 0))
                gat[("z", i)] = inc(v.memset(xselb[i][:], 0.0), "VEC")

            @GE
            def _(g, i=i):
                if i == 0:
                    Wd(g, M["x1d"])
                W(g, "VEC", gat[("z", i)])
                dq_pre(g, "POOLQ")
                gat[("g", i)] = dq_post(g.indirect_dma_start(
                    out=xselb[i][:], out_offset=None,
                    in_=x1_dram, in_offset=IndirectOffsetOnAxis(ap=gidx[:, i:i + 1], axis=0),
                    bounds_check=LT - 1, oob_is_err=False), "POOLQ")

        for i in range(5):
            for b in range(4):
                @TE
                def _(t, i=i, b=b):
                    Wd(t, gat[("g", i)])
                    W(t, "VEC", gat.get(("e", i, b - 2), gat.get(("e", i - 1, b + 2), 0)))
                    for k in range(4):
                        mm = t.transpose(tp5[b % 2][:, k * 128:(k + 1) * 128],
                                         xselb[i][:, (4 * b + k) * 128:(4 * b + k + 1) * 128],
                                         identb[:])
                    inc(mm, "PE")
                    gat[("t", i, b)] = cnt["PE"]

                @VE
                def _(v, i=i, b=b):
                    W(v, "PE", gat[("t", i, b)])
                    mm = v.tensor_copy(
                        xsT[:].rearrange("p (fc t) -> p fc t", fc=16)[:, 4 * b:4 * b + 4,
                                                                     i * 128:(i + 1) * 128],
                        tp5[b % 2][:].rearrange("p (fc t) -> p fc t", fc=4))
                    gat[("e", i, b)] = inc(mm, "VEC")
        M["xsT"] = cnt["VEC"]

        _scope_switch('lnout')
        ln = {}
        A0_base = cnt["ACT"]
        for s in range(NS):
            ln[("rs", s)] = A0_base + 2 * (s + 1)
        for s in range(NS):
            @SY
            def _(q, s=s):
                if s == 0:
                    Wd(q, M["x1d"])
                W(q, "VEC", ln.get(("ub", s - 1), 0))
                Wd(q, ln.get(("st", s - 1), 0))
                dq_pre(q, "SYNQ")
                ln[("xr", s)] = dq_post(q.dma_start(out=v1t[:], in_=x1_dram[s * 128:(s + 1) * 128, :]), "SYNQ")

            @VE
            def _(v, s=s):
                Wd(v, ln[("xr", s)])
                for i in range(4):
                    mm = v.bn_stats(bnst[:, i * 6:(i + 1) * 6], v1t[:, i * 512:(i + 1) * 512])
                inc(mm, "VEC")
                W(v, "VEC", cnt["VEC"])
                inc(v.bn_aggr(mv[:], bnst[:]), "VEC")
                ln[("stat", s)] = cnt["VEC"]
                ln[("ub", s)] = cnt["VEC"]


            @VE
            def _(v, s=s):
                W(v, "ACT", ln[("rs", s)])
                inc(v.tensor_tensor(lnc2[:], mv[:, 0:1], lnc1[:], op=ALU.mult), "VEC")
                W(v, "VEC", cnt["VEC"])
                inc(v.tensor_scalar(lnc2[:], lnc2[:], -1.0, None, op0=ALU.mult), "VEC")
                W(v, "VEC", cnt["VEC"])
                inc(v.tensor_scalar(v1t[:], v1t[:], lnc1[:], lnc2[:],
                                    op0=ALU.mult, op1=ALU.add), "VEC")
                ln[("y", s)] = cnt["VEC"]

            @SY
            def _(q, s=s):
                W(q, "VEC", ln[("y", s)])
                dq_pre(q, "SYNQ")
                ln[("st", s)] = dq_post(q.dma_start(out=out[s * 128:(s + 1) * 128, :], in_=v1t[:]), "SYNQ")

        @SE
        def _(sc):
            for s in range(NS):
                W(sc, "VEC", ln[("stat", s)])
                inc(sc.activation(lnc1[:], mv[:, 1:2], AF.Ln, bias=f32(EPS), scale=f32(1.0)), "ACT")
                W(sc, "ACT", cnt["ACT"])
                got = inc(sc.activation(lnc1[:], lnc1[:], AF.Exp, bias=f32(0.0), scale=f32(-0.5)), "ACT")
                assert got == ln[("rs", s)], (s, got)

        M["base_out"] = snapq("SYNQ")


        _scope_switch('mm1')
        mm1 = {}
        A0_gel = cnt["ACT"]
        for g_ in range(64):
            mm1[("gel", g_)] = A0_gel + g_ + 1
        for ds in range(32):
            @TE
            def _(t, ds=ds):
                if ds == 0:
                    W(t, "VEC", M["xsT"])
                Wd(t, M[f"w1l{ds}"] if ds < 3 else mm1[("w1l", ds)])
                for dfc in range(2):
                    g = 2 * ds + dfc
                    W(t, "ACT", mm1.get(("gel", g - 2), 0))
                    for fc in range(16):
                        lhs = w1b[ds % 3][:, fc * 256 + dfc * 128: fc * 256 + (dfc + 1) * 128]
                        t.matmul(psU[g % 2][:, 0:512], lhs, xsT[:, fc * CAP: fc * CAP + 512],
                                 start=(fc == 0), stop=False)
                        t.matmul(psU[g % 2][:, 512:CAP], lhs,
                                 xsT[:, fc * CAP + 512: fc * CAP + CAP],
                                 start=(fc == 0), stop=False)
                    boff = ds * 256 + dfc * 128
                    t.matmul(psU[g % 2][:, 0:512], b1_b[:, boff:boff + 128],
                             ones_bf[:, 0:512], start=False, stop=True)
                    mm = t.matmul(psU[g % 2][:, 512:CAP], b1_b[:, boff:boff + 128],
                                  ones_bf[:, 0:128], start=False, stop=True)
                    mm1[("u", g)] = inc(mm, "PE")

            if ds + 3 < 32:
                @GE
                def _(g, ds=ds):
                    W(g, "PE", mm1[("u", 2 * ds + 1)])
                    dq_pre(g, "POOLQ")
                    mm = g.dma_start(
                        out=w1b[ds % 3][:].rearrange("p (c n) -> p c n", c=16),
                        in_=w1[:, (ds + 3) * 256:(ds + 4) * 256].rearrange("(c p) n -> p c n", p=128))
                    mm1[("w1l", ds + 3)] = dq_post(mm, "POOLQ")

        @SE
        def _(sc):
            for g in range(64):
                W(sc, "PE", mm1[("u", g)])
                got = inc(sc.activation(
                    hT[:, g * CAP:(g + 1) * CAP], psU[g % 2][:], AF.Gelu_apprx_tanh), "ACT")
                assert got == mm1[("gel", g)], (g, got)

        M["gelu"] = cnt["ACT"]

        _scope_switch('mm2')
        free("w1b2", "w1b1", "w1b0", "xsT", "b1_b")
        free("psU1", "psU0", "tp51", "tp50")
        w2cb = [salloc(f"w2cb{i}", [128, 64 * 128], BF16) for i in range(2)]
        psO2 = [palloc(f"psO2{i}", [128, 128]) for i in range(4)]

        mm2 = {}
        for nb in range(2):
            @GE
            def _(g, nb=nb):
                W(g, "ACT", M["gelu"] if nb == 0 else 0)
                dq_pre(g, "POOLQ")
                mm = g.dma_start(
                    out=w2cb[nb % 2][:].rearrange("p (c n) -> p c n", c=64),
                    in_=w2[:, nb * 128:(nb + 1) * 128].rearrange("(c p) n -> p c n", p=128))
                mm2[("w2l", nb)] = dq_post(mm, "POOLQ")

        for nb in range(16):
            for sl in range(5):
                @TE
                def _(t, nb=nb, sl=sl):
                    g = nb * 5 + sl
                    if sl == 0:
                        Wd(t, mm2[("w2l", nb)])
                    if g >= 4:
                        W(t, "VEC", mm2[("ev", g - 4)])
                    for dfc in range(64):
                        t.matmul(psO2[g % 4][:],
                                 hT[:, dfc * CAP + sl * 128: dfc * CAP + (sl + 1) * 128],
                                 w2cb[nb % 2][:, dfc * 128:(dfc + 1) * 128],
                                 start=(dfc == 0), stop=False)
                    mm = t.matmul(psO2[g % 4][:], ones_bf[0:1, 0:128],
                                  b2_b[:, nb * 128:(nb + 1) * 128], start=False, stop=True)
                    mm2[("mm", g)] = inc(mm, "PE")
                    if sl == 4:
                        mm2[("mmn", nb)] = cnt["PE"]

                @VE
                def _(v, nb=nb, sl=sl):
                    g = nb * 5 + sl
                    W(v, "PE", mm2[("mm", g)])
                    mm2[("ev", g)] = inc(v.tensor_tensor(
                        xselb[sl][:, nb * 128:(nb + 1) * 128], psO2[g % 4][:],
                        xselb[sl][:, nb * 128:(nb + 1) * 128], op=ALU.add), "VEC")

            if nb + 2 < 16:
                @GE
                def _(g, nb=nb):
                    W(g, "PE", mm2[("mmn", nb)])
                    dq_pre(g, "POOLQ")
                    mm = g.dma_start(
                        out=w2cb[nb % 2][:].rearrange("p (c n) -> p c n", c=64),
                        in_=w2[:, (nb + 2) * 128:(nb + 3) * 128].rearrange("(c p) n -> p c n", p=128))
                    mm2[("w2l", nb + 2)] = dq_post(mm, "POOLQ")

        M["mm2e"] = cnt["VEC"]

        sel = {}
        A0_sel = cnt["ACT"]
        for sl in range(5):
            sel[("rs", sl)] = A0_sel + 2 * (sl + 1)
        for sl in range(5):
            @VE
            def _(v, sl=sl):
                if sl == 0:
                    W(v, "VEC", M["mm2e"])
                Wd(v, sel.get(("sc", sl - 1), 0))
                for i in range(4):
                    mm = v.bn_stats(bnst[:, i * 6:(i + 1) * 6], xselb[sl][:, i * 512:(i + 1) * 512])
                inc(mm, "VEC")
                W(v, "VEC", cnt["VEC"])
                inc(v.bn_aggr(mv[:], bnst[:]), "VEC")
                sel[("stat", sl)] = cnt["VEC"]


            @VE
            def _(v, sl=sl):
                W(v, "ACT", sel[("rs", sl)])
                inc(v.tensor_tensor(lnc2[:], mv[:, 0:1], lnc1[:], op=ALU.mult), "VEC")
                W(v, "VEC", cnt["VEC"])
                inc(v.tensor_scalar(lnc2[:], lnc2[:], -1.0, None, op0=ALU.mult), "VEC")
                W(v, "VEC", cnt["VEC"])
                inc(v.tensor_scalar(x1t[:], xselb[sl][:], lnc1[:], lnc2[:],
                                    op0=ALU.mult, op1=ALU.add), "VEC")
                sel[("y", sl)] = cnt["VEC"]

            @GE
            def _(g, sl=sl):
                if sl == 0:
                    Wd(g, M["base_out"])
                W(g, "VEC", sel[("y", sl)])
                dq_pre(g, "POOLQ")
                sel[("sc", sl)] = dq_post(g.indirect_dma_start(
                    out=out, out_offset=IndirectOffsetOnAxis(ap=gidx[:, sl:sl + 1], axis=0),
                    in_=x1t[:], in_offset=None,
                    bounds_check=LT - 1, oob_is_err=False), "POOLQ")

        if _scope_state["cm"] is not None:
            _scope_state["cm"].__exit__(None, None, None)
            _scope_state["cm"] = None

        @SE
        def _(sc):
            for sl in range(5):
                W(sc, "VEC", sel[("stat", sl)])
                inc(sc.activation(lnc1[:], mv[:, 1:2], AF.Ln, bias=f32(EPS), scale=f32(1.0)), "ACT")
                W(sc, "ACT", cnt["ACT"])
                got = inc(sc.activation(lnc1[:], lnc1[:], AF.Exp, bias=f32(0.0), scale=f32(-0.5)), "ACT")
                assert got == sel[("rs", sl)], (sl, got)

        @GE
        def _(g):
            Wd(g, snapq("SYNQ"))
            Wd(g, snapq("POOLQ"))
            Wd(g, snapq("SCLQ"))
            W(g, "PE", cnt["PE"])
            W(g, "ACT", cnt["ACT"])
            W(g, "VEC", cnt["VEC"])
            W(g, "POOL", cnt["POOL"])

    for n in reversed(list(opens)):
        opens.pop(n).__exit__(None, None, None)
    return nc



def _host_consts(c):
    tiles = [q_tile(c, s) for s in range(NS)]
    rows = np.concatenate([np.arange(g * 128, (g + 1) * 128) for g in tiles])
    inv = 10000.0 ** (-(np.arange(64, dtype=np.float64)) / 64.0)
    ang = rows[:, None].astype(np.float64) * inv[None, :]
    cosr = np.tile(np.cos(ang).astype(np.float32), (1, 4))
    sinr = np.tile(np.sin(ang).astype(np.float32), (1, 4))
    masks = np.zeros((128, NS * 1024), dtype=ml_dtypes.bfloat16)
    for s in range(NS):
        pi = (s + c) % NC
        kk = np.arange(128)[:, None]
        jq = np.arange(1024)[None, :]
        jj, ql = jq // 128, jq % 128
        allowed = (jj * 128 + kk) <= (pi * 128 + ql)
        masks[:, s * 1024:(s + 1) * 1024] = allowed.astype(ml_dtypes.bfloat16)
    return rows, cosr, sinr, masks


def kernel(L0_attn_x, L0_attn_Wq, L0_attn_bq, L0_attn_Wk, L0_attn_bk,
           L0_attn_Wv, L0_attn_bv, L0_attn_Wo, L0_attn_bo,
           L0_ffn_w_r, L0_ffn_b_r, L0_ffn_w1, L0_ffn_b1, L0_ffn_w2, L0_ffn_b2):
    x = np.ascontiguousarray(np.asarray(L0_attn_x, dtype=np.float32).reshape(T, C))
    com = dict(
        Wq=np.ascontiguousarray(np.asarray(L0_attn_Wq, np.float32)),
        bq=np.asarray(L0_attn_bq, np.float32).reshape(1, C),
        Wk=np.ascontiguousarray(np.asarray(L0_attn_Wk, np.float32)),
        bk=np.asarray(L0_attn_bk, np.float32).reshape(1, DKV),
        Wv=np.ascontiguousarray(np.asarray(L0_attn_Wv, np.float32)),
        bv=np.asarray(L0_attn_bv, np.float32).reshape(1, DKV),
        Wo=np.ascontiguousarray(np.asarray(L0_attn_Wo, np.float32)),
        bo=np.asarray(L0_attn_bo, np.float32).reshape(1, C),
        w1=np.ascontiguousarray(np.asarray(L0_ffn_w1, np.float32)),
        b1=np.asarray(L0_ffn_b1, np.float32).reshape(1, DF),
        w2=np.ascontiguousarray(np.asarray(L0_ffn_w2, np.float32)),
        b2=np.asarray(L0_ffn_b2, np.float32).reshape(1, C),
        wr=np.asarray(L0_ffn_w_r, np.float32).reshape(1, C),
        br=np.asarray(L0_ffn_b_r, np.float32).reshape(1, 1),
        ident=np.eye(128, dtype=ml_dtypes.bfloat16),
    )
    in_maps = []
    row_lists = []
    for c in range(NC):
        rows, cosr, sinr, masks = _host_consts(c)
        row_lists.append(rows)
        m = dict(com)
        m["x_loc"] = np.ascontiguousarray(x[rows])
        m["cosT"] = np.ascontiguousarray(cosr)
        m["sinT"] = np.ascontiguousarray(sinr)
        m["masks"] = np.ascontiguousarray(masks)
        in_maps.append(m)

    nc_prog = build_program(taps=_TAPS)
    res = run_bass_kernel_spmd(nc_prog, in_maps, core_ids=list(range(NC)))

    full = np.empty((T, C), dtype=np.float32)
    for c in range(NC):
        full[row_lists[c]] = res.results[c]["out"]
    kernel._last_results = res
    return full.reshape(1, T, C)


# version 05 (final)
# speedup vs baseline: 1.1418x; 1.0252x over previous
import numpy as np
import ml_dtypes

import concourse.bass as bass
import concourse.mybir as mybir
from concourse.bass import IndirectOffsetOnAxis
from concourse.bass_utils import run_bass_kernel_spmd

F32 = mybir.dt.float32
BF16 = mybir.dt.bfloat16
U32 = mybir.dt.uint32
AF = mybir.ActivationFunctionType
ALU = mybir.AluOpType
f32 = np.float32

T, C, NKV, DH, DKV, DF = 8192, 2048, 4, 128, 512, 8192
NC, NS = 8, 8
LT = NS * 128
CAP = 640
EPS = 1e-5
SHIFT = 8.0
INVSQD = float(1.0 / np.sqrt(128.0))
BISECT = 48
RG = [list(range(NC))]

_TAPS = ()


def q_tile(c, s):
    return 8 * s + ((s + c) % NC)


def kchunk_owner(j):
    s = j // 8
    return (j % 8 - s) % NC, s


def build_program(taps=()):
    nc = bass.Bass("TRN2", target_bir_lowering=False)

    dI = lambda n, s, d=F32: nc.dram_tensor(n, s, d, kind="ExternalInput").ap()
    x_loc = dI("x_loc", [LT, C])
    Wq = dI("Wq", [C, C]); bq = dI("bq", [1, C])
    Wk = dI("Wk", [C, DKV]); bk = dI("bk", [1, DKV])
    Wv = dI("Wv", [C, DKV]); bv = dI("bv", [1, DKV])
    Wo = dI("Wo", [DKV, C]); bo = dI("bo", [1, C])
    w1 = dI("w1", [C, DF]); b1 = dI("b1", [1, DF])
    w2 = dI("w2", [DF, C]); b2 = dI("b2", [1, C])
    wr = dI("wr", [1, C]); br = dI("br", [1, 1])
    cosT = dI("cosT", [LT, 256])
    sinT = dI("sinT", [LT, 256])
    masksI = dI("masks", [128, NS * 1024], BF16)
    identI = dI("ident", [128, 128], BF16)

    out = nc.dram_tensor("out", [LT, C], F32, kind="ExternalOutput").ap()
    tap_shapes = {"x1": [LT, C], "thr": [1, 1], "mask": [128, NS],
                  "scores": [64, 128], "gidx": [128, 5], "ynT": [128, NKV * 1024]}
    tap_out = {t: nc.dram_tensor("tap_" + t, tap_shapes[t], F32, kind="ExternalOutput").ap()
               for t in taps}

    k_in = nc.dram_tensor("k_in", [DKV, LT], BF16).ap()
    k_out = nc.dram_tensor("k_out", [NC * DKV, LT], BF16, addr_space="Shared").ap()
    v_in = nc.dram_tensor("v_in", [LT, DKV], BF16).ap()
    v_out = nc.dram_tensor("v_out", [NC * LT, DKV], BF16, addr_space="Shared").ap()
    sc_in = nc.dram_tensor("sc_in", [NS, 128], F32).ap()
    sc_out = nc.dram_tensor("sc_out", [NC * NS, 128], F32, addr_space="Shared").ap()
    x1_dram = nc.dram_tensor("x1_dram", [LT, C], F32).ap()
    gbuf = nc.dram_tensor("gbuf", [LT, NS], F32).ap()

    sems = {}
    cnt = {}
    opens = {}

    def semalloc(k):
        cm = nc.semaphore("sem_" + k)
        opens["sem_" + k] = cm
        sems[k] = cm.__enter__()
        cnt[k] = 0

    for k in ("PE", "VEC", "ACT", "POOL", "CC"):
        semalloc(k)

    def inc(instr, k, n=1):
        instr.then_inc(sems[k], n)
        cnt[k] += n
        return cnt[k]

    dqs = {}
    for qn, dep in (("SYNQ", 4), ("POOLQ", 4), ("SCLQ", 4)):
        slots = []
        for i in range(dep):
            cm = nc.semaphore(f"q_{qn}{i}")
            opens[f"q_{qn}{i}"] = cm
            slots.append(cm.__enter__())
        dqs[qn] = {"sems": slots, "val": [0] * dep, "n": 0, "d": dep}

    def dq_pre(eng, qn):
        q = dqs[qn]
        i = q["n"] % q["d"]
        if q["val"][i] > 0:
            eng.wait_ge(q["sems"][i], q["val"][i])

    def dq_post(instr, qn):
        q = dqs[qn]
        i = q["n"] % q["d"]
        q["val"][i] += 16
        instr.then_inc(q["sems"][i], 16)
        q["n"] += 1
        return (qn, i, q["val"][i])

    def snapq(qn):
        q = dqs[qn]
        return [(qn, i, q["val"][i]) for i in range(q["d"]) if q["val"][i] > 0]

    def salloc(n, s, d):
        cm = nc.sbuf_tensor(n, s, d)
        opens[n] = cm
        return cm.__enter__()

    def palloc(n, s, d=F32):
        cm = nc.psum_tensor(n, s, d)
        opens[n] = cm
        return cm.__enter__()

    def free(*names):
        for n in names:
            opens.pop(n).__exit__(None, None, None)

    identb = salloc("identb", [128, 128], BF16)
    ident32 = salloc("ident32", [128, 128], F32)
    ones_bf = salloc("ones_bf", [1, 512], BF16)
    ones_f = salloc("ones_f", [1, 128], F32)
    ones_c = salloc("ones_c", [128, 1], BF16)
    ones64 = salloc("ones64", [64, 1], F32)
    zerop = salloc("zerop", [128, 1], F32)
    bq_eff = salloc("bq_eff", [1, DKV], BF16)
    bk_b = salloc("bk_b", [1, DKV], BF16)
    bv_b = salloc("bv_b", [1, DKV], BF16)
    s_loc = salloc("s_loc", [128, NS], F32)
    m_loc = salloc("m_loc", [128, NS], F32)
    bnst = salloc("bnst", [128, 24], F32)
    mv = salloc("mv", [128, 2], F32)
    lnc1 = salloc("lnc1", [128, 1], F32)
    lnc2 = salloc("lnc2", [128, 1], F32)
    gidx = salloc("gidx", [128, 5], U32)
    gf32 = salloc("gf32", [128, 5], F32)
    x1t = salloc("x1t", [128, C], F32)
    v1t = salloc("v1t", [128, C], F32)

    qTa = salloc("qTa", [128, NKV * 1024], BF16)
    ynT = salloc("ynT", [128, NKV * 1024], BF16)
    Wo_sb = salloc("Wo_sb", [128, NKV * C], BF16)
    bo_b = salloc("bo_b", [1, C], BF16)

    xfm_all = salloc("xfm_all", [128, 16 * 1024], BF16)
    cos4 = salloc("cos4", [128, NS * 256], F32)
    sin4 = salloc("sin4", [128, NS * 256], F32)
    Wk_sb = salloc("Wk_sb", [128, 16 * 512], BF16)
    Wv_sb = salloc("Wv_sb", [128, 16 * 512], BF16)
    Wq_eff = salloc("Wq_eff", [128, 16 * 512], BF16)
    Wq_tmp = [salloc(f"Wq_tmp{i}", [128, C], BF16) for i in range(2)]
    xbuf = [salloc(f"xbuf{i}", [128, C], F32) for i in range(2)]
    xb16 = [salloc(f"xb16{i}", [128, C], BF16) for i in range(2)]
    qr = salloc("qr", [128, DKV], BF16)
    kr = salloc("kr", [128, DKV], BF16)
    vr = salloc("vr", [128, DKV], BF16)
    ktT = salloc("ktT", [128, NKV * 128], BF16)
    rt1 = salloc("rt1", [128, 256], F32)
    rt2 = salloc("rt2", [128, 256], F32)

    btmp = salloc("btmp", [1, C], F32)
    bt1 = salloc("bt1", [1, 512], F32)
    bt2 = salloc("bt2", [1, 512], F32)

    p1ps = [palloc(f"p1ps{i}", [128, 512]) for i in range(2)]
    tpps = [palloc(f"tpps{i}", [128, 512], BF16) for i in range(2)]

    M = {}

    _scope_state = {"cm": None}

    def _scope_switch(name):
        if _scope_state["cm"] is not None:
            _scope_state["cm"].__exit__(None, None, None)
        cm = nc.named_scope(name)
        cm.__enter__()
        _scope_state["cm"] = cm

    with nc.Block() as block:
        TE, VE, SE, GE, SY = block.tensor, block.vector, block.scalar, block.gpsimd, block.sync

        def W(eng, k, v):
            if v > 0:
                eng.wait_ge(sems[k], v)

        def Wd(eng, h):
            if isinstance(h, list):
                for hh in h:
                    Wd(eng, hh)
            elif isinstance(h, tuple):
                qn, i, v = h
                eng.wait_ge(dqs[qn]["sems"][i], v)

        def dma(eng, qn, instr_fn):
            dq_pre(eng, qn)
            return dq_post(instr_fn(), qn)

        @SY
        def _(q):
            dq_pre(q, "SYNQ")
            M["ident"] = dq_post(q.dma_start(out=identb[:], in_=identI), "SYNQ")
            dq_pre(q, "SYNQ")
            dq_post(q.dma_start(out=cos4[:].rearrange("p (s f) -> p s f", s=NS),
                               in_=cosT.rearrange("(s p) f -> p s f", p=128)), "SYNQ")
            dq_pre(q, "SYNQ")
            dq_post(q.dma_start(out=sin4[:].rearrange("p (s f) -> p s f", s=NS),
                               in_=sinT.rearrange("(s p) f -> p s f", p=128)), "SYNQ")
            dq_pre(q, "SYNQ")
            dq_post(q.dma_start(out=btmp[:], in_=bq), "SYNQ")
            M["p0"] = snapq("SYNQ")

        @GE
        def _(g):
            dq_pre(g, "POOLQ")
            dq_post(g.dma_start(out=Wk_sb[:].rearrange("p (c n) -> p c n", c=16),
                               in_=Wk.rearrange("(c p) n -> p c n", p=128)), "POOLQ")
            dq_pre(g, "POOLQ")
            dq_post(g.dma_start(out=Wv_sb[:].rearrange("p (c n) -> p c n", c=16),
                               in_=Wv.rearrange("(c p) n -> p c n", p=128)), "POOLQ")
            dq_pre(g, "POOLQ")
            dq_post(g.dma_start(out=bk_b[:], in_=bk), "POOLQ")
            dq_pre(g, "POOLQ")
            dq_post(g.dma_start(out=bv_b[:], in_=bv), "POOLQ")
            M["wkv"] = snapq("POOLQ")

        @VE
        def _(v):
            v.memset(ones_bf[:], 1.0)
            v.memset(ones_f[:], 1.0)
            v.memset(ones_c[:], 1.0)
            v.memset(ones64[:], 1.0)
            inc(v.memset(zerop[:], 0.0), "VEC")
            M["ones"] = cnt["VEC"]
            Wd(v, M["ident"])
            inc(v.tensor_copy(ident32[:], identb[:]), "VEC")
            Wd(v, M["p0"])
            bqv = btmp[:].rearrange("o (h j i) -> o h j i", h=4, j=4)
            v.tensor_tensor(bt1[:].rearrange("o (h i) -> o h i", h=4),
                            bqv[:, :, 0, :], bqv[:, :, 1, :], op=ALU.add)
            inc(v.tensor_tensor(bt2[:].rearrange("o (h i) -> o h i", h=4),
                                bqv[:, :, 2, :], bqv[:, :, 3, :], op=ALU.add), "VEC")
            W(v, "VEC", cnt["VEC"])
            inc(v.tensor_tensor(bq_eff[:], bt1[:], bt2[:], op=ALU.add), "VEC")
            M["bqe"] = cnt["VEC"]

        for i in range(2):
            @GE
            def _(g, i=i):
                dq_pre(g, "POOLQ")
                M[f"wql{i}"] = dq_post(g.dma_start(out=Wq_tmp[i % 2][:],
                                   in_=Wq[i * 128:(i + 1) * 128, :]), "POOLQ")
        free("bt2", "bt1", "btmp")

        def emit_wq(i):
            @VE
            def _(v, i=i):
                Wd(v, M[f"wql{i}"])
                wv = Wq_tmp[i % 2][:].rearrange("p (h j i) -> p h j i", h=4, j=4)
                eff = Wq_eff[:, i * 512:(i + 1) * 512].rearrange("p (h i) -> p h i", h=4)
                inc(v.tensor_tensor(eff, wv[:, :, 0, :], wv[:, :, 1, :], op=ALU.add), "VEC")
                W(v, "VEC", cnt["VEC"])
                inc(v.tensor_tensor(eff, eff, wv[:, :, 2, :], op=ALU.add), "VEC")
                W(v, "VEC", cnt["VEC"])
                inc(v.tensor_tensor(eff, eff, wv[:, :, 3, :], op=ALU.add), "VEC")
                M[f"wqe{i}"] = cnt["VEC"]

            if i + 2 < 16:
                @GE
                def _(g, i=i):
                    W(g, "VEC", M[f"wqe{i}"])
                    dq_pre(g, "POOLQ")
                    M[f"wql{i + 2}"] = dq_post(g.dma_start(out=Wq_tmp[i % 2][:],
                                       in_=Wq[(i + 2) * 128:(i + 3) * 128, :]), "POOLQ")

        _scope_switch('P1')
        for s in range(NS):
            @SY
            def _(q, s=s):
                W(q, "VEC", M.get(f"xc{s - 2}", 0))
                dq_pre(q, "SYNQ")
                M[f"xl{s}"] = dq_post(q.dma_start(out=xbuf[s % 2][:], in_=x_loc[s * 128:(s + 1) * 128, :]), "SYNQ")

            @VE
            def _(v, s=s):
                Wd(v, M[f"xl{s}"])
                W(v, "PE", M.get(f"xT{s - 2}", 0))
                inc(v.tensor_copy(xb16[s % 2][:], xbuf[s % 2][:]), "VEC")
                M[f"xc{s}"] = cnt["VEC"]

            for b in range(4):
                @TE
                def _(t, s=s, b=b):
                    W(t, "VEC", M[f"xc{s}"])
                    W(t, "VEC", M.get(f"xEb{s}_{b - 2}", M.get(f"xEb{s - 1}_{b + 2}", 0)))
                    for i in range(4):
                        mm = t.transpose(tpps[b % 2][:, i * 128:(i + 1) * 128],
                                         xb16[s % 2][:, (4 * b + i) * 128:(4 * b + i + 1) * 128],
                                         identb[:])
                    inc(mm, "PE")
                    M[f"xTb{s}_{b}"] = cnt["PE"]
                    if b == 3:
                        M[f"xT{s}"] = cnt["PE"]

                @VE
                def _(v, s=s, b=b):
                    W(v, "PE", M[f"xTb{s}_{b}"])
                    inc(v.tensor_copy(
                        xfm_all[:].rearrange("p (fc t) -> p fc t", fc=16)[:, 4 * b:4 * b + 4,
                                                                         s * 128:(s + 1) * 128],
                        tpps[b % 2][:].rearrange("p (fc t) -> p fc t", fc=4)), "VEC")
                    M[f"xEb{s}_{b}"] = cnt["VEC"]
                    if b == 3:
                        M[f"xE{s}"] = cnt["VEC"]

            @TE
            def _(t, s=s):
                if s == 0:
                    Wd(t, M["wkv"])
                    W(t, "VEC", M["ones"])
                W(t, "VEC", M[f"xE{s}"])
                W(t, "VEC", M.get(f"rpk{s - 1}", 0))
                for fc in range(16):
                    t.matmul(p1ps[0][:], xfm_all[:, fc * 1024 + s * 128: fc * 1024 + (s + 1) * 128],
                             Wk_sb[:, fc * 512:(fc + 1) * 512], start=(fc == 0), stop=False)
                inc(t.matmul(p1ps[0][:], ones_bf[:, 0:128], bk_b[:], start=False, stop=True), "PE")
                M[f"kp{s}"] = cnt["PE"]
                W(t, "VEC", M.get(f"vev{s - 1}", 0))
                for fc in range(16):
                    t.matmul(p1ps[1][:], xfm_all[:, fc * 1024 + s * 128: fc * 1024 + (s + 1) * 128],
                             Wv_sb[:, fc * 512:(fc + 1) * 512], start=(fc == 0), stop=False)
                inc(t.matmul(p1ps[1][:], ones_bf[:, 0:128], bv_b[:], start=False, stop=True), "PE")
                M[f"vp{s}"] = cnt["PE"]

            @VE
            def _(v, s=s):
                W(v, "PE", M[f"kp{s}"])
                W(v, "PE", M.get(f"ktT{s - 1}", 0))
                ke = p1ps[0][:, 0:512:2]
                ko = p1ps[0][:, 1:512:2]
                cs = cos4[:, s * 256:(s + 1) * 256]
                sn = sin4[:, s * 256:(s + 1) * 256]
                v.tensor_tensor(rt1[:], ke, cs, op=ALU.mult)
                inc(v.tensor_tensor(rt2[:], ko, sn, op=ALU.mult), "VEC")
                W(v, "VEC", cnt["VEC"])
                krv = kr[:].rearrange("p (h two i) -> p h two i", h=4, two=2)
                inc(v.tensor_tensor(krv[:, :, 0, :], rt1[:], rt2[:], op=ALU.subtract), "VEC")
                W(v, "VEC", cnt["VEC"])
                v.tensor_tensor(rt1[:], ke, sn, op=ALU.mult)
                inc(v.tensor_tensor(rt2[:], ko, cs, op=ALU.mult), "VEC")
                W(v, "VEC", cnt["VEC"])
                inc(v.tensor_tensor(krv[:, :, 1, :], rt1[:], rt2[:], op=ALU.add), "VEC")
                M[f"rpk{s}"] = cnt["VEC"]
                W(v, "PE", M[f"vp{s}"])
                Wd(v, M.get(f"vst{s - 1}", 0))
                inc(v.tensor_copy(vr[:], p1ps[1][:]), "VEC")
                M[f"vev{s}"] = cnt["VEC"]

            @TE
            def _(t, s=s):
                W(t, "VEC", M[f"rpk{s}"])
                W(t, "VEC", M.get(f"ktE{s - 1}", 0))
                for h in range(4):
                    mm = t.transpose(tpps[0][:, h * 128:(h + 1) * 128],
                                     kr[:, h * 128:(h + 1) * 128], identb[:])
                inc(mm, "PE")
                M[f"ktT{s}"] = cnt["PE"]

            @VE
            def _(v, s=s):
                W(v, "PE", M[f"ktT{s}"])
                Wd(v, M.get(f"ktst{s - 1}", 0))
                inc(v.tensor_copy(ktT[:], tpps[0][:]), "VEC")
                M[f"ktE{s}"] = cnt["VEC"]

            @SY
            def _(q, s=s):
                W(q, "VEC", M[f"ktE{s}"])
                dq_pre(q, "SYNQ")
                M[f"ktst{s}"] = dq_post(q.dma_start(
                    out=k_in.rearrange("(h d) t -> d h t", h=4)[:, :, s * 128:(s + 1) * 128],
                    in_=ktT[:].rearrange("d (h t) -> d h t", h=4)), "SYNQ")
                W(q, "VEC", M[f"vev{s}"])
                dq_pre(q, "SYNQ")
                M[f"vst{s}"] = dq_post(q.dma_start(out=v_in[s * 128:(s + 1) * 128, :], in_=vr[:]), "SYNQ")

            emit_wq(2 * s)
            emit_wq(2 * s + 1)

        M["wqeff"] = cnt["VEC"]
        M["kvst"] = snapq("SYNQ")

        @GE
        def _(g):
            Wd(g, M["kvst"])
            inc(g.collective_compute("AllGather", ALU.bypass, replica_groups=RG,
                                     ins=[k_in], outs=[k_out]), "CC")
            inc(g.collective_compute("AllGather", ALU.bypass, replica_groups=RG,
                                     ins=[v_in], outs=[v_out]), "CC")
            M["agkv"] = cnt["CC"]

        for s in range(NS):
            @TE
            def _(t, s=s):
                if s == 0:
                    W(t, "VEC", M["wqeff"])
                    W(t, "VEC", M["bqe"])
                if s <= 1:
                    W(t, "VEC", M[f"vev{NS - 1}"])
                W(t, "VEC", M.get(f"rpq{s - 2}", 0))
                for fc in range(16):
                    t.matmul(p1ps[s % 2][:], xfm_all[:, fc * 1024 + s * 128: fc * 1024 + (s + 1) * 128],
                             Wq_eff[:, fc * 512:(fc + 1) * 512], start=(fc == 0), stop=False)
                inc(t.matmul(p1ps[s % 2][:], ones_bf[:, 0:128], bq_eff[:], start=False, stop=True), "PE")
                M[f"qp{s}"] = cnt["PE"]

            @VE
            def _(v, s=s):
                W(v, "PE", M[f"qp{s}"])
                W(v, "PE", M.get(f"qT{s - 1}", 0))
                qe = p1ps[s % 2][:, 0:512:2]
                qo = p1ps[s % 2][:, 1:512:2]
                cs = cos4[:, s * 256:(s + 1) * 256]
                sn = sin4[:, s * 256:(s + 1) * 256]
                v.tensor_tensor(rt1[:], qe, cs, op=ALU.mult)
                inc(v.tensor_tensor(rt2[:], qo, sn, op=ALU.mult), "VEC")
                W(v, "VEC", cnt["VEC"])
                qrv = qr[:].rearrange("p (h two i) -> p h two i", h=4, two=2)
                inc(v.tensor_tensor(qrv[:, :, 0, :], rt1[:], rt2[:], op=ALU.subtract), "VEC")
                W(v, "VEC", cnt["VEC"])
                v.tensor_tensor(rt1[:], qe, sn, op=ALU.mult)
                inc(v.tensor_tensor(rt2[:], qo, cs, op=ALU.mult), "VEC")
                W(v, "VEC", cnt["VEC"])
                inc(v.tensor_tensor(qrv[:, :, 1, :], rt1[:], rt2[:], op=ALU.add), "VEC")
                M[f"rpq{s}"] = cnt["VEC"]

            @TE
            def _(t, s=s):
                W(t, "VEC", M[f"rpq{s}"])
                if s == 0:
                    W(t, "VEC", M[f"xE{NS - 1}"])
                W(t, "VEC", M.get(f"qE{s - 1}", 0))
                for h in range(4):
                    mm = t.transpose(tpps[1][:, h * 128:(h + 1) * 128],
                                     qr[:, h * 128:(h + 1) * 128], identb[:])
                inc(mm, "PE")
                M[f"qT{s}"] = cnt["PE"]

            @VE
            def _(v, s=s):
                W(v, "PE", M[f"qT{s}"])
                for h in range(4):
                    mm = v.tensor_copy(qTa[:, h * 1024 + s * 128: h * 1024 + (s + 1) * 128],
                                       tpps[1][:, h * 128:(h + 1) * 128])
                inc(mm, "VEC")
                M[f"qE{s}"] = cnt["VEC"]

        M["qT"] = cnt["VEC"]

        _scope_switch('P2attn')
        free("rt2", "rt1", "ktT", "vr", "kr", "qr",
             "xb161", "xb160", "xbuf1", "xbuf0", "Wq_tmp1", "Wq_tmp0",
             "Wq_eff", "Wv_sb", "Wk_sb", "sin4", "cos4", "xfm_all")
        free("tpps1", "tpps0", "p1ps1", "p1ps0")

        mask_sb = salloc("mask_sb", [128, NS * 1024], BF16)

        @SY
        def _(q):
            W(q, "VEC", M["qT"])
            dq_pre(q, "SYNQ")
            M["maskl"] = dq_post(q.dma_start(out=mask_sb[:], in_=masksI), "SYNQ")

        @GE
        def _(g):
            dq_pre(g, "POOLQ")
            dq_post(g.dma_start(out=bo_b[:], in_=bo), "POOLQ")
            dq_pre(g, "POOLQ")
            dq_post(g.dma_start(out=Wo_sb[:].rearrange("p (c n) -> p c n", c=4),
                               in_=Wo.rearrange("(c p) n -> p c n", p=128)), "POOLQ")
            M["wo"] = snapq("POOLQ")

        Kh = [salloc(f"Kh{i}", [128, NC * 1024], BF16) for i in range(2)]
        Vh = [salloc(f"Vh{i}", [128, 64 * 128], BF16) for i in range(2)]
        pT = [salloc(f"pT{i}", [128, 1024], BF16) for i in range(2)]
        dnm = salloc("dnm", [1, 1024], F32)
        rb_sb = salloc("rb_sb", [128, 1024], F32)

        sT_ps = [palloc(f"sT_ps{i}", [128, 1024]) for i in range(2)]
        yA_ps = palloc("yA_ps", [128, 512])
        yB_ps = palloc("yB_ps", [128, 512])
        dn_ps = palloc("dn_ps", [128, 512])

        exp_m, mask_m, sco_m, av_m = {}, {}, {}, {}
        A0_attn = cnt["ACT"]

        def kv_load(h):
            @SE
            def _(sc, h=h):
                if h == 0:
                    W(sc, "CC", M["agkv"])
                if h <= 1:
                    W(sc, "VEC", M["qT"])
                    W(sc, "PE", M[f"qT{NS - 1}"])
                W(sc, "VEC", M.get(f"ynm{h - 2}", 0))
                dq_pre(sc, "SCLQ")
                dq_post(sc.dma_start(
                    out=Kh[h % 2][:].rearrange("d (r t) -> d r t", r=NC),
                    in_=k_out.rearrange("(r h d) t -> h d r t", r=NC, h=4)[h]), "SCLQ")
                for r in range(NC):
                    dq_pre(sc, "SCLQ")
                    dq_post(sc.dma_start(
                        out=Vh[h % 2][:].rearrange("p (r t d) -> p r t d", r=NC, t=8)[:, r],
                        in_=v_out[r * 1024:(r + 1) * 1024, h * 128:(h + 1) * 128]
                        .rearrange("(t p) d -> p t d", p=128)), "SCLQ")
                M[f"kvh{h}"] = snapq("SCLQ")

        kv_load(0)
        kv_load(1)

        def attn_scores(t, h, j):
            W(t, "ACT", exp_m.get((h, j - 2), 0))
            rj, sj = kchunk_owner(j)
            qs = 128 * (j // 8)
            lhs = Kh[h % 2][:, rj * 1024 + sj * 128: rj * 1024 + (sj + 1) * 128]
            if qs < 512:
                t.matmul(sT_ps[j % 2][:, qs:512], lhs,
                         qTa[:, h * 1024 + qs: h * 1024 + 512], start=True, stop=True)
                mm = t.matmul(sT_ps[j % 2][:, 512:1024], lhs,
                              qTa[:, h * 1024 + 512: h * 1024 + 1024], start=True, stop=True)
            else:
                mm = t.matmul(sT_ps[j % 2][:, qs:1024], lhs,
                              qTa[:, h * 1024 + qs: h * 1024 + 1024], start=True, stop=True)
            sco_m[(h, j)] = inc(mm, "PE")

        def attn_av(t, h, j):
            W(t, "VEC", mask_m[(h, j)])
            rj, sj = kchunk_owner(j)
            qs = 128 * (j // 8)
            vlhs = Vh[h % 2][:, (rj * 8 + sj) * 128:(rj * 8 + sj + 1) * 128]
            if qs < 512:
                t.matmul(yA_ps[:, qs:512], vlhs, pT[j % 2][:, qs:512],
                         start=(j == 0), stop=(j == 31))
                t.matmul(yB_ps[:], vlhs, pT[j % 2][:, 512:1024],
                         start=(j == 0), stop=False)
                t.matmul(dn_ps[0:1, qs:512], ones_c[:], pT[j % 2][:, qs:512],
                         start=(j == 0), stop=(j == 31))
                mm = t.matmul(dn_ps[32:33, :], ones_c[:], pT[j % 2][:, 512:1024],
                              start=(j == 0), stop=False)
            else:
                t.matmul(yB_ps[:, qs - 512:512], vlhs, pT[j % 2][:, qs:1024],
                         start=False, stop=(j == 63))
                mm = t.matmul(dn_ps[32:33, qs - 512:512], ones_c[:],
                              pT[j % 2][:, qs:1024], start=False, stop=(j == 63))
            av_m[(h, j)] = inc(mm, "PE")

        for h in range(4):
            @TE
            def _(t, h=h):
                if h == 0:
                    W(t, "VEC", M["qT"])
                Wd(t, M[f"kvh{h}"])
                if h > 0:
                    W(t, "VEC", M[f"ynm{h - 1}"])
                attn_scores(t, h, 0)

            for j in range(64):
                exp_m[(h, j)] = A0_attn + 64 * h + j + 1

                @VE
                def _(v, h=h, j=j):
                    if (h, j) == (0, 0):
                        Wd(v, M["maskl"])
                    W(v, "ACT", exp_m[(h, j)])
                    s0 = j // 8
                    qs = 128 * s0
                    mask_m[(h, j)] = inc(v.tensor_tensor(
                        pT[j % 2][:, qs:qs + 128], pT[j % 2][:, qs:qs + 128],
                        mask_sb[:, s0 * 1024 + (j % 8) * 128: s0 * 1024 + (j % 8 + 1) * 128],
                        op=ALU.mult), "VEC")

                @TE
                def _(t, h=h, j=j):
                    if j < 63:
                        attn_scores(t, h, j + 1)
                    attn_av(t, h, j)

            @SE
            def _(sc, h=h):
                for j in range(64):
                    W(sc, "PE", sco_m[(h, j)])
                    W(sc, "PE", av_m.get((h, j - 2), 0))
                    qs = 128 * (j // 8)
                    got = inc(sc.activation(
                        pT[j % 2][:, qs:1024], sT_ps[j % 2][:, qs:1024], AF.Exp,
                        bias=f32(-SHIFT), scale=f32(INVSQD)), "ACT")
                    assert got == exp_m[(h, j)], (h, j, got)

            @VE
            def _(v, h=h):
                W(v, "PE", av_m[(h, 63)])
                v.reciprocal(dnm[:, 0:512], dn_ps[0:1, :])
                inc(v.reciprocal(dnm[:, 512:1024], dn_ps[32:33, :]), "VEC")
                M[f"rcp{h}"] = cnt["VEC"]

            @TE
            def _(t, h=h):
                W(t, "VEC", M[f"rcp{h}"])
                W(t, "ACT", exp_m[(h, 63)])
                t.matmul(sT_ps[0][:, 0:512], ones_f[:], dnm[:, 0:512], start=True, stop=True)
                inc(t.matmul(sT_ps[0][:, 512:1024], ones_f[:], dnm[:, 512:1024],
                             start=True, stop=True), "PE")
                M[f"rbm{h}"] = cnt["PE"]

            @VE
            def _(v, h=h):
                W(v, "PE", M[f"rbm{h}"])
                inc(v.tensor_copy(rb_sb[:], sT_ps[0][:]), "VEC")
                W(v, "VEC", cnt["VEC"])
                v.tensor_tensor(ynT[:, h * 1024: h * 1024 + 512], yA_ps[:],
                                rb_sb[:, 0:512], op=ALU.mult)
                inc(v.tensor_tensor(ynT[:, h * 1024 + 512: h * 1024 + 1024], yB_ps[:],
                                    rb_sb[:, 512:1024], op=ALU.mult), "VEC")
                M[f"ynm{h}"] = cnt["VEC"]

            if h + 2 < 4:
                kv_load(h + 2)

        M["attn"] = cnt["VEC"]

        _scope_switch('P3')
        free("rb_sb", "dnm", "pT1", "pT0", "Vh1", "Vh0", "Kh1", "Kh0", "mask_sb")
        free("dn_ps", "yB_ps", "yA_ps", "sT_ps1", "sT_ps0")

        wr_bc = salloc("wr_bc", [128, C], F32)
        br_bc = salloc("br_bc", [128, 1], F32)
        sjunk = salloc("sjunk", [128, C], BF16)
        sc_row = salloc("sc_row", [NS, 128], F32)
        oPs = palloc("oPs", [128, 2048])
        tp3f = palloc("tp3f", [128, 512])

        @GE
        def _(g):
            W(g, "VEC", M["attn"])
            dq_pre(g, "POOLQ")
            dq_post(g.dma_start(out=wr_bc[0:1, :], in_=wr), "POOLQ")
            dq_pre(g, "POOLQ")
            dq_post(g.dma_start(out=br_bc[0:1, 0:1], in_=br), "POOLQ")
            M["wrl"] = snapq("POOLQ")

        for i in range(4):
            @TE
            def _(t, i=i):
                if i == 0:
                    Wd(t, M["wrl"])
                    W(t, "VEC", M["attn"])
                else:
                    W(t, "VEC", M[f"wrbe{i - 1}"])
                inc(t.matmul(tp3f[:], ones_f[:], wr_bc[0:1, i * 512:(i + 1) * 512],
                             start=True, stop=True), "PE")
                M[f"wrb{i}"] = cnt["PE"]

            @VE
            def _(v, i=i):
                W(v, "PE", M[f"wrb{i}"])
                inc(v.tensor_copy(wr_bc[:, i * 512:(i + 1) * 512], tp3f[:]), "VEC")
                M[f"wrbe{i}"] = cnt["VEC"]

        @TE
        def _(t):
            W(t, "VEC", M["wrbe3"])
            inc(t.matmul(tp3f[:, 0:1], ones_f[:], br_bc[0:1, 0:1], start=True, stop=True), "PE")
            M["brb"] = cnt["PE"]

        @VE
        def _(v):
            W(v, "PE", M["brb"])
            inc(v.tensor_copy(br_bc[:], tp3f[:, 0:1]), "VEC")
            M["brbe"] = cnt["VEC"]

        A0_p3 = cnt["ACT"]
        for s in range(NS):
            M[f"rs1_{s}"] = A0_p3 + 2 * (s + 1)
        for s in range(NS):
            @SY
            def _(q, s=s):
                W(q, "VEC", M.get(f"oev{s - 1}", 0))
                dq_pre(q, "SYNQ")
                M[f"xr{s}"] = dq_post(q.dma_start(out=v1t[:], in_=x_loc[s * 128:(s + 1) * 128, :]), "SYNQ")

            @TE
            def _(t, s=s):
                if s == 0:
                    Wd(t, M["wo"])
                    W(t, "VEC", M["brbe"])
                W(t, "VEC", M.get(f"oev{s - 1}", 0))
                for ncol in range(4):
                    for h in range(4):
                        t.matmul(oPs[:, ncol * 512:(ncol + 1) * 512],
                                 ynT[:, h * 1024 + s * 128: h * 1024 + (s + 1) * 128],
                                 Wo_sb[:, h * 2048 + ncol * 512: h * 2048 + (ncol + 1) * 512],
                                 start=(h == 0), stop=False)
                    mm = t.matmul(oPs[:, ncol * 512:(ncol + 1) * 512], ones_bf[:, 0:128],
                                  bo_b[:, ncol * 512:(ncol + 1) * 512], start=False, stop=True)
                inc(mm, "PE")
                M[f"op{s}"] = cnt["PE"]

            @VE
            def _(v, s=s):
                W(v, "PE", M[f"op{s}"])
                Wd(v, M[f"xr{s}"])
                Wd(v, M.get(f"x1s{s - 1}", 0))
                for i in range(4):
                    mm = v.tensor_tensor(v1t[:, i * 512:(i + 1) * 512],
                                         v1t[:, i * 512:(i + 1) * 512],
                                         oPs[:, i * 512:(i + 1) * 512], op=ALU.add)
                inc(mm, "VEC")
                M[f"oev{s}"] = cnt["VEC"]
                W(v, "VEC", cnt["VEC"])
                for i in range(4):
                    mm = v.bn_stats(bnst[:, i * 6:(i + 1) * 6], v1t[:, i * 512:(i + 1) * 512])
                inc(mm, "VEC")
                W(v, "VEC", cnt["VEC"])
                inc(v.bn_aggr(mv[:], bnst[:]), "VEC")
                M[f"st1_{s}"] = cnt["VEC"]


            @VE
            def _(v, s=s):
                W(v, "ACT", M[f"rs1_{s}"])
                inc(v.tensor_tensor(lnc2[:], mv[:, 0:1], lnc1[:], op=ALU.mult), "VEC")
                W(v, "VEC", cnt["VEC"])
                inc(v.tensor_scalar(lnc2[:], lnc2[:], -1.0, None, op0=ALU.mult), "VEC")
                W(v, "VEC", cnt["VEC"])
                inc(v.tensor_scalar(x1t[:], v1t[:], lnc1[:], lnc2[:],
                                    op0=ALU.mult, op1=ALU.add), "VEC")
                M[f"x1_{s}"] = cnt["VEC"]
                W(v, "VEC", cnt["VEC"])
                inc(v.scalar_tensor_tensor(sjunk[:], x1t[:], 1.0, wr_bc[:],
                                           op0=ALU.mult, op1=ALU.mult,
                                           accum_out=lnc2[:]), "VEC")
                W(v, "VEC", cnt["VEC"])
                inc(v.tensor_tensor(s_loc[:, s:s + 1], lnc2[:], br_bc[:], op=ALU.add), "VEC")
                M[f"sl{s}"] = cnt["VEC"]

            @SY
            def _(q, s=s):
                W(q, "VEC", M[f"x1_{s}"])
                dq_pre(q, "SYNQ")
                M[f"x1s{s}"] = dq_post(q.dma_start(out=x1_dram[s * 128:(s + 1) * 128, :], in_=x1t[:]), "SYNQ")

        @SE
        def _(sc):
            for s in range(NS):
                W(sc, "VEC", M[f"st1_{s}"])
                inc(sc.activation(lnc1[:], mv[:, 1:2], AF.Ln, bias=f32(EPS), scale=f32(1.0)), "ACT")
                W(sc, "ACT", cnt["ACT"])
                got = inc(sc.activation(lnc1[:], lnc1[:], AF.Exp, bias=f32(0.0), scale=f32(-0.5)), "ACT")
                assert got == M[f"rs1_{s}"], (s, got)

        M["x1d"] = snapq("SYNQ")

        _scope_switch('P4bisect')
        sc_flat = salloc("sc_flat", [1, 8192], F32)
        srep = salloc("srep", [64, 8192], F32)
        bjunk = salloc("bjunk", [64, 8192], BF16)
        rowcnt = salloc("rowcnt", [64, 1], F32)
        lo_ = salloc("lo_", [1, 1], F32)
        d64_ = salloc("d64_", [1, 1], F32)
        pv_i = salloc("pv_i", [64, 1], U32)
        pvec1 = salloc("pvec1", [64, 1], F32)
        tcol1 = salloc("tcol1", [64, 1], F32)
        thr_col = salloc("thr_col", [64, 1], F32)
        ps4b = palloc("ps4b", [64, 2])
        th128 = palloc("th128", [128, 1])
        cnt_ps = palloc("cnt_ps", [1, 1])

        @TE
        def _(t):
            W(t, "VEC", M[f"sl{NS - 1}"])
            inc(t.transpose(tp3f[0:8, 0:128], s_loc[:, 0:8], ident32[:]), "PE")
            M["slT"] = cnt["PE"]

        @VE
        def _(v):
            W(v, "PE", M["slT"])
            inc(v.tensor_copy(sc_row[:], tp3f[0:8, 0:128]), "VEC")
            M["srow"] = cnt["VEC"]

        @SY
        def _(q):
            W(q, "VEC", M["srow"])
            dq_pre(q, "SYNQ")
            M["scs"] = dq_post(q.dma_start(out=sc_in, in_=sc_row[:]), "SYNQ")

        @GE
        def _(g):
            Wd(g, M["scs"])
            inc(g.collective_compute("AllGather", ALU.bypass, replica_groups=RG,
                                     ins=[sc_in], outs=[sc_out]), "CC")
            M["agsc"] = cnt["CC"]


        @SY
        def _(q):
            W(q, "CC", M["agsc"])
            dq_pre(q, "SYNQ")
            M["sclf"] = dq_post(q.dma_start(out=sc_flat[:], in_=sc_out.rearrange("a b -> (a b)").rearrange("(o n) -> o n", o=1)), "SYNQ")

        @GE
        def _(g):
            inc(g.iota(pv_i[:], pattern=[[0, 1]], base=1, channel_multiplier=1), "POOL")
            M["pvi"] = cnt["POOL"]

        @VE
        def _(v):
            W(v, "POOL", M["pvi"])
            inc(v.tensor_copy(pvec1[:], pv_i[:]), "VEC")
            v.memset(lo_[:], -128.0)
            inc(v.memset(d64_[:], 4.0), "VEC")
            M["bisinit"] = cnt["VEC"]

        for i in range(16):
            @TE
            def _(t, i=i):
                if i == 0:
                    Wd(t, M["sclf"])
                    W(t, "VEC", M["srow"])
                else:
                    W(t, "VEC", M[f"srepc{i - 1}"])
                inc(t.matmul(tp3f[0:64, :], ones_f[:, 0:64],
                             sc_flat[:, i * 512:(i + 1) * 512], start=True, stop=True), "PE")
                M[f"srep{i}"] = cnt["PE"]

            @VE
            def _(v, i=i):
                W(v, "PE", M[f"srep{i}"])
                inc(v.tensor_copy(srep[:, i * 512:(i + 1) * 512], tp3f[0:64, :]), "VEC")
                M[f"srepc{i}"] = cnt["VEC"]
                if i == 15:
                    M["srep"] = cnt["VEC"]

        for it in range(6):
            @TE
            def _(t, it=it):
                W(t, "VEC", M["bisinit"] if it == 0 else M[f"b64u{it - 1}"])
                t.matmul(ps4b[:, 0:1], ones_f[:, 0:64], d64_[:], start=True, stop=True)
                inc(t.matmul(ps4b[:, 1:2], ones_f[:, 0:64], lo_[:], start=True, stop=True), "PE")
                M[f"b64b{it}"] = cnt["PE"]

            @VE
            def _(v, it=it):
                W(v, "PE", M[f"b64b{it}"])
                if it == 0:
                    W(v, "VEC", M["srep"])
                inc(v.tensor_scalar(tcol1[:], pvec1[:], ps4b[:, 0:1], None, op0=ALU.mult), "VEC")
                W(v, "VEC", cnt["VEC"])
                inc(v.tensor_scalar(thr_col[:], tcol1[:], ps4b[:, 1:2], None, op0=ALU.add), "VEC")
                W(v, "VEC", cnt["VEC"])
                inc(v.tensor_scalar(bjunk[:], srep[:], thr_col[:], None,
                                    op0=ALU.is_ge, op1=ALU.add, accum_out=rowcnt[:]), "VEC")
                W(v, "VEC", cnt["VEC"])
                inc(v.tensor_scalar(rowcnt[:], rowcnt[:], 4096.0, None, op0=ALU.is_ge), "VEC")
                M[f"b64c{it}"] = cnt["VEC"]

            @TE
            def _(t, it=it):
                W(t, "VEC", M[f"b64c{it}"])
                inc(t.matmul(cnt_ps[:], rowcnt[:], ones64[:], start=True, stop=True), "PE")
                M[f"b64s{it}"] = cnt["PE"]

            @VE
            def _(v, it=it):
                W(v, "PE", M[f"b64s{it}"])
                inc(v.scalar_tensor_tensor(lo_[:], cnt_ps[:], d64_[:], lo_[:],
                                           op0=ALU.mult, op1=ALU.add), "VEC")
                W(v, "VEC", cnt["VEC"])
                inc(v.tensor_scalar(d64_[:], d64_[:], 1.0 / 64.0, None, op0=ALU.mult), "VEC")
                M[f"b64u{it}"] = cnt["VEC"]
        M[f"bi{BISECT - 1}"] = M["b64u5"]

        _scope_switch('P4idx')
        pos_i = salloc("pos_i", [128, NS], F32)
        posu = salloc("posu", [128, NS], U32)
        rc128 = salloc("rc128", [128, 1], F32)
        prow = salloc("prow", [1, 128], F32)
        poff = salloc("poff", [128, 1], F32)
        zero1 = salloc("zero1", [1, 1], F32)
        trow = salloc("trow", [128, 8 * NS], F32)
        tcol = salloc("tcol", [128, 8 * NS], U32)
        bigt = salloc("bigt", [128, NS], F32)
        gf32b = salloc("gf32b", [128, 5 * NS], F32)

        @TE
        def _(t):
            W(t, "VEC", M[f"bi{BISECT - 1}"])
            inc(t.matmul(th128[:], ones_f[:], lo_[:], start=True, stop=True), "PE")
            M["th128"] = cnt["PE"]

        @VE
        def _(v):
            W(v, "PE", M["th128"])
            inc(v.tensor_scalar(m_loc[:], s_loc[:], th128[:], None, op0=ALU.is_ge), "VEC")
            W(v, "VEC", cnt["VEC"])
            inc(v.tensor_reduce(rc128[:], m_loc[:], axis=mybir.AxisListType.X, op=ALU.add), "VEC")
            M["rc"] = cnt["VEC"]

        @TE
        def _(t):
            W(t, "VEC", M["rc"])
            inc(t.transpose(tp3f[0:1, 0:128], rc128[:], ident32[:]), "PE")
            M["rcT"] = cnt["PE"]

        @VE
        def _(v):
            W(v, "PE", M["rcT"])
            v.memset(zero1[:], 0.0)
            inc(v.tensor_copy(prow[:], tp3f[0:1, 0:128]), "VEC")
            W(v, "VEC", cnt["VEC"])
            inc(v.tensor_tensor_scan(prow[:], prow[:], prow[:], zero1[:],
                                     op0=ALU.add, op1=ALU.bypass), "VEC")
            M["scan"] = cnt["VEC"]

        @TE
        def _(t):
            W(t, "VEC", M["scan"])
            inc(t.matmul(tp3f[:, 0:1], prow[:], ones_f[:, 0:1], start=True, stop=True), "PE")
            M["scanT"] = cnt["PE"]

        @VE
        def _(v):
            W(v, "PE", M["scanT"])
            inc(v.tensor_copy(poff[:], tp3f[:, 0:1]), "VEC")
            W(v, "VEC", cnt["VEC"])
            inc(v.tensor_tensor(poff[:], poff[:], rc128[:], op=ALU.subtract), "VEC")
            W(v, "VEC", cnt["VEC"])
            inc(v.tensor_tensor_scan(pos_i[:], m_loc[:], m_loc[:], zerop[:],
                                     op0=ALU.add, op1=ALU.bypass), "VEC")
            W(v, "VEC", cnt["VEC"])
            inc(v.scalar_tensor_tensor(pos_i[:], pos_i[:], poff[:], m_loc[:],
                                       op0=ALU.add, op1=ALU.subtract), "VEC")
            W(v, "VEC", cnt["VEC"])
            inc(v.tensor_scalar(bigt[:, 0:NS], m_loc[:], -1.0e9, 1.0e9,
                                op0=ALU.mult, op1=ALU.add), "VEC")
            W(v, "VEC", cnt["VEC"])
            inc(v.tensor_tensor(pos_i[:], pos_i[:], bigt[:, 0:NS], op=ALU.add), "VEC")
            W(v, "VEC", cnt["VEC"])
            inc(v.tensor_copy(posu[:], pos_i[:]), "VEC")
            M["posu"] = cnt["VEC"]

        @GE
        def _(g):
            inc(g.iota(tcol[:], pattern=[[128, NS], [0, 8]], base=0, channel_multiplier=1), "POOL")
            M["iot8"] = cnt["POOL"]

        @VE
        def _(v):
            W(v, "POOL", M["iot8"])
            inc(v.tensor_copy(trow[:], tcol[:]), "VEC")
            inc(v.memset(bigt[:, 0:NS], 2.0e9), "VEC")
            M["trow8"] = cnt["VEC"]

        @GE
        def _(g):
            W(g, "VEC", M["trow8"])
            for s in range(NS):
                dq_pre(g, "POOLQ")
                dq_post(g.dma_start(out=gbuf[s * 128:(s + 1) * 128, 0:NS], in_=bigt[:, 0:NS]), "POOLQ")
            M["gfill"] = snapq("POOLQ")

        for s in range(NS):
            @GE
            def _(g, s=s):
                if s == 0:
                    Wd(g, M["gfill"])
                    W(g, "VEC", M["posu"])
                dq_pre(g, "POOLQ")
                M[f"gsc{s}"] = dq_post(g.indirect_dma_start(
                    out=gbuf[:, 0:NS], out_offset=IndirectOffsetOnAxis(ap=posu[:, s:s + 1], axis=0),
                    in_=trow[:, 8 * s:8 * s + NS], in_offset=None,
                    bounds_check=LT - 1, oob_is_err=False), "POOLQ")

        M["gscall"] = snapq("POOLQ")

        @SY
        def _(q):
            Wd(q, M["gscall"])
            dq_pre(q, "SYNQ")
            M["gld"] = dq_post(q.dma_start(
                out=gf32b[:].rearrange("p (i f) -> p i f", f=NS),
                in_=gbuf[0:CAP, :].rearrange("(i p) f -> p i f", p=128)), "SYNQ")

        @VE
        def _(v):
            Wd(v, M["gld"])
            inc(v.tensor_copy(gf32[:], gf32b[:].rearrange("p (i f) -> p i f", f=NS)[:, :, 0]), "VEC")
            W(v, "VEC", cnt["VEC"])
            inc(v.tensor_copy(gidx[:], gf32[:]), "VEC")
            M["gidx"] = cnt["VEC"]

        if taps:
            @GE
            def _(g):
                if "x1" in taps:
                    Wd(g, M["x1d"])
                    dq_pre(g, "POOLQ")
                    dq_post(g.dma_start(out=tap_out["x1"], in_=x1_dram), "POOLQ")
                if "thr" in taps:
                    W(g, "VEC", M[f"bi{BISECT - 1}"])
                    dq_pre(g, "POOLQ")
                    dq_post(g.dma_start(out=tap_out["thr"], in_=lo_[:]), "POOLQ")
                if "mask" in taps:
                    W(g, "VEC", M["posu"])
                    dq_pre(g, "POOLQ")
                    dq_post(g.dma_start(out=tap_out["mask"], in_=m_loc[:]), "POOLQ")
                if "gidx" in taps:
                    W(g, "VEC", M["gidx"])
                    dq_pre(g, "POOLQ")
                    dq_post(g.dma_start(out=tap_out["gidx"], in_=gf32[:]), "POOLQ")
                if "ynT" in taps:
                    W(g, "VEC", M["attn"])
                    dq_pre(g, "POOLQ")
                    dq_post(g.dma_start(out=tap_out["ynT"], in_=ynT[:]), "POOLQ")
                M["tapd"] = snapq("POOLQ")

        _scope_switch('P5gather')
        free("gf32b", "bigt", "tcol", "trow", "zero1", "poff", "prow", "rc128", "posu",
             "pos_i", "thr_col", "tcol1", "pvec1", "pv_i", "d64_", "lo_",
             "rowcnt", "bjunk", "srep", "sc_flat", "sc_row", "sjunk", "br_bc", "wr_bc")
        free("cnt_ps", "th128", "ps4b", "tp3f", "oPs")
        free("bo_b", "Wo_sb", "ynT", "qTa")

        hT = salloc("hT", [128, 64 * CAP], BF16)
        xsT = salloc("xsT", [128, 16 * CAP], BF16)
        v2sel_all = salloc("v2sel_all", [128, 5 * C], BF16)
        b2_b = salloc("b2_b", [1, C], BF16)
        b1_b = salloc("b1_b", [1, DF], BF16)
        w1b = [salloc(f"w1b{i}", [128, 16 * 256], BF16) for i in range(3)]
        xselb = [salloc(f"xselb{i}", [128, C], BF16) for i in range(5)]
        tp5 = [palloc(f"tp5{i}", [128, 512], BF16) for i in range(2)]
        psU = [palloc(f"psU{i}", [128, CAP]) for i in range(2)]

        @GE
        def _(g):
            W(g, "VEC", M["gidx"])
            Wd(g, snapq("POOLQ"))
            dq_pre(g, "POOLQ")
            dq_post(g.dma_start(out=b1_b[:], in_=b1), "POOLQ")
            dq_pre(g, "POOLQ")
            dq_post(g.dma_start(out=b2_b[:], in_=b2), "POOLQ")
            for i in range(3):
                dq_pre(g, "POOLQ")
                M[f"w1l{i}"] = dq_post(g.dma_start(
                    out=w1b[i][:].rearrange("p (c n) -> p c n", c=16),
                    in_=w1[:, i * 256:(i + 1) * 256].rearrange("(c p) n -> p c n", p=128)), "POOLQ")

        gat = {}
        for i in range(5):
            @VE
            def _(v, i=i):
                if i == 0:
                    W(v, "VEC", M["gidx"])
                    Wd(v, M.get("tapd", 0))
                gat[("z", i)] = inc(v.memset(xselb[i][:], 0.0), "VEC")

            @GE
            def _(g, i=i):
                if i == 0:
                    Wd(g, M["x1d"])
                W(g, "VEC", gat[("z", i)])
                dq_pre(g, "POOLQ")
                gat[("g", i)] = dq_post(g.indirect_dma_start(
                    out=xselb[i][:], out_offset=None,
                    in_=x1_dram, in_offset=IndirectOffsetOnAxis(ap=gidx[:, i:i + 1], axis=0),
                    bounds_check=LT - 1, oob_is_err=False), "POOLQ")

        for i in range(5):
            for b in range(4):
                @TE
                def _(t, i=i, b=b):
                    Wd(t, gat[("g", i)])
                    W(t, "VEC", gat.get(("e", i, b - 2), gat.get(("e", i - 1, b + 2), 0)))
                    for k in range(4):
                        mm = t.transpose(tp5[b % 2][:, k * 128:(k + 1) * 128],
                                         xselb[i][:, (4 * b + k) * 128:(4 * b + k + 1) * 128],
                                         identb[:])
                    inc(mm, "PE")
                    gat[("t", i, b)] = cnt["PE"]

                @VE
                def _(v, i=i, b=b):
                    W(v, "PE", gat[("t", i, b)])
                    mm = v.tensor_copy(
                        xsT[:].rearrange("p (fc t) -> p fc t", fc=16)[:, 4 * b:4 * b + 4,
                                                                     i * 128:(i + 1) * 128],
                        tp5[b % 2][:].rearrange("p (fc t) -> p fc t", fc=4))
                    gat[("e", i, b)] = inc(mm, "VEC")
        M["xsT"] = cnt["VEC"]

        _scope_switch('lnout')
        ln = {}
        A0_base = cnt["ACT"]
        for s in range(NS):
            ln[("rs", s)] = A0_base + 2 * (s + 1)
        for s in range(NS):
            @SY
            def _(q, s=s):
                if s == 0:
                    Wd(q, M["x1d"])
                W(q, "VEC", ln.get(("ub", s - 1), 0))
                Wd(q, ln.get(("st", s - 1), 0))
                dq_pre(q, "SYNQ")
                ln[("xr", s)] = dq_post(q.dma_start(out=v1t[:], in_=x1_dram[s * 128:(s + 1) * 128, :]), "SYNQ")

            @VE
            def _(v, s=s):
                Wd(v, ln[("xr", s)])
                for i in range(4):
                    mm = v.bn_stats(bnst[:, i * 6:(i + 1) * 6], v1t[:, i * 512:(i + 1) * 512])
                inc(mm, "VEC")
                W(v, "VEC", cnt["VEC"])
                inc(v.bn_aggr(mv[:], bnst[:]), "VEC")
                ln[("stat", s)] = cnt["VEC"]
                ln[("ub", s)] = cnt["VEC"]


            @VE
            def _(v, s=s):
                W(v, "ACT", ln[("rs", s)])
                inc(v.tensor_tensor(lnc2[:], mv[:, 0:1], lnc1[:], op=ALU.mult), "VEC")
                W(v, "VEC", cnt["VEC"])
                inc(v.tensor_scalar(lnc2[:], lnc2[:], -1.0, None, op0=ALU.mult), "VEC")
                W(v, "VEC", cnt["VEC"])
                inc(v.tensor_scalar(v1t[:], v1t[:], lnc1[:], lnc2[:],
                                    op0=ALU.mult, op1=ALU.add), "VEC")
                ln[("y", s)] = cnt["VEC"]

            @SY
            def _(q, s=s):
                W(q, "VEC", ln[("y", s)])
                dq_pre(q, "SYNQ")
                ln[("st", s)] = dq_post(q.dma_start(out=out[s * 128:(s + 1) * 128, :], in_=v1t[:]), "SYNQ")

        @SE
        def _(sc):
            for s in range(NS):
                W(sc, "VEC", ln[("stat", s)])
                inc(sc.activation(lnc1[:], mv[:, 1:2], AF.Ln, bias=f32(EPS), scale=f32(1.0)), "ACT")
                W(sc, "ACT", cnt["ACT"])
                got = inc(sc.activation(lnc1[:], lnc1[:], AF.Exp, bias=f32(0.0), scale=f32(-0.5)), "ACT")
                assert got == ln[("rs", s)], (s, got)

        M["base_out"] = snapq("SYNQ")


        _scope_switch('mm1')
        mm1 = {}
        A0_gel = cnt["ACT"]
        for g_ in range(64):
            mm1[("gel", g_)] = A0_gel + g_ + 1
        for ds in range(32):
            @TE
            def _(t, ds=ds):
                if ds == 0:
                    W(t, "VEC", M["xsT"])
                Wd(t, M[f"w1l{ds}"] if ds < 3 else mm1[("w1l", ds)])
                for dfc in range(2):
                    g = 2 * ds + dfc
                    W(t, "ACT", mm1.get(("gel", g - 2), 0))
                    for fc in range(16):
                        lhs = w1b[ds % 3][:, fc * 256 + dfc * 128: fc * 256 + (dfc + 1) * 128]
                        t.matmul(psU[g % 2][:, 0:512], lhs, xsT[:, fc * CAP: fc * CAP + 512],
                                 start=(fc == 0), stop=False)
                        t.matmul(psU[g % 2][:, 512:CAP], lhs,
                                 xsT[:, fc * CAP + 512: fc * CAP + CAP],
                                 start=(fc == 0), stop=False)
                    boff = ds * 256 + dfc * 128
                    t.matmul(psU[g % 2][:, 0:512], b1_b[:, boff:boff + 128],
                             ones_bf[:, 0:512], start=False, stop=True)
                    mm = t.matmul(psU[g % 2][:, 512:CAP], b1_b[:, boff:boff + 128],
                                  ones_bf[:, 0:128], start=False, stop=True)
                    mm1[("u", g)] = inc(mm, "PE")

            if ds + 3 < 32:
                @GE
                def _(g, ds=ds):
                    W(g, "PE", mm1[("u", 2 * ds + 1)])
                    dq_pre(g, "POOLQ")
                    mm = g.dma_start(
                        out=w1b[ds % 3][:].rearrange("p (c n) -> p c n", c=16),
                        in_=w1[:, (ds + 3) * 256:(ds + 4) * 256].rearrange("(c p) n -> p c n", p=128))
                    mm1[("w1l", ds + 3)] = dq_post(mm, "POOLQ")

        @SE
        def _(sc):
            for g in range(64):
                W(sc, "PE", mm1[("u", g)])
                got = inc(sc.activation(
                    hT[:, g * CAP:(g + 1) * CAP], psU[g % 2][:], AF.Gelu_apprx_tanh), "ACT")
                assert got == mm1[("gel", g)], (g, got)

        M["gelu"] = cnt["ACT"]

        _scope_switch('mm2')
        free("xselb4", "xselb3", "xselb2", "xselb1", "xselb0",
             "w1b2", "w1b1", "w1b0", "b1_b")
        free("psU1", "psU0", "tp51", "tp50")
        w2cb = [salloc(f"w2cb{i}", [128, 64 * 128], BF16) for i in range(2)]
        v2T = [salloc(f"v2T{i}", [128, CAP], BF16) for i in range(2)]
        oT_ps = [palloc(f"oT_ps{i}", [128, CAP]) for i in range(2)]
        tpT = palloc("tpT", [128, 640], BF16)

        mm2 = {}
        for nb in range(2):
            @GE
            def _(g, nb=nb):
                W(g, "ACT", M["gelu"] if nb == 0 else 0)
                dq_pre(g, "POOLQ")
                mm = g.dma_start(
                    out=w2cb[nb % 2][:].rearrange("p (c n) -> p c n", c=64),
                    in_=w2[:, nb * 128:(nb + 1) * 128].rearrange("(c p) n -> p c n", p=128))
                mm2[("w2l", nb)] = dq_post(mm, "POOLQ")

        for nb in range(16):
            @TE
            def _(t, nb=nb):
                Wd(t, mm2[("w2l", nb)])
                if nb >= 2:
                    W(t, "VEC", mm2[("ev", nb - 2)])
                for dfc in range(64):
                    lhs = w2cb[nb % 2][:, dfc * 128:(dfc + 1) * 128]
                    t.matmul(oT_ps[nb % 2][:, 0:512], lhs,
                             hT[:, dfc * CAP: dfc * CAP + 512],
                             start=(dfc == 0), stop=False)
                    t.matmul(oT_ps[nb % 2][:, 512:CAP], lhs,
                             hT[:, dfc * CAP + 512: dfc * CAP + CAP],
                             start=(dfc == 0), stop=False)
                t.matmul(oT_ps[nb % 2][:, 0:512], b2_b[:, nb * 128:(nb + 1) * 128],
                         ones_bf[:, 0:512], start=False, stop=True)
                mm = t.matmul(oT_ps[nb % 2][:, 512:CAP], b2_b[:, nb * 128:(nb + 1) * 128],
                              ones_bf[:, 0:128], start=False, stop=True)
                mm2[("mm", nb)] = inc(mm, "PE")

            @VE
            def _(v, nb=nb):
                W(v, "PE", mm2[("mm", nb)])
                if nb >= 2:
                    W(v, "PE", mm2[("tT", nb - 2)])
                mm2[("ev", nb)] = inc(v.tensor_tensor(
                    v2T[nb % 2][:], oT_ps[nb % 2][:],
                    xsT[:, nb * CAP:(nb + 1) * CAP], op=ALU.add), "VEC")

            @TE
            def _(t, nb=nb):
                W(t, "VEC", mm2[("ev", nb)])
                if nb >= 1:
                    W(t, "VEC", mm2[("tc", nb - 1)])
                for sl in range(5):
                    mm = t.transpose(tpT[:, sl * 128:(sl + 1) * 128],
                                     v2T[nb % 2][:, sl * 128:(sl + 1) * 128], identb[:])
                mm2[("tT", nb)] = inc(mm, "PE")

            @VE
            def _(v, nb=nb):
                W(v, "PE", mm2[("tT", nb)])
                mm2[("tc", nb)] = inc(v.tensor_copy(
                    v2sel_all[:].rearrange("p (sl t) -> p sl t", sl=5)[:, :,
                                                                      nb * 128:(nb + 1) * 128],
                    tpT[:].rearrange("p (sl t) -> p sl t", sl=5)), "VEC")

            if nb + 2 < 16:
                @GE
                def _(g, nb=nb):
                    W(g, "PE", mm2[("mm", nb)])
                    dq_pre(g, "POOLQ")
                    mm = g.dma_start(
                        out=w2cb[nb % 2][:].rearrange("p (c n) -> p c n", c=64),
                        in_=w2[:, (nb + 2) * 128:(nb + 3) * 128].rearrange("(c p) n -> p c n", p=128))
                    mm2[("w2l", nb + 2)] = dq_post(mm, "POOLQ")

        M["mm2e"] = cnt["VEC"]

        sel = {}
        A0_sel = cnt["ACT"]
        for sl in range(5):
            sel[("rs", sl)] = A0_sel + 2 * (sl + 1)
        for sl in range(5):
            @VE
            def _(v, sl=sl):
                if sl == 0:
                    W(v, "VEC", M["mm2e"])
                Wd(v, sel.get(("sc", sl - 1), 0))
                for i in range(4):
                    mm = v.bn_stats(bnst[:, i * 6:(i + 1) * 6],
                                    v2sel_all[:, sl * C + i * 512: sl * C + (i + 1) * 512])
                inc(mm, "VEC")
                W(v, "VEC", cnt["VEC"])
                inc(v.bn_aggr(mv[:], bnst[:]), "VEC")
                sel[("stat", sl)] = cnt["VEC"]


            @VE
            def _(v, sl=sl):
                W(v, "ACT", sel[("rs", sl)])
                inc(v.tensor_tensor(lnc2[:], mv[:, 0:1], lnc1[:], op=ALU.mult), "VEC")
                W(v, "VEC", cnt["VEC"])
                inc(v.tensor_scalar(lnc2[:], lnc2[:], -1.0, None, op0=ALU.mult), "VEC")
                W(v, "VEC", cnt["VEC"])
                inc(v.tensor_scalar(x1t[:], v2sel_all[:, sl * C:(sl + 1) * C], lnc1[:], lnc2[:],
                                    op0=ALU.mult, op1=ALU.add), "VEC")
                sel[("y", sl)] = cnt["VEC"]

            @GE
            def _(g, sl=sl):
                if sl == 0:
                    Wd(g, M["base_out"])
                W(g, "VEC", sel[("y", sl)])
                dq_pre(g, "POOLQ")
                sel[("sc", sl)] = dq_post(g.indirect_dma_start(
                    out=out, out_offset=IndirectOffsetOnAxis(ap=gidx[:, sl:sl + 1], axis=0),
                    in_=x1t[:], in_offset=None,
                    bounds_check=LT - 1, oob_is_err=False), "POOLQ")

        if _scope_state["cm"] is not None:
            _scope_state["cm"].__exit__(None, None, None)
            _scope_state["cm"] = None

        @SE
        def _(sc):
            for sl in range(5):
                W(sc, "VEC", sel[("stat", sl)])
                inc(sc.activation(lnc1[:], mv[:, 1:2], AF.Ln, bias=f32(EPS), scale=f32(1.0)), "ACT")
                W(sc, "ACT", cnt["ACT"])
                got = inc(sc.activation(lnc1[:], lnc1[:], AF.Exp, bias=f32(0.0), scale=f32(-0.5)), "ACT")
                assert got == sel[("rs", sl)], (sl, got)

        @GE
        def _(g):
            Wd(g, snapq("SYNQ"))
            Wd(g, snapq("POOLQ"))
            Wd(g, snapq("SCLQ"))
            W(g, "PE", cnt["PE"])
            W(g, "ACT", cnt["ACT"])
            W(g, "VEC", cnt["VEC"])
            W(g, "POOL", cnt["POOL"])

    for n in reversed(list(opens)):
        opens.pop(n).__exit__(None, None, None)
    return nc



def _host_consts(c):
    tiles = [q_tile(c, s) for s in range(NS)]
    rows = np.concatenate([np.arange(g * 128, (g + 1) * 128) for g in tiles])
    inv = 10000.0 ** (-(np.arange(64, dtype=np.float64)) / 64.0)
    ang = rows[:, None].astype(np.float64) * inv[None, :]
    cosr = np.tile(np.cos(ang).astype(np.float32), (1, 4))
    sinr = np.tile(np.sin(ang).astype(np.float32), (1, 4))
    masks = np.zeros((128, NS * 1024), dtype=ml_dtypes.bfloat16)
    for s in range(NS):
        pi = (s + c) % NC
        kk = np.arange(128)[:, None]
        jq = np.arange(1024)[None, :]
        jj, ql = jq // 128, jq % 128
        allowed = (jj * 128 + kk) <= (pi * 128 + ql)
        masks[:, s * 1024:(s + 1) * 1024] = allowed.astype(ml_dtypes.bfloat16)
    return rows, cosr, sinr, masks


def kernel(L0_attn_x, L0_attn_Wq, L0_attn_bq, L0_attn_Wk, L0_attn_bk,
           L0_attn_Wv, L0_attn_bv, L0_attn_Wo, L0_attn_bo,
           L0_ffn_w_r, L0_ffn_b_r, L0_ffn_w1, L0_ffn_b1, L0_ffn_w2, L0_ffn_b2):
    x = np.ascontiguousarray(np.asarray(L0_attn_x, dtype=np.float32).reshape(T, C))
    com = dict(
        Wq=np.ascontiguousarray(np.asarray(L0_attn_Wq, np.float32)),
        bq=np.asarray(L0_attn_bq, np.float32).reshape(1, C),
        Wk=np.ascontiguousarray(np.asarray(L0_attn_Wk, np.float32)),
        bk=np.asarray(L0_attn_bk, np.float32).reshape(1, DKV),
        Wv=np.ascontiguousarray(np.asarray(L0_attn_Wv, np.float32)),
        bv=np.asarray(L0_attn_bv, np.float32).reshape(1, DKV),
        Wo=np.ascontiguousarray(np.asarray(L0_attn_Wo, np.float32)),
        bo=np.asarray(L0_attn_bo, np.float32).reshape(1, C),
        w1=np.ascontiguousarray(np.asarray(L0_ffn_w1, np.float32)),
        b1=np.asarray(L0_ffn_b1, np.float32).reshape(1, DF),
        w2=np.ascontiguousarray(np.asarray(L0_ffn_w2, np.float32)),
        b2=np.asarray(L0_ffn_b2, np.float32).reshape(1, C),
        wr=np.asarray(L0_ffn_w_r, np.float32).reshape(1, C),
        br=np.asarray(L0_ffn_b_r, np.float32).reshape(1, 1),
        ident=np.eye(128, dtype=ml_dtypes.bfloat16),
    )
    in_maps = []
    row_lists = []
    for c in range(NC):
        rows, cosr, sinr, masks = _host_consts(c)
        row_lists.append(rows)
        m = dict(com)
        m["x_loc"] = np.ascontiguousarray(x[rows])
        m["cosT"] = np.ascontiguousarray(cosr)
        m["sinT"] = np.ascontiguousarray(sinr)
        m["masks"] = np.ascontiguousarray(masks)
        in_maps.append(m)

    nc_prog = build_program(taps=_TAPS)
    res = run_bass_kernel_spmd(nc_prog, in_maps, core_ids=list(range(NC)))

    full = np.empty((T, C), dtype=np.float32)
    for c in range(NC):
        full[row_lists[c]] = res.results[c]["out"]
    kernel._last_results = res
    return full.reshape(1, T, C)
